# version 07
# baseline (speedup 1.0000x reference)
import sys
sys.path.insert(0, '/opt/trn_rl_repo')
import math
import numpy as np
import ml_dtypes
from contextlib import ExitStack
import concourse.bass as bass
import concourse.mybir as mybir
from concourse.bass_utils import run_bass_kernel_spmd

F32 = mybir.dt.float32
BF16 = mybir.dt.bfloat16
AF = mybir.ActivationFunctionType
ALU = mybir.AluOpType
AX = mybir.AxisListType
BF = ml_dtypes.bfloat16

B, T, D = 2, 16384, 1024
H, HD = 16, 64
KV = H * HD
NH = 4
TT = 512
NTILE = T // TT
C = 128
TF = 4096
FT = 512
NFT = TF // FT
I_FFN = 2816
NMC = 22
NMH = 8
NE = 8
LN8 = math.log(0.125)

_BUILD_MODE = "full"


class Sched:
    ENGS = ("tensor", "vector", "scalar", "gpsimd", "sync")

    SEM_ROLL = 30000

    def __init__(self, nc):
        self.nc = nc
        self.ops = []
        self._semidx = 0
        self.sem_c = {e: self._new_sem() for e in self.ENGS}
        self.sem_d = {e: self._new_sem() for e in ("sync", "gpsimd",
                                                   "scalar")}

    def _new_sem(self):
        self._semidx += 1
        return self.nc.alloc_semaphore(f"sem{self._semidx}")

    def add(self, eng, fn, r=(), w=(), dma=False, force_inc=False):
        op = {"eng": eng, "fn": fn, "r": tuple(r), "w": tuple(w), "dma": dma,
              "need": bool(dma or force_inc), "deps": None, "tok": None}
        self.ops.append(op)
        return op

    def resolve(self):
        lastw = {}
        readers = {}
        for op in self.ops:
            deps = []
            seen = set()

            def dep(p):
                if p is None or p is op or id(p) in seen:
                    return
                if (p["eng"] == op["eng"]) and not p["dma"] \
                        and p["eng"] == "tensor":
                    return
                p["need"] = True
                seen.add(id(p))
                deps.append(p)

            for r in op["r"]:
                dep(lastw.get(r))
            for r in op["w"]:
                dep(lastw.get(r))
                for rd in readers.get(r, ()):
                    dep(rd)
            op["deps"] = deps
            for r in op["r"]:
                readers.setdefault(r, []).append(op)
            for r in op["w"]:
                lastw[r] = op
                readers[r] = []
        cnt = {}
        for op in self.ops:
            if op["need"]:
                pool = self.sem_d if op["dma"] else self.sem_c
                sem = pool[op["eng"]]
                amt = 16 if op["dma"] else 1
                if cnt.get(id(sem), 0) + amt > self.SEM_ROLL:
                    sem = self._new_sem()
                    pool[op["eng"]] = sem
                k = id(sem)
                cnt[k] = cnt.get(k, 0) + amt
                op["tok"] = (sem, cnt[k], amt)

    def emit(self, engname, engobj):
        hw = {}
        for op in self.ops:
            if op["eng"] != engname:
                continue
            best = {}
            for p in op["deps"]:
                sem, val, _ = p["tok"]
                cur = best.get(id(sem))
                if cur is None or cur[1] < val:
                    best[id(sem)] = (sem, val)
            for k, (sem, val) in best.items():
                if hw.get(k, -1) < val:
                    engobj.wait_ge(sem, val)
                    hw[k] = val
            inst = op["fn"](engobj)
            if op["tok"] is not None:
                sem, val, amt = op["tok"]
                inst.then_inc(sem, amt)


def _lhsT_blocks(wT, kdim, mdim):
    assert wT.shape == (kdim, mdim)
    return np.ascontiguousarray(
        wT.reshape(kdim // 128, 128, mdim).transpose(1, 0, 2))


def _prep_shared(inp):
    sh = {}
    sh["identbf"] = np.eye(128, dtype=BF)
    sh["identf"] = np.eye(128, dtype=np.float32)
    jj, ii = np.mgrid[0:128, 0:128]
    sh["triu"] = (jj <= ii).astype(np.float32)
    mb = np.empty((128, 256), np.float32)
    mb[:, 0:128] = np.where(jj <= ii, 0.0, -1e30)
    mb[:, 128:256] = np.where(jj < ii, 0.0, -1e30)
    sh["maskb"] = mb
    sh["m8"] = (-((jj // 8 == ii // 8) & (jj < ii)).astype(np.float32)).astype(BF)
    sh["l0"] = (-((jj // 32 == ii // 32) & (jj // 8 < ii // 8))
                .astype(np.float32)).astype(BF)
    sh["l1"] = (-(jj // 32 < ii // 32).astype(np.float32)).astype(BF)
    sh["ibf"] = np.eye(128, dtype=BF)

    gs = [np.asarray(inp["L0_ffn_sg"], np.float32)] + \
         [np.asarray(inp["L0_ffn_rg"][e], np.float32) for e in range(7)]
    us = [np.asarray(inp["L0_ffn_su"], np.float32)] + \
         [np.asarray(inp["L0_ffn_ru"][e], np.float32) for e in range(7)]
    ds = [np.asarray(inp["L0_ffn_sd"], np.float32)] + \
         [np.asarray(inp["L0_ffn_rd"][e], np.float32) for e in range(7)]
    gu = np.empty((NE, 2, 128, 8, I_FFN), BF)
    dd = np.empty((NE, 128, NMC, D), BF)
    for e in range(NE):
        gu[e, 0] = _lhsT_blocks(np.ascontiguousarray(gs[e].T).astype(BF),
                                D, I_FFN)
        gu[e, 1] = _lhsT_blocks(np.ascontiguousarray(us[e].T).astype(BF),
                                D, I_FFN)
        dd[e] = _lhsT_blocks(np.ascontiguousarray(ds[e].T).astype(BF),
                             I_FFN, D)
    sh["ffn_gu"] = gu
    sh["ffn_d"] = dd
    sh["wr"] = _lhsT_blocks(
        np.ascontiguousarray(np.asarray(inp["L0_ffn_wr"], np.float32).T),
        D, 7)
    sh["rb_col"] = np.asarray(inp["L0_ffn_rb"], np.float32).reshape(7, 1)
    sh["rb_b"] = np.tile(np.asarray(inp["L0_ffn_rb"], np.float32), (128, 1))
    nw = np.asarray(inp["L0_attn_norm_w"], np.float32)
    sh["norm_w"] = np.tile(nw, 2).reshape(128, 1)
    return sh


def _prep_core(c, inp, sh):
    b, r = c // 4, c % 4
    hs = [4 * r + j for j in range(NH)]
    W_in = np.asarray(inp["L0_attn_W_in"], np.float32)
    W_ba = np.asarray(inp["L0_attn_W_ba"], np.float32)
    conv_w = np.asarray(inp["L0_attn_conv_w"], np.float32)[:, 0, :]
    x = np.asarray(inp["L0_attn_x"], np.float32)

    cols = []
    for blk in range(3):
        for h in hs:
            cols.append(W_in[:, blk * KV + 64 * h: blk * KV + 64 * h + 64])
    for h in hs:
        cols.append(W_in[:, 3 * KV + 64 * h: 3 * KV + 64 * h + 64])
    bacols = [W_ba[:, [h]] for h in hs] + [W_ba[:, [H + h]] for h in hs]
    wcat = np.concatenate(cols + bacols, axis=1)
    d = {}
    d["win"] = _lhsT_blocks(wcat.astype(BF), D, 1032)

    rows = []
    for blk in range(3):
        for h in hs:
            rows.append(conv_w[blk * KV + 64 * h: blk * KV + 64 * h + 64, :])
    cwg = np.concatenate(rows, axis=0)
    cd = np.zeros((128, 6, 4, 128), np.float32)
    for cc in range(6):
        for tap in range(4):
            np.fill_diagonal(cd[:, cc, tap, :],
                             cwg[128 * cc:128 * cc + 128, tap])
    d["convd"] = cd.astype(BF)

    W_o = np.asarray(inp["L0_attn_W_o"], np.float32)
    worows = np.concatenate([W_o[64 * h:64 * h + 64, :] for h in hs], axis=0)
    d["wo"] = _lhsT_blocks(worows.astype(BF), 256, D)

    gp = np.zeros((1, 8), np.float32)
    gp[0, 0:4] = np.asarray(inp["L0_attn_dt_bias"], np.float32)[hs]
    gp[0, 4:8] = np.asarray(inp["L0_attn_A_log"], np.float32)[hs]
    d["gatep"] = gp
    d["x_fm"] = np.ascontiguousarray(x[b].T)
    d["x_ffn"] = np.ascontiguousarray(x[b, 4096 * r:4096 * (r + 1)].T)
    cm = np.zeros((128, 4), np.float32)
    cm[:, r] = 1.0
    d["cmb"] = cm
    d.update(sh)
    return d


def build_program(mode="full"):
    nc = bass.Bass("TRN2", target_bir_lowering=False, debug=False,
                   disable_frame_to_traceback=True)
    mixer_on = mode in ("full", "mixer")
    ffn_on = mode in ("full", "ffn")

    dt = {}

    def din(name, shape, ty=F32):
        dt[name] = nc.dram_tensor(name, list(shape), ty, kind="ExternalInput")

    din("identbf", (128, 128), BF16)
    din("identf", (128, 128))
    din("ibf", (128, 128), BF16)
    if mixer_on:
        din("x_fm", (D, T))
        din("win", (128, 8, 1032), BF16)
        din("convd", (128, 6, 4, 128), BF16)
        din("wo", (128, 2, D), BF16)
        din("gatep", (1, 8))
        din("norm_w", (128, 1))
        din("triu", (128, 128))
        din("maskb", (128, 256))
        din("m8", (128, 128), BF16)
        din("l0", (128, 128), BF16)
        din("l1", (128, 128), BF16)
    if ffn_on:
        din("x_ffn", (D, TF))
        din("ffn_gu", (NE, 2, 128, 8, I_FFN), BF16)
        din("ffn_d", (NE, 128, NMC, D), BF16)
        din("wr", (128, 8, 7))
        din("rb_b", (128, 7))
        din("cmb", (128, 4))

    d_sp = nc.dram_tensor("d_sp", [D, T], F32, kind="Internal")
    d_sr = nc.dram_tensor("d_sr", [D, T], F32, kind="Internal")
    d_wb = nc.dram_tensor("d_wb", [7, FT], F32, kind="Internal")
    if mode == "mixer":
        d_out = nc.dram_tensor("out_s", [D, T], F32, kind="ExternalOutput")
    else:
        d_out = nc.dram_tensor("out_fm", [D, TF], F32, kind="ExternalOutput")
    dbg = {}
    if mode == "mixer":
        for nm_, shp, ty in (("mxn", (128, 8, TT), BF16),
                             ("mqkv", (128, 6, TT), BF16),
                             ("msilz", (128, 2 * TT), BF16),
                             ("mgat", (128, 64), F32),
                             ("myt", (128, 2, TT), BF16),
                             ("mamt", (128, 256), BF16),
                             ("mwsol", (128, 128), BF16),
                             ("mktm", (128, 64), BF16),
                             ("mrkv", (128, 128), BF16),
                             ("mqt", (128, 128), BF16)):
            dbg[nm_] = nc.dram_tensor("dbg_" + nm_, list(shp), ty,
                                      kind="ExternalOutput")
    if mode == "ffn":
        dbg["xn"] = nc.dram_tensor("dbg_xn", [128, 8, FT], BF16,
                                   kind="ExternalOutput")
        dbg["ltm"] = nc.dram_tensor("dbg_ltm", [128, 4, 7], F32,
                                    kind="ExternalOutput")
        dbg["wtm"] = nc.dram_tensor("dbg_wtm", [128, 4, 7], F32,
                                    kind="ExternalOutput")
        dbg["acc"] = nc.dram_tensor("dbg_acc", [128, 8, FT], F32,
                                    kind="ExternalOutput")
        dbg["pt"] = nc.dram_tensor("dbg_pt", [128, NMC, FT], BF16,
                                   kind="ExternalOutput")
        dbg["lsb"] = nc.dram_tensor("dbg_lsb", [7, FT], F32,
                                    kind="ExternalOutput")
        dbg["rcol"] = nc.dram_tensor("dbg_rcol", [128, 4], F32,
                                     kind="ExternalOutput")
        dbg["xtmp"] = nc.dram_tensor("dbg_xtmp", [128, 8, FT], F32,
                                     kind="ExternalOutput")
        dbg["rt1"] = nc.dram_tensor("dbg_rt1", [128, 4, 8], F32,
                                    kind="ExternalOutput")
        dbg["rt2"] = nc.dram_tensor("dbg_rt2", [128, 4, 7], F32,
                                    kind="ExternalOutput")

    sc = Sched(nc)
    S = {}
    scope = {"stack": None}

    def sb(name, shape, ty=F32):
        if scope["stack"] is not None:
            S[name] = scope["stack"].enter_context(
                nc.sbuf_tensor("s_" + name, list(shape), ty))
        else:
            S[name] = nc.alloc_sbuf_tensor("s_" + name, list(shape), ty)
        return S[name]

    PQ = [nc.alloc_psum_tensor(f"pq{i}", [128, 1024], F32) for i in range(4)]

    def pb(i):
        return PQ[i // 2][:, 512 * (i % 2):512 * (i % 2) + 512]

    def pbb(i):
        full = PQ[i // 2][:].bitcast(BF16)
        return full[:, 1024 * (i % 2):1024 * (i % 2) + 1024]

    PBR = [f"pb{i}" for i in range(8)]

    sb("identbf", (128, 128), BF16)
    sb("identf", (128, 128))
    sb("ibf", (128, 128), BF16)
    sb("onesf", (128, 1))
    sb("onesb", (128, 1), BF16)
    sb("cbias", (128, 4))

    def dma(eng, dst, src, r=(), w=()):
        return sc.add(eng, lambda e, a=dst, b_=src: e.dma_start(a, b_),
                      r=r, w=w, dma=True)

    for nm in ("identbf", "identf", "ibf"):
        dma("sync", S[nm][:], dt[nm].ap(), w=(nm,))
    sc.add("gpsimd", lambda e: e.memset(S["onesf"][:], 1.0), w=("onesf",))
    sc.add("gpsimd", lambda e: e.memset(S["onesb"][:], 1.0), w=("onesb",))
    sc.add("gpsimd", lambda e: e.memset(S["cbias"][:, 0:1], 1e-6),
           w=("cbias",))
    sc.add("gpsimd", lambda e: e.memset(S["cbias"][:, 1:2], 1e-5),
           w=("cbias",))
    sc.add("gpsimd", lambda e: e.memset(S["cbias"][:, 2:3], 1.0),
           w=("cbias",))
    nc.const_aps.register(F32, 1e-6, S["cbias"][:, 0:1])
    nc.const_aps.register(F32, 1e-5, S["cbias"][:, 1:2])
    nc.const_aps.register(F32, 1.0, S["cbias"][:, 2:3])

    def ones_f32_row(n):
        return bass.AP(S["onesf"], 0, [[1, 1], [0, n]])

    def ones_bf_row(n):
        return bass.AP(S["onesb"], 0, [[1, 1], [0, n]])

    def allones_f32(n):
        return bass.AP(S["onesf"], 0, [[1, 128], [0, n]])

    H_ = dict(nc=nc, sc=sc, S=S, dt=dt, pb=pb, pbb=pbb, sb=sb, dma=dma,
              PBR=PBR, o1r=ones_f32_row, o1b=ones_bf_row, oall=allones_f32,
              dbg=dbg)

    if mixer_on:
        if mode == "full":
            scope["stack"] = ExitStack()
        _build_mixer(H_, d_sp, d_out, mode)
    if mode == "full":
        def cc(e):
            return e.collective_compute(
                "AllReduce", ALU.add,
                replica_groups=[[0, 1, 2, 3], [4, 5, 6, 7]],
                ins=[d_sp.ap().opt()], outs=[d_sr.ap().opt()])
        sc.add("gpsimd", cc, r=tuple(f"d_sp{t}" for t in range(NTILE)),
               w=("d_sr",), force_inc=True)
        _phase_barrier(sc)
        scope["stack"].close()
        scope["stack"] = None
    if ffn_on:
        _build_ffn(H_, d_sr, d_wb, d_out, mode)

    outres = tuple(k for k in _all_written(sc) if k.startswith("OUT"))
    sc.add("sync", lambda e: e.nop(), r=outres)

    sc.resolve()
    with nc.Block() as blk:
        @blk.tensor
        def _(t):
            sc.emit("tensor", t)

        @blk.vector
        def _(v):
            sc.emit("vector", v)

        @blk.scalar
        def _(s):
            sc.emit("scalar", s)

        @blk.gpsimd
        def _(g):
            sc.emit("gpsimd", g)

        @blk.sync
        def _(s):
            sc.emit("sync", s)
    return nc


def _all_written(sc):
    out = set()
    for op in sc.ops:
        out.update(op["w"])
    return out


def _phase_barrier(sc, extra_reads=()):
    engs = Sched.ENGS
    for e in engs:
        sc.add(e, lambda x: x.nop(), r=tuple(extra_reads), w=("bar_" + e,))
    for e in engs:
        sc.add(e, lambda x: x.nop(), r=tuple("bar_" + f for f in engs))


def _bcast_mid(ap2d, n):
    a = ap2d.ap
    return bass.AP(ap2d.tensor, ap2d.offset, [list(a[0]), [0, n], list(a[1])])


def _ln_ops(Hn, xt, xres, xout, xoutres, xbf_name, row_name, rowb_name,
            nseg, width, bmean, bsq, bbm, bbr, xtmp=None, xtmpres=None):
    sc, S, pb, o1b = Hn["sc"], Hn["S"], Hn["pb"], Hn["o1b"]
    add = sc.add
    PBR = Hn["PBR"]
    W = nseg * width
    add("vector", lambda e: e.tensor_copy(
        S[xbf_name][:], xt[:].rearrange("p k t -> p (k t)")),
        r=(xres,), w=(xbf_name,))
    xb3 = S[xbf_name][:].rearrange("p (k t) -> p k t", k=nseg)
    for kk in range(nseg):
        add("tensor", lambda e, kk=kk: e.matmul(
            pb(bmean)[0:1, 0:width], S["onesb"][:, 0:1], xb3[:, kk, :],
            start=(kk == 0), stop=(kk == nseg - 1)),
            r=(xbf_name, "onesb"), w=(PBR[bmean],))
    add("scalar", lambda e: e.activation(
        S[xbf_name][:], S[xbf_name][:], AF.Square),
        r=(xbf_name,), w=(xbf_name,))
    for kk in range(nseg):
        add("tensor", lambda e, kk=kk: e.matmul(
            pb(bsq)[0:1, 0:width], S["onesb"][:, 0:1], xb3[:, kk, :],
            start=(kk == 0), stop=(kk == nseg - 1)),
            r=(xbf_name, "onesb"), w=(PBR[bsq],))
    row = S[row_name]
    add("vector", lambda e: e.tensor_scalar(
        row[0:1, 0:width], pb(bmean)[0:1, 0:width], 1.0 / D, None, ALU.mult),
        r=(PBR[bmean],), w=(row_name,))
    add("vector", lambda e: e.tensor_scalar(
        row[0:1, width:2 * width], pb(bsq)[0:1, 0:width], 1.0 / D, None,
        ALU.mult), r=(PBR[bsq],), w=(row_name,))
    add("scalar", lambda e: e.activation(
        row[0:1, 2 * width:3 * width], row[0:1, 0:width], AF.Square),
        r=(row_name,), w=(row_name,))
    add("vector", lambda e: e.tensor_tensor(
        row[0:1, width:2 * width], row[0:1, width:2 * width],
        row[0:1, 2 * width:3 * width], ALU.subtract),
        r=(row_name,), w=(row_name,))
    add("scalar", lambda e: e.activation(
        row[0:1, width:2 * width], row[0:1, width:2 * width], AF.Ln,
        bias=1e-5), r=(row_name,), w=(row_name,))
    add("scalar", lambda e: e.activation(
        row[0:1, width:2 * width], row[0:1, width:2 * width], AF.Exp,
        scale=-0.5), r=(row_name,), w=(row_name,))
    add("vector", lambda e: e.tensor_copy(
        S[rowb_name][:], row[0:1, 0:2 * width]),
        r=(row_name,), w=(rowb_name,))
    add("tensor", lambda e: e.matmul(
        pb(bbm)[0:128, 0:width], o1b(128), S[rowb_name][0:1, 0:width],
        start=True, stop=True), r=(rowb_name, "onesb"), w=(PBR[bbm],))
    add("tensor", lambda e: e.matmul(
        pb(bbr)[0:128, 0:width], o1b(128),
        S[rowb_name][0:1, width:2 * width],
        start=True, stop=True), r=(rowb_name, "onesb"), w=(PBR[bbr],))
    m_b = _bcast_mid(pb(bbm)[0:128, 0:width], nseg)
    r_b = _bcast_mid(pb(bbr)[0:128, 0:width], nseg)
    tgt = S[xtmp] if xtmp else xt
    tres = xtmpres if xtmp else xres
    add("vector", lambda e, tgt=tgt: e.tensor_tensor(
        tgt[:], xt[:], m_b, ALU.subtract), r=(xres, PBR[bbm]), w=(tres,))
    add("vector", lambda e, tgt=tgt: e.tensor_tensor(
        xout[:], tgt[:], r_b, ALU.mult), r=(tres, PBR[bbr]), w=(xoutres,))


def _build_mixer(Hn, d_sp, d_out, mode):
    nc, sc, S, dt, pb, pbb, sb, dma, PBR = (
        Hn["nc"], Hn["sc"], Hn["S"], Hn["dt"], Hn["pb"], Hn["pbb"], Hn["sb"],
        Hn["dma"], Hn["PBR"])
    o1b, oall = Hn["o1b"], Hn["oall"]
    add = sc.add

    for nm_, shp, ty in (("win", (128, 8, 1032), BF16),
                         ("convd", (128, 6, 4, 128), BF16),
                         ("wo", (128, 2, D), BF16),
                         ("triu", (128, 128), F32),
                         ("maskb", (128, 256), F32),
                         ("m8", (128, 128), BF16),
                         ("l0", (128, 128), BF16),
                         ("l1", (128, 128), BF16),
                         ("gatep", (1, 8), F32)):
        sb(nm_, shp, ty)
        dma("sync", S[nm_][:], dt[nm_].ap(), w=(nm_,))
    sb("normw", (128, 1))
    dma("sync", S["normw"][:], dt["norm_w"].ap(), w=("normw",))
    sb("dtb", (128, 4)); sb("negA", (128, 4))

    sb("xt0", (128, 8, TT)); sb("xt1", (128, 8, TT))
    sb("xbf", (128, 8 * TT), BF16)
    sb("xnbf", (128, 8, TT), BF16)
    sb("rowln", (1, 3 * TT))
    sb("rowlnb", (1, 2 * TT), BF16)
    sb("preconv", (128, 6, TT + 3), BF16)
    sb("qkvt", (128, 6, TT), BF16)
    sb("zt", (128, 2 * TT), BF16)
    sb("silz", (128, 2 * TT), BF16)
    sb("barow", (8, TT))
    sb("bat", (128, 4, 8))
    sb("beta", (128, 16)); sb("lnb", (128, 16)); sb("gdec", (128, 16))
    sb("gtmp", (128, 16)); sb("gtmp2", (128, 16))
    sb("bcol", (128, 16)); sb("bCb", (128, 16)); sb("expbC", (128, 16))
    sb("ytile", (128, 2, TT), BF16)
    sb("ssb0", (128, 8, TT)); sb("ssb1", (128, 8, TT))
    sb("Ssb", (128, 128), BF16)
    for s in range(16):
        sb(f"amt{s}", (128, 256), BF16)
        sb(f"wsol{s}", (128, 128), BF16)
        sb(f"ktm{s}", (128, 64), BF16)
        sb(f"rkv{s}", (128, 128), BF16)
        sb(f"qt{s}", (128, 128), BF16)
    for s in range(2):
        sb(f"ktmr{s}", (128, 64), BF16)
        sb(f"vtmr{s}", (128, 64), BF16)
        sb(f"rows{s}", (2, 384))
        sb(f"rowe{s}", (1, 128), BF16)
        sb(f"cols{s}", (128, 6))
        sb(f"dcl{s}", (128, 3))
        sb(f"lnr{s}", (128, 2))
        sb(f"ecol{s}", (128, 2))
        sb(f"btmp{s}", (128, 256))
    for p in range(2):
        sb(f"xh{p}", (128, 128), BF16)
        sb(f"g{p}", (128, 128), BF16)
        sb(f"eneg{p}", (128, 128), BF16)
        sb(f"t32{p}", (128, 128), BF16)
        sb(f"t32T{p}", (128, 128), BF16)
        sb(f"td8T{p}", (128, 128), BF16)
        sb(f"hh{p}", (128, 128), BF16)
        sb(f"d8m{p}", (128, 128), BF16)
        sb(f"l0m{p}", (128, 128), BF16)
        sb(f"l1m{p}", (128, 128), BF16)
        sb(f"cdl{p}", (128, 64), BF16)
        sb(f"fsb{p}", (128, 128), BF16)
        sb(f"o2{p}", (128, 128), BF16)
        sb(f"rstf{p}", (1, 128))
        sb(f"rst{p}", (1, 128), BF16)
        sb(f"ytmp{p}", (128, 128))

    add("tensor", lambda e: e.matmul(
        pb(0)[0:128, 0:8], Hn["o1r"](128), S["gatep"][:],
        start=True, stop=True), r=("gatep", "onesf"), w=("pb0",))
    add("vector", lambda e: e.tensor_copy(S["dtb"][:], pb(0)[0:128, 0:4]),
        r=("pb0",), w=("dtb",))
    add("vector", lambda e: e.tensor_scalar(
        S["negA"][:], pb(0)[0:128, 4:8], -2.3, None, ALU.max),
        r=("pb0",), w=("negA",))
    add("scalar", lambda e: e.activation(S["negA"][:], S["negA"][:], AF.Exp),
        r=("negA",), w=("negA",))
    add("vector", lambda e: e.tensor_scalar(
        S["negA"][:], S["negA"][:], -1.0, None, ALU.mult),
        r=("negA",), w=("negA",))
    add("gpsimd", lambda e: e.memset(S["Ssb"][:], 0.0),
        w=("Ssb0", "Ssb1", "Ssb2", "Ssb3"))
    add("gpsimd", lambda e: e.memset(S["preconv"][:, :, 0:3], 0.0),
        w=("halo",))

    x_fm = dt["x_fm"].ap().rearrange("(k p) t -> p k t", p=128)
    if mode == "mixer":
        out_v = d_out.ap().rearrange("(k p) t -> p k t", p=128)
    else:
        out_v = d_sp.ap().rearrange("(k p) t -> p k t", p=128)

    for t in range(NTILE):
        xt = S["xt0"] if t % 2 == 0 else S["xt1"]
        xres = f"xt{t % 2}"
        ssb = S["ssb0"] if t % 2 == 0 else S["ssb1"]
        ssr = f"ssb{t % 2}"
        dma("sync", xt[:], x_fm[:, :, TT * t:TT * (t + 1)], w=(xres,))

        _ln_ops(Hn, xt, xres, S["xnbf"], "xnbf", "xbf", "rowln", "rowlnb",
                8, TT, 0, 1, 2, 3)

        for mc in range(9):
            cols = 128 if mc < 8 else 8
            bank = 4 + mc % 2
            for kk in range(8):
                add("tensor", lambda e, mc=mc, kk=kk, cols=cols, bank=bank:
                    e.matmul(pb(bank)[0:cols, 0:TT],
                             S["win"][:, kk, 128 * mc:128 * mc + cols],
                             S["xnbf"][:, kk, :],
                             start=(kk == 0), stop=(kk == 7)),
                    r=("win", "xnbf"), w=(PBR[bank],))
            if mc < 6:
                add("scalar", lambda e, mc=mc, bank=bank: e.copy(
                    S["preconv"][:, mc, 3:3 + TT], pb(bank)[0:128, 0:TT]),
                    r=(PBR[bank],), w=("preconv",))
            elif mc < 8:
                add("scalar", lambda e, mc=mc, bank=bank: e.copy(
                    S["zt"][:, TT * (mc - 6):TT * (mc - 5)],
                    pb(bank)[0:128, 0:TT]), r=(PBR[bank],), w=("zt",))
            else:
                add("vector", lambda e, bank=bank: e.tensor_copy(
                    S["barow"][:], pb(bank)[0:8, 0:TT]),
                    r=(PBR[bank],), w=("barow",))

        for cc in range(6):
            bank = 6 + cc % 2
            for tap in range(4):
                add("tensor", lambda e, cc=cc, tap=tap, bank=bank: e.matmul(
                    pb(bank)[0:128, 0:TT], S["convd"][:, cc, tap, :],
                    S["preconv"][:, cc, tap:tap + TT],
                    start=(tap == 0), stop=(tap == 3)),
                    r=("convd", "preconv", "halo"), w=(PBR[bank],))
            add("scalar", lambda e, cc=cc, bank=bank: e.activation(
                S["qkvt"][:, cc, :], pb(bank)[0:128, 0:TT], AF.Silu),
                r=(PBR[bank],), w=("qkvt",))
        add("scalar", lambda e: e.activation(
            S["silz"][:], S["zt"][:], AF.Silu), r=("zt",), w=("silz",))
        add("vector", lambda e: e.tensor_copy(
            S["preconv"][:, :, 0:3], S["preconv"][:, :, TT:TT + 3]),
            r=("preconv",), w=("halo",))

        for c4 in range(4):
            add("tensor", lambda e, c4=c4: e.transpose(
                pb(0)[0:128, 0:8],
                S["barow"][0:8, 128 * c4:128 * c4 + 128],
                S["identf"][0:8, 0:8]), r=("barow", "identf"), w=("pb0",))
            add("vector", lambda e, c4=c4: e.tensor_copy(
                S["bat"][:, c4, :], pb(0)[0:128, 0:8]),
                r=("pb0",), w=("bat",))
        b_all = S["bat"][:, :, 0:4]
        a_all = S["bat"][:, :, 4:8]
        beta = S["beta"][:].rearrange("p (c h) -> p c h", c=4)
        lnb = S["lnb"][:].rearrange("p (c h) -> p c h", c=4)
        gdec = S["gdec"][:].rearrange("p (c h) -> p c h", c=4)
        g1 = S["gtmp"][:].rearrange("p (c h) -> p c h", c=4)
        g2 = S["gtmp2"][:].rearrange("p (c h) -> p c h", c=4)
        dtb_b = _bcast_mid(S["dtb"][:], 4)
        negA_b = _bcast_mid(S["negA"][:], 4)
        add("scalar", lambda e: e.activation(g1, b_all, AF.Exp, scale=-1.0),
            r=("bat",), w=("gtmp",))
        add("vector", lambda e: e.tensor_scalar(
            g2, g1, 1.0, None, ALU.add), r=("gtmp",), w=("gtmp2",))
        add("vector", lambda e: e.reciprocal(beta, g2),
            r=("gtmp2",), w=("beta",))
        add("scalar", lambda e: e.activation(g1, b_all, AF.Abs),
            r=("bat",), w=("gtmp",))
        add("scalar", lambda e: e.activation(g1, g1, AF.Exp, scale=-1.0),
            r=("gtmp",), w=("gtmp",))
        add("scalar", lambda e: e.activation(g1, g1, AF.Ln, bias=1.0),
            r=("gtmp",), w=("gtmp",))
        add("vector", lambda e: e.tensor_scalar(
            g2, b_all, 0.0, None, ALU.min), r=("bat",), w=("gtmp2",))
        add("vector", lambda e: e.tensor_tensor(lnb, g2, g1, ALU.subtract),
            r=("gtmp", "gtmp2"), w=("lnb",))
        add("vector", lambda e: e.tensor_tensor(g1, a_all, dtb_b, ALU.add),
            r=("bat", "dtb"), w=("gtmp",))
        add("scalar", lambda e: e.activation(g2, g1, AF.Abs),
            r=("gtmp",), w=("gtmp2",))
        add("scalar", lambda e: e.activation(g2, g2, AF.Exp, scale=-1.0),
            r=("gtmp2",), w=("gtmp2",))
        add("scalar", lambda e: e.activation(g2, g2, AF.Ln, bias=1.0),
            r=("gtmp2",), w=("gtmp2",))
        add("vector", lambda e: e.tensor_scalar(
            g1, g1, 0.0, None, ALU.max), r=("gtmp",), w=("gtmp",))
        add("vector", lambda e: e.tensor_tensor(g1, g1, g2, ALU.add),
            r=("gtmp", "gtmp2"), w=("gtmp",))
        add("vector", lambda e: e.tensor_tensor(gdec, g1, negA_b, ALU.mult),
            r=("gtmp", "negA"), w=("gdec",))
        for c4 in range(4):
            add("tensor", lambda e, c4=c4: e.matmul(
                pb(1)[0:128, 0:4], S["triu"][:],
                S["gdec"][:, 4 * c4:4 * c4 + 4], start=True, stop=True),
                r=("triu", "gdec"), w=("pb1",))
            add("vector", lambda e, c4=c4: e.tensor_copy(
                S["bcol"][:, 4 * c4:4 * c4 + 4], pb(1)[0:128, 0:4]),
                r=("pb1",), w=("bcol",))
            add("tensor", lambda e, c4=c4: e.matmul(
                pb(2)[0:128, 0:4], oall(128),
                S["gdec"][:, 4 * c4:4 * c4 + 4], start=True, stop=True),
                r=("onesf", "gdec"), w=("pb2",))
            add("vector", lambda e, c4=c4: e.tensor_copy(
                S["bCb"][:, 4 * c4:4 * c4 + 4], pb(2)[0:128, 0:4]),
                r=("pb2",), w=("bCb",))
        add("scalar", lambda e: e.activation(
            S["expbC"][:], S["bCb"][:], AF.Exp), r=("bCb",), w=("expbC",))

        def drive_seq(g):
            for _ in g:
                pass
        for c4 in range(4):
            for h in range(NH):
                drive_seq(_scan_waveA(Hn, c4, h))
        for sl in range(16):
            drive_seq(_scan_solve(Hn, sl))
        for c4 in range(4):
            for h in range(NH):
                _scan_waveC(Hn, c4, h)

        for mh in range(8):
            bank = mh % 2
            for kk in range(2):
                add("tensor", lambda e, mh=mh, kk=kk, bank=bank: e.matmul(
                    pb(bank)[0:128, 0:TT],
                    S["wo"][:, kk, 128 * mh:128 * mh + 128],
                    S["ytile"][:, kk, :], start=(kk == 0), stop=(kk == 1)),
                    r=("wo", "ytile"), w=(PBR[bank],))
            add("scalar", lambda e, mh=mh, bank=bank, ssb=ssb: e.copy(
                ssb[:, mh, :], pb(bank)[0:128, 0:TT]),
                r=(PBR[bank],), w=(ssr,))
        wtag = (f"OUTs{t}",) if mode == "mixer" else (f"d_sp{t}",)
        dma("gpsimd", out_v[:, :, TT * t:TT * (t + 1)], ssb[:],
            r=(ssr,), w=wtag)
        if t == 0 and mode == "mixer" and Hn.get("dbg"):
            dbgt = Hn["dbg"]
            dma("gpsimd", dbgt["mxn"].ap(), S["xnbf"][:], r=("xnbf",),
                w=("OUTd0",))
            dma("gpsimd", dbgt["mqkv"].ap(), S["qkvt"][:], r=("qkvt",),
                w=("OUTd1",))
            dma("gpsimd", dbgt["msilz"].ap(), S["silz"][:], r=("silz",),
                w=("OUTd2",))
            dma("gpsimd", dbgt["mgat"].ap()[:, 0:16], S["beta"][:],
                r=("beta",), w=("OUTd3",))
            dma("gpsimd", dbgt["mgat"].ap()[:, 16:32], S["lnb"][:],
                r=("lnb",), w=("OUTd4",))
            dma("gpsimd", dbgt["mgat"].ap()[:, 32:48], S["gdec"][:],
                r=("gdec",), w=("OUTd5",))
            dma("gpsimd", dbgt["mgat"].ap()[:, 48:64], S["bcol"][:],
                r=("bcol",), w=("OUTd6",))
            dma("gpsimd", dbgt["myt"].ap(), S["ytile"][:], r=("ytile",),
                w=("OUTd7",))
            dma("gpsimd", dbgt["mamt"].ap(), S["amt0"][:], r=("amt0",),
                w=("OUTd8",))
            dma("gpsimd", dbgt["mwsol"].ap(), S["wsol0"][:], r=("wsol0",),
                w=("OUTd9",))
            dma("gpsimd", dbgt["mktm"].ap(), S["ktm0"][:], r=("ktm0",),
                w=("OUTd10",))
            dma("gpsimd", dbgt["mrkv"].ap(), S["rkv0"][:], r=("rkv0",),
                w=("OUTd11",))
            dma("gpsimd", dbgt["mqt"].ap(), S["qt0"][:], r=("qt0",),
                w=("OUTd12",))


def _scan_waveA(Hn, c4, h):
    sc, S, pb, pbb, PBR, o1b = (Hn["sc"], Hn["S"], Hn["pb"], Hn["pbb"],
                                Hn["PBR"], Hn["o1b"])
    add = sc.add
    sl = 4 * c4 + h
    ss = str(sl)
    sp = str(sl % 2)
    hb = 64 * (h % 2)
    qs = S["qkvt"][hb:hb + 64, h // 2, 128 * c4:128 * c4 + 128]
    ks = S["qkvt"][hb:hb + 64, 2 + h // 2, 128 * c4:128 * c4 + 128]
    vs = S["qkvt"][hb:hb + 64, 4 + h // 2, 128 * c4:128 * c4 + 128]
    idsl = S["identbf"][hb:hb + 64, hb:hb + 64]
    pset = sl % 2
    bp, bd = (0, 1) if pset == 0 else (2, 3)

    add("tensor", lambda e: e.transpose(pbb(bp)[0:128, 768:832], ks, idsl),
        r=("qkvt", "identbf"), w=(PBR[bp],))
    add("vector", lambda e: e.tensor_copy(
        S["ktmr" + sp][:], pbb(bp)[0:128, 768:832]),
        r=(PBR[bp],), w=("ktmr" + sp,))
    add("tensor", lambda e: e.transpose(pbb(bp)[0:128, 896:960], vs, idsl),
        r=("qkvt", "identbf"), w=(PBR[bp],))
    add("vector", lambda e: e.tensor_copy(
        S["vtmr" + sp][:], pbb(bp)[0:128, 896:960]),
        r=(PBR[bp],), w=("vtmr" + sp,))

    yield
    add("tensor", lambda e: e.matmul(pb(bp)[0:128, 0:128], ks, qs,
                                     start=True, stop=False),
        r=("qkvt",), w=(PBR[bp],))
    add("tensor", lambda e: e.matmul(pb(bp)[0:128, 128:256], ks, ks,
                                     start=False, stop=False),
        r=("qkvt",), w=(PBR[bp],))
    add("tensor", lambda e: e.matmul(pb(bp)[0:128, 256:384], qs, qs,
                                     start=False, stop=True),
        r=("qkvt",), w=(PBR[bp],))
    yield
    diag_in = pb(bp)[0:128, 128:384].rearrange("p (c x) -> p c x", c=2)
    ident2 = _bcast_mid(S["identf"][:], 2).rearrange("p c x -> p c x") \
        if False else bass.AP(S["identf"], 0, [[128, 128], [0, 2], [1, 128]])
    add("vector", lambda e: e.tensor_tensor(
        S["btmp" + sp][:].rearrange("p (c x) -> p c x", c=2), diag_in,
        ident2, ALU.mult), r=(PBR[bp], "identf"), w=("btmp" + sp,))
    add("vector", lambda e: e.reduce_sum(
        S["lnr" + sp][:], S["btmp" + sp][:].rearrange(
            "p (c x) -> p c x", c=2), axis=AX.X),
        r=("btmp" + sp,), w=("lnr" + sp,))
    add("scalar", lambda e: e.activation(
        S["lnr" + sp][:], S["lnr" + sp][:], AF.Ln, bias=1e-6),
        r=("lnr" + sp,), w=("lnr" + sp,))
    yield
    bcolh = S["bcol"][:, 4 * c4 + h:4 * c4 + h + 1]
    lnrk = S["lnr" + sp][:, 0:1]
    lnrq = S["lnr" + sp][:, 1:2]
    lnbh = S["lnb"][:, 4 * c4 + h:4 * c4 + h + 1]
    bCh = S["bCb"][:, 4 * c4 + h:4 * c4 + h + 1]
    beth = S["beta"][:, 4 * c4 + h:4 * c4 + h + 1]
    dcl = S["dcl" + sp]
    add("vector", lambda e: e.scalar_tensor_tensor(
        dcl[:, 0:1], lnrk, 0.5, bcolh, ALU.mult, ALU.add),
        r=("lnr" + sp, "bcol"), w=("dcl" + sp,))
    add("vector", lambda e: e.tensor_scalar(
        dcl[:, 0:1], dcl[:, 0:1], -1.0, None, ALU.mult),
        r=("dcl" + sp,), w=("dcl" + sp,))
    add("vector", lambda e: e.scalar_tensor_tensor(
        dcl[:, 1:2], lnrq, -0.5, bcolh, ALU.mult, ALU.add),
        r=("lnr" + sp, "bcol"), w=("dcl" + sp,))
    add("vector", lambda e: e.tensor_scalar(
        dcl[:, 1:2], dcl[:, 1:2], LN8, None, ALU.add),
        r=("dcl" + sp,), w=("dcl" + sp,))
    add("vector", lambda e: e.scalar_tensor_tensor(
        dcl[:, 2:3], lnrk, -1.0, lnbh, ALU.mult, ALU.add),
        r=("lnr" + sp, "lnb"), w=("dcl" + sp,))
    add("vector", lambda e: e.tensor_tensor(
        dcl[:, 2:3], dcl[:, 2:3], dcl[:, 0:1], ALU.subtract),
        r=("dcl" + sp,), w=("dcl" + sp,))
    yield
    add("scalar", lambda e: e.activation(
        S["ecol" + sp][:, 0:1], dcl[:, 2:3], AF.Exp),
        r=("dcl" + sp,), w=("ecol" + sp,))
    add("vector", lambda e: e.tensor_tensor(
        S["ecol" + sp][:, 1:2], bCh, dcl[:, 0:1], ALU.add),
        r=("bCb", "dcl" + sp), w=("ecol" + sp,))
    add("scalar", lambda e: e.activation(
        S["ecol" + sp][:, 1:2], S["ecol" + sp][:, 1:2], AF.Exp),
        r=("ecol" + sp,), w=("ecol" + sp,))
    yield
    add("vector", lambda e: e.tensor_scalar(
        S["ktm" + ss][:], S["ktmr" + sp][:], S["ecol" + sp][:, 1:2], None,
        ALU.mult), r=("ktmr" + sp, "ecol" + sp), w=("ktm" + ss,))
    add("vector", lambda e: e.tensor_scalar(
        S["rkv" + ss][:, 0:64], S["vtmr" + sp][:], beth, None, ALU.mult),
        r=("vtmr" + sp, "beta"), w=("rkv" + ss,))
    add("vector", lambda e: e.tensor_scalar(
        S["rkv" + ss][:, 64:128], S["ktmr" + sp][:], S["ecol" + sp][:, 0:1],
        None, ALU.mult), r=("ktmr" + sp, "ecol" + sp), w=("rkv" + ss,))
    yield
    cl = S["cols" + sp]
    add("vector", lambda e: e.tensor_scalar(
        cl[:, 0:1], dcl[:, 0:1], 0.0, 1.0, ALU.mult, ALU.add),
        r=("dcl" + sp,), w=("cols" + sp,))
    add("vector", lambda e: e.tensor_copy(cl[:, 1:2], dcl[:, 0:1]),
        r=("dcl" + sp,), w=("cols" + sp,))
    add("vector", lambda e: e.tensor_copy(cl[:, 2:3], dcl[:, 1:2]),
        r=("dcl" + sp,), w=("cols" + sp,))
    add("vector", lambda e: e.tensor_scalar(
        cl[:, 3:4], dcl[:, 0:1], 0.0, 1.0, ALU.mult, ALU.add),
        r=("dcl" + sp,), w=("cols" + sp,))
    add("vector", lambda e: e.tensor_copy(cl[:, 4:5], dcl[:, 2:3]),
        r=("dcl" + sp,), w=("cols" + sp,))
    add("vector", lambda e: e.tensor_scalar(
        cl[:, 5:6], dcl[:, 0:1], 0.0, 1.0, ALU.mult, ALU.add),
        r=("dcl" + sp,), w=("cols" + sp,))
    yield
    for pi in range(3):
        add("tensor", lambda e, pi=pi: e.transpose(
            pb(bd)[0:2, 256:384], cl[:, 2 * pi:2 * pi + 2], S["identf"][:]),
            r=("cols" + sp, "identf"), w=(PBR[bd],))
        add("vector", lambda e, pi=pi: e.tensor_copy(
            S["rows" + sp][0:2, 128 * pi:128 * pi + 128],
            pb(bd)[0:2, 256:384]), r=(PBR[bd],), w=("rows" + sp,))
    yield
    add("tensor", lambda e: e.matmul(
        pb(bd)[0:128, 0:128], S["rows" + sp][0:2, 0:128],
        S["rows" + sp][0:2, 128:256], start=True, stop=False),
        r=("rows" + sp,), w=(PBR[bd],))
    add("tensor", lambda e: e.matmul(
        pb(bd)[0:128, 128:256], S["rows" + sp][0:2, 0:128],
        S["rows" + sp][0:2, 256:384], start=False, stop=True),
        r=("rows" + sp,), w=(PBR[bd],))
    yield
    add("vector", lambda e: e.tensor_tensor(
        S["btmp" + sp][:], pb(bd)[0:128, 0:256], S["maskb"][:], ALU.add),
        r=(PBR[bd], "maskb"), w=("btmp" + sp,))
    add("scalar", lambda e: e.activation(
        S["btmp" + sp][:], S["btmp" + sp][:], AF.Exp),
        r=("btmp" + sp,), w=("btmp" + sp,))
    add("vector", lambda e: e.tensor_tensor(
        S["amt" + ss][:], S["btmp" + sp][:], pb(bp)[0:128, 0:256], ALU.mult),
        r=("btmp" + sp, PBR[bp]), w=("amt" + ss,))
    yield
    add("scalar", lambda e: e.activation(
        S["rowe" + sp][:], S["rows" + sp][0:1, 128:256], AF.Exp),
        r=("rows" + sp,), w=("rowe" + sp,))
    add("tensor", lambda e: e.matmul(
        pb(bd)[hb:hb + 64, 384:512], o1b(64), S["rowe" + sp][:],
        start=True, stop=True, tile_position=(0, hb)),
        r=("rowe" + sp, "onesb"), w=(PBR[bd],))
    add("vector", lambda e: e.tensor_tensor(
        S["qt" + ss][hb:hb + 64, :], qs, pb(bd)[hb:hb + 64, 384:512],
        ALU.mult), r=("qkvt", PBR[bd]), w=("qt" + ss,))


def _scan_solve(Hn, sl):
    sc, S, pb, pbb, PBR = Hn["sc"], Hn["S"], Hn["pb"], Hn["pbb"], Hn["PBR"]
    add = sc.add
    ss = str(sl)
    p = sl % 2
    bs = 4 + p
    amtM = S["amt" + ss][:, 128:256]
    add("vector", lambda e: e.tensor_tensor(
        S[f"d8m{p}"][:], amtM, S["m8"][:], ALU.mult),
        r=("amt" + ss, "m8"), w=(f"d8m{p}",))
    add("vector", lambda e: e.tensor_tensor(
        S[f"l0m{p}"][:], amtM, S["l0"][:], ALU.mult),
        r=("amt" + ss, "l0"), w=(f"l0m{p}",))
    add("vector", lambda e: e.tensor_tensor(
        S[f"l1m{p}"][:], amtM, S["l1"][:], ALU.mult),
        r=("amt" + ss, "l1"), w=(f"l1m{p}",))
    for it in range(7):
        rhs = S["identbf"] if it == 0 else S[f"xh{p}"]
        rr = "identbf" if it == 0 else f"xh{p}"
        add("tensor", lambda e, rhs=rhs: e.matmul(
            pb(bs)[0:128, 0:128], S[f"d8m{p}"][:], rhs[:],
            start=True, stop=False), r=(f"d8m{p}", rr), w=(PBR[bs],))
        add("tensor", lambda e: e.matmul(
            pb(bs)[0:128, 0:128], S["identbf"][:], S["identbf"][:],
            start=False, stop=True), r=("identbf",), w=(PBR[bs],))
        if it % 2 == 0:
            add("vector", lambda e: e.tensor_copy(
                S[f"xh{p}"][:], pb(bs)[0:128, 0:128]),
                r=(PBR[bs],), w=(f"xh{p}",))
        else:
            add("scalar", lambda e: e.copy(
                S[f"xh{p}"][:], pb(bs)[0:128, 0:128]),
                r=(PBR[bs],), w=(f"xh{p}",))
        yield
    add("tensor", lambda e: e.matmul(
        pb(bs)[0:128, 0:128], S[f"xh{p}"][:], S[f"l0m{p}"][:],
        start=True, stop=True), r=(f"xh{p}", f"l0m{p}"), w=(PBR[bs],))
    add("vector", lambda e: e.tensor_copy(
        S[f"eneg{p}"][:], pb(bs)[0:128, 0:128]),
        r=(PBR[bs],), w=(f"eneg{p}",))
    for it in range(3):
        rhs = S["identbf"] if it == 0 else S[f"g{p}"]
        rr = "identbf" if it == 0 else f"g{p}"
        add("tensor", lambda e, rhs=rhs: e.matmul(
            pb(bs)[0:128, 0:128], S[f"eneg{p}"][:], rhs[:],
            start=True, stop=False), r=(f"eneg{p}", rr), w=(PBR[bs],))
        add("tensor", lambda e: e.matmul(
            pb(bs)[0:128, 0:128], S["identbf"][:], S["identbf"][:],
            start=False, stop=True), r=("identbf",), w=(PBR[bs],))
        add("scalar", lambda e: e.copy(S[f"g{p}"][:], pb(bs)[0:128, 0:128]),
            r=(PBR[bs],), w=(f"g{p}",))
        yield
    add("tensor", lambda e: e.transpose(
        pbb(bs)[0:128, 512:640], S[f"xh{p}"][:], S["identbf"][:]),
        r=(f"xh{p}", "identbf"), w=(PBR[bs],))
    add("vector", lambda e: e.tensor_copy(
        S[f"td8T{p}"][:], pbb(bs)[0:128, 512:640]),
        r=(PBR[bs],), w=(f"td8T{p}",))
    yield
    add("tensor", lambda e: e.matmul(
        pb(bs)[0:128, 0:128], S[f"td8T{p}"][:], S[f"g{p}"][:],
        start=True, stop=True), r=(f"td8T{p}", f"g{p}"), w=(PBR[bs],))
    add("scalar", lambda e: e.copy(S[f"t32{p}"][:], pb(bs)[0:128, 0:128]),
        r=(PBR[bs],), w=(f"t32{p}",))
    yield
    add("tensor", lambda e: e.transpose(
        pbb(bs)[0:128, 512:640], S[f"t32{p}"][:], S["identbf"][:]),
        r=(f"t32{p}", "identbf"), w=(PBR[bs],))
    add("vector", lambda e: e.tensor_copy(
        S[f"t32T{p}"][:], pbb(bs)[0:128, 512:640]),
        r=(PBR[bs],), w=(f"t32T{p}",))
    yield
    add("tensor", lambda e: e.matmul(
        pb(bs)[0:128, 0:128], S[f"t32T{p}"][:], S["rkv" + ss][:],
        start=True, stop=True), r=(f"t32T{p}", "rkv" + ss), w=(PBR[bs],))
    add("vector", lambda e: e.tensor_copy(
        S[f"hh{p}"][:], pb(bs)[0:128, 0:128]), r=(PBR[bs],), w=(f"hh{p}",))
    yield
    for it in range(3):
        add("tensor", lambda e: e.matmul(
            pb(bs)[0:128, 0:128], S[f"l1m{p}"][:], S[f"hh{p}"][:],
            start=True, stop=False), r=(f"l1m{p}", f"hh{p}"), w=(PBR[bs],))
        add("tensor", lambda e: e.matmul(
            pb(bs)[0:128, 0:128], S["identbf"][:], S["rkv" + ss][:],
            start=False, stop=True), r=("identbf", "rkv" + ss),
            w=(PBR[bs],))
        add("scalar", lambda e: e.copy(S[f"hh{p}"][:], pb(bs)[0:128, 0:128]),
            r=(PBR[bs],), w=(f"hh{p}",))
        yield
        add("tensor", lambda e: e.matmul(
            pb(bs)[0:128, 0:128], S[f"t32T{p}"][:], S[f"hh{p}"][:],
            start=True, stop=True), r=(f"t32T{p}", f"hh{p}"), w=(PBR[bs],))
        if it < 2:
            add("vector", lambda e: e.tensor_copy(
                S[f"hh{p}"][:], pb(bs)[0:128, 0:128]),
                r=(PBR[bs],), w=(f"hh{p}",))
        yield
    add("vector", lambda e: e.tensor_copy(
        S["wsol" + ss][:], pb(bs)[0:128, 0:128]),
        r=(PBR[bs],), w=("wsol" + ss,))


def _scan_waveC(Hn, c4, h):
    sc, S, pb, pbb, PBR, o1b = (Hn["sc"], Hn["S"], Hn["pb"], Hn["pbb"],
                                Hn["PBR"], Hn["o1b"])
    add = sc.add
    sl = 4 * c4 + h
    ss = str(sl)
    p = sl % 2
    hb = 64 * (h % 2)
    hf = 64 * (h // 2)
    Ssl = S["Ssb"][hb:hb + 64, hf:hf + 64]
    Sres = f"Ssb{h}"
    bO = 7 if p == 0 else 2
    bF = 2 if p == 0 else 7
    ech = S["expbC"][hb:hb + 64, 4 * c4 + h:4 * c4 + h + 1]

    add("tensor", lambda e: e.matmul(
        pb(6)[hb:hb + 64, 0:64], S["ktm" + ss][:], S["wsol" + ss][:, 0:64],
        start=True, stop=False, tile_position=(0, hb)),
        r=("ktm" + ss, "wsol" + ss), w=("pb6",))
    add("tensor", lambda e: e.matmul(
        pb(bF)[hb:hb + 64, 0:128], S["wsol" + ss][:, 64:128],
        S["amt" + ss][:, 0:128], start=True, stop=False,
        tile_position=(0, hb)),
        r=("wsol" + ss, "amt" + ss), w=(PBR[bF],))
    add("tensor", lambda e: e.matmul(
        pb(bF)[hb:hb + 64, 128:192], S["wsol" + ss][:, 64:128],
        S["ktm" + ss][:], start=False, stop=True, tile_position=(0, hb)),
        r=("wsol" + ss, "ktm" + ss), w=(PBR[bF],))
    add("vector", lambda e: e.tensor_scalar(
        S[f"fsb{p}"][hb:hb + 64, :], pb(bF)[hb:hb + 64, 0:128], -1.0, None,
        ALU.mult), r=(PBR[bF],), w=(f"fsb{p}",))
    add("vector", lambda e: e.scalar_tensor_tensor(
        S[f"cdl{p}"][hb:hb + 64, :], S["ibf"][hb:hb + 64, hb:hb + 64], ech,
        pb(bF)[hb:hb + 64, 128:192], ALU.mult, ALU.subtract),
        r=("ibf", "expbC", PBR[bF]), w=(f"cdl{p}",))
    add("tensor", lambda e: e.matmul(
        pb(bO)[hb:hb + 64, 0:128], S["wsol" + ss][:, 0:64],
        S["amt" + ss][:, 0:128], start=True, stop=False,
        tile_position=(0, hb)),
        r=("wsol" + ss, "amt" + ss), w=(PBR[bO],))
    add("tensor", lambda e: e.matmul(
        pb(bO)[hb:hb + 64, 0:128], Ssl, S[f"fsb{p}"][hb:hb + 64, :],
        start=False, stop=False, tile_position=(hb, hb)),
        r=(Sres, f"fsb{p}"), w=(PBR[bO],))
    add("tensor", lambda e: e.matmul(
        pb(bO)[hb:hb + 64, 0:128], Ssl, S["qt" + ss][hb:hb + 64, :],
        start=False, stop=True, tile_position=(hb, hb)),
        r=(Sres, "qt" + ss), w=(PBR[bO],))
    add("tensor", lambda e: e.matmul(
        pb(6)[hb:hb + 64, 0:64], S[f"cdl{p}"][hb:hb + 64, :], Ssl,
        start=False, stop=True, tile_position=(hb, hb)),
        r=(f"cdl{p}", Sres), w=("pb6",))
    add("vector", lambda e: e.tensor_copy(Ssl, pb(6)[hb:hb + 64, 0:64]),
        r=("pb6",), w=(Sres,))
    add("scalar", lambda e: e.activation(
        S[f"o2{p}"][hb:hb + 64, :], pb(bO)[hb:hb + 64, 0:128], AF.Square),
        r=(PBR[bO],), w=(f"o2{p}",))
    bE = 4 + p
    add("tensor", lambda e: e.matmul(
        pb(bE)[0:1, 0:128], S["onesb"][hb:hb + 64, 0:1],
        S[f"o2{p}"][hb:hb + 64, :], start=True, stop=True,
        tile_position=(hb, 0)), r=(f"o2{p}", "onesb"), w=(PBR[bE],))
    add("scalar", lambda e: e.activation(
        S[f"rstf{p}"][:], pb(bE)[0:1, 0:128], AF.Ln, scale=1.0 / 64,
        bias=1e-6), r=(PBR[bE],), w=(f"rstf{p}",))
    add("scalar", lambda e: e.activation(
        S[f"rst{p}"][:], S[f"rstf{p}"][:], AF.Exp, scale=-0.5),
        r=(f"rstf{p}",), w=(f"rst{p}",))
    add("tensor", lambda e: e.matmul(
        pb(bE)[hb:hb + 64, 128:256], o1b(64), S[f"rst{p}"][:],
        start=True, stop=True, tile_position=(0, hb)),
        r=(f"rst{p}", "onesb"), w=(PBR[bE],))
    silzs = S["silz"][hb:hb + 64, TT * (h // 2) + 128 * c4:
                      TT * (h // 2) + 128 * c4 + 128]
    add("vector", lambda e: e.scalar_tensor_tensor(
        S[f"ytmp{p}"][hb:hb + 64, :], pb(bO)[hb:hb + 64, 0:128],
        S["normw"][hb:hb + 64, 0:1], silzs, ALU.mult, ALU.mult),
        r=(PBR[bO], "normw", "silz"), w=(f"ytmp{p}",))
    ysl = S["ytile"][hb:hb + 64, h // 2, 128 * c4:128 * c4 + 128]
    add("vector", lambda e: e.tensor_tensor(
        ysl, S[f"ytmp{p}"][hb:hb + 64, :], pb(bE)[hb:hb + 64, 128:256],
        ALU.mult), r=(f"ytmp{p}", PBR[bE]), w=("ytile",))


def _build_ffn(Hn, d_sr, d_wb, d_out, mode):
    nc, sc, S, dt, pb, pbb, sb, dma, PBR = (
        Hn["nc"], Hn["sc"], Hn["S"], Hn["dt"], Hn["pb"], Hn["pbb"], Hn["sb"],
        Hn["dma"], Hn["PBR"])
    o1b = Hn["o1b"]
    add = sc.add
    sb("xf0", (128, 8, FT))
    sb("xftmp", (128, 8, FT))
    sb("xfbf", (128, 8 * FT), BF16)
    sb("xnf", (128, 8, FT), BF16)
    sb("rowln2", (1, 3 * FT))
    sb("rowln2b", (1, 2 * FT), BF16)
    sb("wr", (128, 8, 7))
    sb("wrb", (128, 8, 7), BF16)
    sb("rb_b", (128, 7))
    sb("lsb", (7, FT))
    sb("ltm", (128, 4, 7))
    sb("rcol", (128, 4))
    sb("rt1", (128, 4, 8)); sb("rt2", (128, 4, 7)); sb("rt3", (128, 4, 7))
    sb("wtm", (128, 4, 7))
    sb("wrow", (1, 7 * FT))
    sb("wrowb", (1, 7 * FT), BF16)
    sb("wbc", (128, 7, FT), BF16)
    sb("ptile", (128, NMC, FT), BF16)
    sb("pt1", (128, FT), BF16)
    sb("acc", (128, 8, FT))
    sb("scnd", (128, 4, FT))
    sb("outb", (128, 2, FT))
    for p in range(2):
        sb(f"wg{p}", (128, 8, 512), BF16)
        sb(f"wu{p}", (128, 8, 512), BF16)
        sb(f"wd{p}", (128, NMC, 256), BF16)
    dma("sync", S["wr"][:], dt["wr"].ap(), w=("wr",))
    add("vector", lambda e: e.tensor_copy(S["wrb"][:], S["wr"][:]),
        r=("wr",), w=("wrb",))
    dma("sync", S["rb_b"][:], dt["rb_b"].ap(), w=("rb_b",))
    sb("cmb", (128, 4))
    dma("sync", S["cmb"][:], dt["cmb"].ap(), w=("cmb",))

    x_fv = dt["x_ffn"].ap().rearrange("(k p) t -> p k t", p=128)
    out_v = d_out.ap().rearrange("(k p) t -> p k t", p=128)
    gu_v = dt["ffn_gu"].ap()
    dd_v = dt["ffn_d"].ap()

    for u in range(NFT):
        xf = S["xf0"]
        xres = "xf0"
        dma("sync", xf[:], x_fv[:, :, FT * u:FT * (u + 1)], w=(xres,))
        _ln_ops(Hn, xf, xres, S["xnf"], "xnf", "xfbf", "rowln2", "rowln2b",
                8, FT, 0, 1, 2, 3, xtmp="xftmp", xtmpres="xftmp")

        for kk in range(8):
            add("tensor", lambda e, kk=kk: e.matmul(
                pb(4)[0:7, 0:FT], S["wr"][:, kk, :], S["xftmp"][:, kk, :],
                start=(kk == 0), stop=(kk == 7)),
                r=("wr", "xftmp"), w=("pb4",))
        add("vector", lambda e: e.tensor_copy(S["lsb"][:], pb(4)[0:7, 0:FT]),
            r=("pb4",), w=("lsb",))
        for j in range(4):
            add("tensor", lambda e, j=j: e.transpose(
                pb(5)[0:128, 0:7], S["lsb"][0:7, 128 * j:128 * j + 128],
                S["identf"][0:7, 0:7]), r=("lsb", "identf"), w=("pb5",))
            add("vector", lambda e, j=j: e.tensor_copy(
                S["ltm"][:, j, :], pb(5)[0:128, 0:7]),
                r=("pb5",), w=("ltm",))
            add("tensor", lambda e, j=j: e.transpose(
                pb(5)[0:128, 8:9],
                S["rowln2"][0:1, FT + 128 * j:FT + 128 * j + 128],
                S["identf"][0:1, 0:1]), r=("rowln2", "identf"), w=("pb5",))
            add("vector", lambda e, j=j: e.tensor_copy(
                S["rcol"][:, j:j + 1], pb(5)[0:128, 8:9]),
                r=("pb5",), w=("rcol",))
        rc_b = bass.AP(S["rcol"], 0, [[4, 128], [1, 4], [0, 7]])
        rb_bb = _bcast_mid(S["rb_b"][:], 4)
        add("vector", lambda e: e.tensor_tensor(
            S["ltm"][:], S["ltm"][:], rc_b, ALU.mult),
            r=("ltm", "rcol"), w=("ltm",))
        add("vector", lambda e: e.tensor_tensor(
            S["ltm"][:], S["ltm"][:], rb_bb, ALU.add),
            r=("ltm", "rb_b"), w=("ltm",))
        lt = S["ltm"][:]
        r1 = S["rt1"][:]
        m1 = r1[:, :, 0:1]; m2 = r1[:, :, 1:2]; cnt = r1[:, :, 2:3]
        mm2 = r1[:, :, 3:4]; den = r1[:, :, 4:5]
        t5 = r1[:, :, 5:6]; t6 = r1[:, :, 6:7]; t7 = r1[:, :, 7:8]
        r2 = S["rt2"][:]
        r3 = S["rt3"][:]

        def bc(col):
            a = col.ap
            return bass.AP(col.tensor, col.offset,
                           [list(a[0]), list(a[1]), [0, 7]])
        add("vector", lambda e: e.reduce_max(m1, lt, axis=AX.X),
            r=("ltm",), w=("rt1",))
        add("vector", lambda e: e.tensor_tensor(
            r2, lt, bc(m1), ALU.is_ge), r=("ltm", "rt1"), w=("rt2",))
        add("vector", lambda e: e.reduce_sum(cnt, r2, axis=AX.X),
            r=("rt2",), w=("rt1",))
        add("vector", lambda e: e.scalar_tensor_tensor(
            r3, r2, -1e30, lt, ALU.mult, ALU.add), r=("rt2", "ltm"),
            w=("rt3",))
        add("vector", lambda e: e.reduce_max(m2, r3, axis=AX.X),
            r=("rt3",), w=("rt1",))
        add("vector", lambda e: e.tensor_scalar(
            t5, cnt, 2.0, None, ALU.is_ge), r=("rt1",), w=("rt1",))
        add("vector", lambda e: e.tensor_tensor(
            t6, m1, m2, ALU.subtract), r=("rt1",), w=("rt1",))
        add("vector", lambda e: e.tensor_tensor(
            t7, t5, t6, ALU.mult), r=("rt1",), w=("rt1",))
        add("vector", lambda e: e.tensor_tensor(
            mm2, m2, t7, ALU.add), r=("rt1",), w=("rt1",))
        add("vector", lambda e: e.tensor_tensor(
            t5, mm2, m1, ALU.subtract), r=("rt1",), w=("rt1",))
        add("scalar", lambda e: e.activation(t5, t5, AF.Exp),
            r=("rt1",), w=("rt1",))
        add("vector", lambda e: e.tensor_scalar(
            den, t5, 1.0, None, ALU.add), r=("rt1",), w=("rt1",))
        add("vector", lambda e: e.reciprocal(den, den), r=("rt1",),
            w=("rt1",))
        add("vector", lambda e: e.tensor_tensor(
            r2, lt, bc(mm2), ALU.is_ge), r=("ltm", "rt1"), w=("rt2",))
        add("vector", lambda e: e.tensor_tensor(
            r3, lt, bc(m1), ALU.subtract), r=("ltm", "rt1"), w=("rt3",))
        add("scalar", lambda e: e.activation(r3, r3, AF.Exp),
            r=("rt3",), w=("rt3",))
        add("vector", lambda e: e.tensor_tensor(r3, r3, r2, ALU.mult),
            r=("rt3", "rt2"), w=("rt3",))
        add("vector", lambda e: e.tensor_tensor(
            S["wtm"][:], r3, bc(den), ALU.mult), r=("rt3", "rt1"),
            w=("wtm",))
        for j in range(4):
            add("tensor", lambda e, j=j: e.transpose(
                pb(5)[0:7, 0:128], S["wtm"][:, j, :], S["identf"][:]),
                r=("wtm", "identf"), w=("pb5",))
            add("vector", lambda e, j=j: e.tensor_copy(
                S["lsb"][0:7, 128 * j:128 * j + 128], pb(5)[0:7, 0:128]),
                r=("pb5",), w=("lsb",))
        dma("sync", d_wb.ap(), S["lsb"][:], r=("lsb",), w=("d_wb",))
        dma("sync", S["wrow"][:],
            d_wb.ap().rearrange("e t -> (e t)").rearrange("(o n) -> o n",
                                                          o=1),
            r=("d_wb",), w=("wrow",))
        add("vector", lambda e: e.tensor_copy(S["wrowb"][:], S["wrow"][:]),
            r=("wrow",), w=("wrowb",))
        for ee in range(7):
            add("tensor", lambda e, ee=ee: e.matmul(
                pb(5)[0:128, 0:FT], o1b(128),
                S["wrowb"][0:1, FT * ee:FT * (ee + 1)],
                start=True, stop=True), r=("wrowb", "onesb"), w=("pb5",))
            add("vector", lambda e, ee=ee: e.tensor_copy(
                S["wbc"][:, ee, :], pb(5)[0:128, 0:FT]),
                r=("pb5",), w=("wbc",))

        gu_blocks = [(b0, min(b0 + 4, NMC)) for b0 in range(0, NMC, 4)]
        mh_blocks = [(b0, b0 + 2) for b0 in range(0, 8, 2)]

        def em_gu_dma(e_i, bi):
            b0, b1 = gu_blocks[bi]
            pwm = bi % 2
            dma("sync", S[f"wg{pwm}"][:, :, 0:128 * (b1 - b0)],
                gu_v[e_i, 0, :, :, 128 * b0:128 * b1],
                w=(f"wg{pwm}",))
            dma("sync", S[f"wu{pwm}"][:, :, 0:128 * (b1 - b0)],
                gu_v[e_i, 1, :, :, 128 * b0:128 * b1],
                w=(f"wu{pwm}",))

        def em_wd_dma(e_i, bi):
            b0, b1 = mh_blocks[bi]
            pw = bi % 2
            dma("sync", S[f"wd{pw}"][:],
                dd_v[e_i][:, :, 128 * b0:128 * b1], w=(f"wd{pw}",))

        for e_i in range(NE):
            if e_i == 0:
                em_gu_dma(0, 0)
            for bi, (b0, b1) in enumerate(gu_blocks):
                pwm = bi % 2
                if bi + 1 < len(gu_blocks):
                    em_gu_dma(e_i, bi + 1)
                else:
                    em_wd_dma(e_i, 0)
                for mc in range(b0, b1):
                    off = 128 * (mc - b0)
                    for kk in range(8):
                        add("tensor", lambda e, kk=kk, pwm=pwm, off=off:
                            e.matmul(
                                pb(0)[0:128, 0:FT],
                                S[f"wg{pwm}"][:, kk, off:off + 128],
                                S["xnf"][:, kk, :],
                                start=(kk == 0), stop=(kk == 7)),
                            r=(f"wg{pwm}", "xnf"), w=("pb0",))
                    for kk in range(8):
                        add("tensor", lambda e, kk=kk, pwm=pwm, off=off:
                            e.matmul(
                                pb(1)[0:128, 0:FT],
                                S[f"wu{pwm}"][:, kk, off:off + 128],
                                S["xnf"][:, kk, :],
                                start=(kk == 0), stop=(kk == 7)),
                            r=(f"wu{pwm}", "xnf"), w=("pb1",))
                    add("scalar", lambda e: e.activation(
                        S["pt1"][:], pb(0)[0:128, 0:FT], AF.Silu),
                        r=("pb0",), w=("pt1",))
                    if e_i == 0:
                        add("vector", lambda e, mc=mc: e.tensor_tensor(
                            S["ptile"][:, mc, :], S["pt1"][:],
                            pb(1)[0:128, 0:FT], ALU.mult),
                            r=("pt1", "pb1"), w=("ptile",))
                    else:
                        add("vector", lambda e, e_i=e_i: e.tensor_tensor(
                            S["pt1"][:], S["pt1"][:],
                            S["wbc"][:, e_i - 1, :], ALU.mult),
                            r=("pt1", "wbc"), w=("pt1",))
                        add("vector", lambda e, mc=mc: e.tensor_tensor(
                            S["ptile"][:, mc, :], S["pt1"][:],
                            pb(1)[0:128, 0:FT], ALU.mult),
                            r=("pt1", "pb1"), w=("ptile",))
            for bi, (b0, b1) in enumerate(mh_blocks):
                pw = bi % 2
                if bi + 1 < len(mh_blocks):
                    em_wd_dma(e_i, bi + 1)
                elif e_i + 1 < NE:
                    em_gu_dma(e_i + 1, 0)
                for mh in range(b0, b1):
                    bank = 2 + mh % 2
                    off = 128 * (mh - b0)
                    for kk in range(NMC):
                        add("tensor", lambda e, kk=kk, pw=pw, off=off,
                            bank=bank: e.matmul(
                                pb(bank)[0:128, 0:FT],
                                S[f"wd{pw}"][:, kk, off:off + 128],
                                S["ptile"][:, kk, :],
                                start=(kk == 0), stop=(kk == NMC - 1)),
                            r=(f"wd{pw}", "ptile"), w=(PBR[bank],))
                    if e_i == 0:
                        add("scalar", lambda e, mh=mh, bank=bank: e.copy(
                            S["acc"][:, mh, :], pb(bank)[0:128, 0:FT]),
                            r=(PBR[bank],), w=("acc",))
                    else:
                        add("vector", lambda e, mh=mh, bank=bank:
                            e.tensor_tensor(
                                S["acc"][:, mh, :], S["acc"][:, mh, :],
                                pb(bank)[0:128, 0:FT], ALU.add),
                            r=(PBR[bank],), w=("acc",))

        if u == 0 and Hn.get("dbg"):
            dbg = Hn["dbg"]
            dma("gpsimd", dbg["xn"].ap(), S["xnf"][:], r=("xnf",),
                w=("OUTdxn",))
            dma("gpsimd", dbg["ltm"].ap(), S["ltm"][:], r=("ltm",),
                w=("OUTdltm",))
            dma("gpsimd", dbg["lsb"].ap(), S["lsb"][:], r=("lsb",),
                w=("OUTdlsb",))
            dma("gpsimd", dbg["rcol"].ap(), S["rcol"][:], r=("rcol",),
                w=("OUTdrcol",))
            dma("gpsimd", dbg["xtmp"].ap(), S["xftmp"][:], r=("xftmp",),
                w=("OUTdxtmp",))
            dma("gpsimd", dbg["rt1"].ap(), S["rt1"][:], r=("rt1",),
                w=("OUTdrt1",))
            dma("gpsimd", dbg["rt2"].ap(), S["rt2"][:], r=("rt2",),
                w=("OUTdrt2",))
            dma("gpsimd", dbg["wtm"].ap(), S["wtm"][:], r=("wtm",),
                w=("OUTdwtm",))
            dma("gpsimd", dbg["acc"].ap(), S["acc"][:], r=("acc",),
                w=("OUTdacc",))
            dma("gpsimd", dbg["pt"].ap(), S["ptile"][:], r=("ptile",),
                w=("OUTdpt",))
        for mh in range(8):
            if mode == "full":
                src = d_sr.ap()[128 * mh:128 * mh + 128, :] \
                    .rearrange("p (r q) -> p r q", r=4)[:, :,
                                                        FT * u:FT * (u + 1)]
                dma("gpsimd", S["scnd"][:], src, r=("d_sr",), w=("scnd",))
                add("vector", lambda e: e.tensor_scalar(
                    S["outb"][:, 0, :], S["scnd"][:, 0, :],
                    S["cmb"][:, 0:1], None, ALU.mult),
                    r=("scnd", "cmb"), w=("outb",))
                for rr in range(1, 4):
                    add("vector", lambda e, rr=rr: e.scalar_tensor_tensor(
                        S["outb"][:, 0, :], S["scnd"][:, rr, :],
                        S["cmb"][:, rr:rr + 1], S["outb"][:, 0, :],
                        ALU.mult, ALU.add), r=("scnd", "cmb", "outb"),
                        w=("outb",))
            add("scalar", lambda e, mh=mh: e.activation(
                S["outb"][:, 1, :], S["acc"][:, mh, :], AF.Silu),
                r=("acc",), w=("outb",))
            if mode == "full":
                add("vector", lambda e: e.tensor_tensor(
                    S["outb"][:, 1, :], S["outb"][:, 1, :],
                    S["outb"][:, 0, :], ALU.mult), r=("outb",), w=("outb",))
                add("vector", lambda e, mh=mh, xf=xf: e.tensor_tensor(
                    S["outb"][:, 1, :], S["outb"][:, 1, :], xf[:, mh, :],
                    ALU.add), r=("outb", xres), w=("outb",))
            dma("gpsimd", out_v[:, mh, FT * u:FT * (u + 1)],
                S["outb"][:, 1, :], r=("outb",), w=(f"OUTo{u}_{mh}",))


def kernel(**inputs):
    sh = _prep_shared(inputs)
    in_maps = [_prep_core(c, inputs, sh) for c in range(8)]
    nc = build_program(_BUILD_MODE)
    res = run_bass_kernel_spmd(nc, in_maps, core_ids=list(range(8)))
    out = np.empty((B, T, D), np.float32)
    for c in range(8):
        b, r = c // 4, c % 4
        out[b, 4096 * r:4096 * (r + 1), :] = res.results[c]["out_fm"].T
    return out


# version 08
# speedup vs baseline: 1.2551x; 1.2551x over previous
import sys
sys.path.insert(0, '/opt/trn_rl_repo')
import math
import numpy as np
import ml_dtypes
from contextlib import ExitStack
import concourse.bass as bass
import concourse.mybir as mybir
from concourse.bass_utils import run_bass_kernel_spmd

F32 = mybir.dt.float32
BF16 = mybir.dt.bfloat16
AF = mybir.ActivationFunctionType
ALU = mybir.AluOpType
AX = mybir.AxisListType
BF = ml_dtypes.bfloat16

B, T, D = 2, 16384, 1024
H, HD = 16, 64
KV = H * HD
NH = 4
TT = 512
NTILE = T // TT
C = 128
TF = 4096
FT = 512
NFT = TF // FT
I_FFN = 2816
NMC = 22
NMH = 8
NE = 8
LN8 = math.log(0.125)

_BUILD_MODE = "full"


class Sched:
    ENGS = ("tensor", "vector", "scalar", "gpsimd", "sync")

    SEM_ROLL = 30000

    def __init__(self, nc):
        self.nc = nc
        self.ops = []
        self._semidx = 0
        self.sem_c = {e: self._new_sem() for e in self.ENGS}
        self.sem_d = {e: self._new_sem() for e in ("sync", "gpsimd",
                                                   "scalar")}

    def _new_sem(self):
        self._semidx += 1
        return self.nc.alloc_semaphore(f"sem{self._semidx}")

    def add(self, eng, fn, r=(), w=(), dma=False, force_inc=False):
        op = {"eng": eng, "fn": fn, "r": tuple(r), "w": tuple(w), "dma": dma,
              "need": bool(dma or force_inc), "deps": None, "tok": None}
        self.ops.append(op)
        return op

    def resolve(self):
        lastw = {}
        readers = {}
        for op in self.ops:
            deps = []
            seen = set()

            def dep(p):
                if p is None or p is op or id(p) in seen:
                    return
                if (p["eng"] == op["eng"]) and not p["dma"] \
                        and p["eng"] == "tensor":
                    return
                p["need"] = True
                seen.add(id(p))
                deps.append(p)

            for r in op["r"]:
                dep(lastw.get(r))
            for r in op["w"]:
                dep(lastw.get(r))
                for rd in readers.get(r, ()):
                    dep(rd)
            op["deps"] = deps
            for r in op["r"]:
                readers.setdefault(r, []).append(op)
            for r in op["w"]:
                lastw[r] = op
                readers[r] = []
        cnt = {}
        for op in self.ops:
            if op["need"]:
                pool = self.sem_d if op["dma"] else self.sem_c
                sem = pool[op["eng"]]
                amt = 16 if op["dma"] else 1
                if cnt.get(id(sem), 0) + amt > self.SEM_ROLL:
                    sem = self._new_sem()
                    pool[op["eng"]] = sem
                k = id(sem)
                cnt[k] = cnt.get(k, 0) + amt
                op["tok"] = (sem, cnt[k], amt)

    def emit(self, engname, engobj):
        hw = {}
        for op in self.ops:
            if op["eng"] != engname:
                continue
            best = {}
            for p in op["deps"]:
                sem, val, _ = p["tok"]
                cur = best.get(id(sem))
                if cur is None or cur[1] < val:
                    best[id(sem)] = (sem, val)
            for k, (sem, val) in best.items():
                if hw.get(k, -1) < val:
                    engobj.wait_ge(sem, val)
                    hw[k] = val
            inst = op["fn"](engobj)
            if op["tok"] is not None:
                sem, val, amt = op["tok"]
                inst.then_inc(sem, amt)


def _lhsT_blocks(wT, kdim, mdim):
    assert wT.shape == (kdim, mdim)
    return np.ascontiguousarray(
        wT.reshape(kdim // 128, 128, mdim).transpose(1, 0, 2))


def _prep_shared(inp):
    sh = {}
    sh["identbf"] = np.eye(128, dtype=BF)
    sh["identf"] = np.eye(128, dtype=np.float32)
    jj, ii = np.mgrid[0:128, 0:128]
    sh["triu"] = (jj <= ii).astype(np.float32)
    mb = np.empty((128, 256), np.float32)
    mb[:, 0:128] = np.where(jj <= ii, 0.0, -1e30)
    mb[:, 128:256] = np.where(jj < ii, 0.0, -1e30)
    sh["maskb"] = mb
    sh["m8"] = (-((jj // 8 == ii // 8) & (jj < ii)).astype(np.float32)).astype(BF)
    sh["l0"] = (-((jj // 32 == ii // 32) & (jj // 8 < ii // 8))
                .astype(np.float32)).astype(BF)
    sh["l1"] = (-(jj // 32 < ii // 32).astype(np.float32)).astype(BF)
    sh["ibf"] = np.eye(128, dtype=BF)

    gs = [np.asarray(inp["L0_ffn_sg"], np.float32)] + \
         [np.asarray(inp["L0_ffn_rg"][e], np.float32) for e in range(7)]
    us = [np.asarray(inp["L0_ffn_su"], np.float32)] + \
         [np.asarray(inp["L0_ffn_ru"][e], np.float32) for e in range(7)]
    ds = [np.asarray(inp["L0_ffn_sd"], np.float32)] + \
         [np.asarray(inp["L0_ffn_rd"][e], np.float32) for e in range(7)]
    gu = np.empty((NE, 2, 128, 8, I_FFN), BF)
    dd = np.empty((NE, 128, NMC, D), BF)
    for e in range(NE):
        gu[e, 0] = _lhsT_blocks(np.ascontiguousarray(gs[e].T).astype(BF),
                                D, I_FFN)
        gu[e, 1] = _lhsT_blocks(np.ascontiguousarray(us[e].T).astype(BF),
                                D, I_FFN)
        dd[e] = _lhsT_blocks(np.ascontiguousarray(ds[e].T).astype(BF),
                             I_FFN, D)
    sh["ffn_gu"] = gu
    sh["ffn_d"] = dd
    sh["wr"] = _lhsT_blocks(
        np.ascontiguousarray(np.asarray(inp["L0_ffn_wr"], np.float32).T),
        D, 7)
    sh["rb_col"] = np.asarray(inp["L0_ffn_rb"], np.float32).reshape(7, 1)
    sh["rb_b"] = np.tile(np.asarray(inp["L0_ffn_rb"], np.float32), (128, 1))
    nw = np.asarray(inp["L0_attn_norm_w"], np.float32)
    sh["norm_w"] = np.tile(nw, 2).reshape(128, 1)
    return sh


def _prep_core(c, inp, sh):
    b, r = c // 4, c % 4
    hs = [4 * r + j for j in range(NH)]
    W_in = np.asarray(inp["L0_attn_W_in"], np.float32)
    W_ba = np.asarray(inp["L0_attn_W_ba"], np.float32)
    conv_w = np.asarray(inp["L0_attn_conv_w"], np.float32)[:, 0, :]
    x = np.asarray(inp["L0_attn_x"], np.float32)

    cols = []
    for blk in range(3):
        for h in hs:
            cols.append(W_in[:, blk * KV + 64 * h: blk * KV + 64 * h + 64])
    for h in hs:
        cols.append(W_in[:, 3 * KV + 64 * h: 3 * KV + 64 * h + 64])
    bacols = [W_ba[:, [h]] for h in hs] + [W_ba[:, [H + h]] for h in hs]
    wcat = np.concatenate(cols + bacols, axis=1)
    d = {}
    d["win"] = _lhsT_blocks(wcat.astype(BF), D, 1032)

    rows = []
    for blk in range(3):
        for h in hs:
            rows.append(conv_w[blk * KV + 64 * h: blk * KV + 64 * h + 64, :])
    cwg = np.concatenate(rows, axis=0)
    cd = np.zeros((128, 6, 4, 128), np.float32)
    for cc in range(6):
        for tap in range(4):
            np.fill_diagonal(cd[:, cc, tap, :],
                             cwg[128 * cc:128 * cc + 128, tap])
    d["convd"] = cd.astype(BF)

    W_o = np.asarray(inp["L0_attn_W_o"], np.float32)
    worows = np.concatenate([W_o[64 * h:64 * h + 64, :] for h in hs], axis=0)
    d["wo"] = _lhsT_blocks(worows.astype(BF), 256, D)

    gp = np.zeros((1, 8), np.float32)
    gp[0, 0:4] = np.asarray(inp["L0_attn_dt_bias"], np.float32)[hs]
    gp[0, 4:8] = np.asarray(inp["L0_attn_A_log"], np.float32)[hs]
    d["gatep"] = gp
    d["x_fm"] = np.ascontiguousarray(x[b].T)
    d["x_ffn"] = np.ascontiguousarray(x[b, 4096 * r:4096 * (r + 1)].T)
    cm = np.zeros((128, 4), np.float32)
    cm[:, r] = 1.0
    d["cmb"] = cm
    d.update(sh)
    return d


def build_program(mode="full"):
    nc = bass.Bass("TRN2", target_bir_lowering=False, debug=False,
                   disable_frame_to_traceback=True)
    mixer_on = mode in ("full", "mixer")
    ffn_on = mode in ("full", "ffn")

    dt = {}

    def din(name, shape, ty=F32):
        dt[name] = nc.dram_tensor(name, list(shape), ty, kind="ExternalInput")

    din("identbf", (128, 128), BF16)
    din("identf", (128, 128))
    din("ibf", (128, 128), BF16)
    if mixer_on:
        din("x_fm", (D, T))
        din("win", (128, 8, 1032), BF16)
        din("convd", (128, 6, 4, 128), BF16)
        din("wo", (128, 2, D), BF16)
        din("gatep", (1, 8))
        din("norm_w", (128, 1))
        din("triu", (128, 128))
        din("maskb", (128, 256))
        din("m8", (128, 128), BF16)
        din("l0", (128, 128), BF16)
        din("l1", (128, 128), BF16)
    if ffn_on:
        din("x_ffn", (D, TF))
        din("ffn_gu", (NE, 2, 128, 8, I_FFN), BF16)
        din("ffn_d", (NE, 128, NMC, D), BF16)
        din("wr", (128, 8, 7))
        din("rb_b", (128, 7))
        din("cmb", (128, 4))

    d_sp = nc.dram_tensor("d_sp", [D, T], F32, kind="Internal")
    d_sr = nc.dram_tensor("d_sr", [D, T], F32, kind="Internal")
    d_wb = nc.dram_tensor("d_wb", [7, FT], F32, kind="Internal")
    if mode == "mixer":
        d_out = nc.dram_tensor("out_s", [D, T], F32, kind="ExternalOutput")
    else:
        d_out = nc.dram_tensor("out_fm", [D, TF], F32, kind="ExternalOutput")
    dbg = {}
    if mode == "mixer":
        for nm_, shp, ty in (("mxn", (128, 8, TT), BF16),
                             ("mqkv", (128, 6, TT), BF16),
                             ("msilz", (128, 2 * TT), BF16),
                             ("mgat", (128, 64), F32),
                             ("myt", (128, 2, TT), BF16),
                             ("mamt", (128, 256), BF16),
                             ("mwsol", (128, 128), BF16),
                             ("mktm", (128, 64), BF16),
                             ("mrkv", (128, 128), BF16),
                             ("mqt", (128, 128), BF16)):
            dbg[nm_] = nc.dram_tensor("dbg_" + nm_, list(shp), ty,
                                      kind="ExternalOutput")
    if mode == "ffn":
        dbg["xn"] = nc.dram_tensor("dbg_xn", [128, 8, FT], BF16,
                                   kind="ExternalOutput")
        dbg["ltm"] = nc.dram_tensor("dbg_ltm", [128, 4, 7], F32,
                                    kind="ExternalOutput")
        dbg["wtm"] = nc.dram_tensor("dbg_wtm", [128, 4, 7], F32,
                                    kind="ExternalOutput")
        dbg["acc"] = nc.dram_tensor("dbg_acc", [128, 8, FT], F32,
                                    kind="ExternalOutput")
        dbg["pt"] = nc.dram_tensor("dbg_pt", [128, NMC, FT], BF16,
                                   kind="ExternalOutput")
        dbg["lsb"] = nc.dram_tensor("dbg_lsb", [7, FT], F32,
                                    kind="ExternalOutput")
        dbg["rcol"] = nc.dram_tensor("dbg_rcol", [128, 4], F32,
                                     kind="ExternalOutput")
        dbg["xtmp"] = nc.dram_tensor("dbg_xtmp", [128, 8, FT], F32,
                                     kind="ExternalOutput")
        dbg["rt1"] = nc.dram_tensor("dbg_rt1", [128, 4, 8], F32,
                                    kind="ExternalOutput")
        dbg["rt2"] = nc.dram_tensor("dbg_rt2", [128, 4, 7], F32,
                                    kind="ExternalOutput")

    sc = Sched(nc)
    S = {}
    scope = {"stack": None}

    def sb(name, shape, ty=F32):
        if scope["stack"] is not None:
            S[name] = scope["stack"].enter_context(
                nc.sbuf_tensor("s_" + name, list(shape), ty))
        else:
            S[name] = nc.alloc_sbuf_tensor("s_" + name, list(shape), ty)
        return S[name]

    PQ = [nc.alloc_psum_tensor(f"pq{i}", [128, 1024], F32) for i in range(4)]

    def pb(i):
        return PQ[i // 2][:, 512 * (i % 2):512 * (i % 2) + 512]

    def pbb(i):
        full = PQ[i // 2][:].bitcast(BF16)
        return full[:, 1024 * (i % 2):1024 * (i % 2) + 1024]

    PBR = [f"pb{i}" for i in range(8)]

    sb("identbf", (128, 128), BF16)
    sb("identf", (128, 128))
    sb("ibf", (128, 128), BF16)
    sb("onesf", (128, 1))
    sb("onesb", (128, 1), BF16)
    sb("cbias", (128, 4))

    def dma(eng, dst, src, r=(), w=()):
        return sc.add(eng, lambda e, a=dst, b_=src: e.dma_start(a, b_),
                      r=r, w=w, dma=True)

    for nm in ("identbf", "identf", "ibf"):
        dma("sync", S[nm][:], dt[nm].ap(), w=(nm,))
    sc.add("gpsimd", lambda e: e.memset(S["onesf"][:], 1.0), w=("onesf",))
    sc.add("gpsimd", lambda e: e.memset(S["onesb"][:], 1.0), w=("onesb",))
    sc.add("gpsimd", lambda e: e.memset(S["cbias"][:, 0:1], 1e-6),
           w=("cbias",))
    sc.add("gpsimd", lambda e: e.memset(S["cbias"][:, 1:2], 1e-5),
           w=("cbias",))
    sc.add("gpsimd", lambda e: e.memset(S["cbias"][:, 2:3], 1.0),
           w=("cbias",))
    nc.const_aps.register(F32, 1e-6, S["cbias"][:, 0:1])
    nc.const_aps.register(F32, 1e-5, S["cbias"][:, 1:2])
    nc.const_aps.register(F32, 1.0, S["cbias"][:, 2:3])

    def ones_f32_row(n):
        return bass.AP(S["onesf"], 0, [[1, 1], [0, n]])

    def ones_bf_row(n):
        return bass.AP(S["onesb"], 0, [[1, 1], [0, n]])

    def allones_f32(n):
        return bass.AP(S["onesf"], 0, [[1, 128], [0, n]])

    H_ = dict(nc=nc, sc=sc, S=S, dt=dt, pb=pb, pbb=pbb, sb=sb, dma=dma,
              PBR=PBR, o1r=ones_f32_row, o1b=ones_bf_row, oall=allones_f32,
              dbg=dbg)

    if mixer_on:
        if mode == "full":
            scope["stack"] = ExitStack()
        _build_mixer(H_, d_sp, d_out, mode)
    if mode == "full":
        def cc(e):
            return e.collective_compute(
                "AllReduce", ALU.add,
                replica_groups=[[0, 1, 2, 3], [4, 5, 6, 7]],
                ins=[d_sp.ap().opt()], outs=[d_sr.ap().opt()])
        sc.add("gpsimd", cc, r=tuple(f"d_sp{t}" for t in range(NTILE)),
               w=("d_sr",), force_inc=True)
        _phase_barrier(sc)
        scope["stack"].close()
        scope["stack"] = None
    if ffn_on:
        _build_ffn(H_, d_sr, d_wb, d_out, mode)

    outres = tuple(k for k in _all_written(sc) if k.startswith("OUT"))
    sc.add("sync", lambda e: e.nop(), r=outres)

    sc.resolve()
    with nc.Block() as blk:
        @blk.tensor
        def _(t):
            sc.emit("tensor", t)

        @blk.vector
        def _(v):
            sc.emit("vector", v)

        @blk.scalar
        def _(s):
            sc.emit("scalar", s)

        @blk.gpsimd
        def _(g):
            sc.emit("gpsimd", g)

        @blk.sync
        def _(s):
            sc.emit("sync", s)
    return nc


def _all_written(sc):
    out = set()
    for op in sc.ops:
        out.update(op["w"])
    return out


def _phase_barrier(sc, extra_reads=()):
    engs = Sched.ENGS
    for e in engs:
        sc.add(e, lambda x: x.nop(), r=tuple(extra_reads), w=("bar_" + e,))
    for e in engs:
        sc.add(e, lambda x: x.nop(), r=tuple("bar_" + f for f in engs))


def _bcast_mid(ap2d, n):
    a = ap2d.ap
    return bass.AP(ap2d.tensor, ap2d.offset, [list(a[0]), [0, n], list(a[1])])


def _ln_ops(Hn, xt, xres, xout, xoutres, xbf_name, row_name, rowb_name,
            nseg, width, bmean, bsq, bbm, bbr, xtmp=None, xtmpres=None):
    sc, S, pb, o1b = Hn["sc"], Hn["S"], Hn["pb"], Hn["o1b"]
    add = sc.add
    PBR = Hn["PBR"]
    W = nseg * width
    add("vector", lambda e: e.tensor_copy(
        S[xbf_name][:], xt[:].rearrange("p k t -> p (k t)")),
        r=(xres,), w=(xbf_name,))
    xb3 = S[xbf_name][:].rearrange("p (k t) -> p k t", k=nseg)
    for kk in range(nseg):
        add("tensor", lambda e, kk=kk: e.matmul(
            pb(bmean)[0:1, 0:width], S["onesb"][:, 0:1], xb3[:, kk, :],
            start=(kk == 0), stop=(kk == nseg - 1)),
            r=(xbf_name, "onesb"), w=(PBR[bmean],))
    add("scalar", lambda e: e.activation(
        S[xbf_name][:], S[xbf_name][:], AF.Square),
        r=(xbf_name,), w=(xbf_name,))
    for kk in range(nseg):
        add("tensor", lambda e, kk=kk: e.matmul(
            pb(bsq)[0:1, 0:width], S["onesb"][:, 0:1], xb3[:, kk, :],
            start=(kk == 0), stop=(kk == nseg - 1)),
            r=(xbf_name, "onesb"), w=(PBR[bsq],))
    row = S[row_name]
    add("vector", lambda e: e.tensor_scalar(
        row[0:1, 0:width], pb(bmean)[0:1, 0:width], 1.0 / D, None, ALU.mult),
        r=(PBR[bmean],), w=(row_name,))
    add("vector", lambda e: e.tensor_scalar(
        row[0:1, width:2 * width], pb(bsq)[0:1, 0:width], 1.0 / D, None,
        ALU.mult), r=(PBR[bsq],), w=(row_name,))
    add("scalar", lambda e: e.activation(
        row[0:1, 2 * width:3 * width], row[0:1, 0:width], AF.Square),
        r=(row_name,), w=(row_name,))
    add("vector", lambda e: e.tensor_tensor(
        row[0:1, width:2 * width], row[0:1, width:2 * width],
        row[0:1, 2 * width:3 * width], ALU.subtract),
        r=(row_name,), w=(row_name,))
    add("scalar", lambda e: e.activation(
        row[0:1, width:2 * width], row[0:1, width:2 * width], AF.Ln,
        bias=1e-5), r=(row_name,), w=(row_name,))
    add("scalar", lambda e: e.activation(
        row[0:1, width:2 * width], row[0:1, width:2 * width], AF.Exp,
        scale=-0.5), r=(row_name,), w=(row_name,))
    add("vector", lambda e: e.tensor_copy(
        S[rowb_name][:], row[0:1, 0:2 * width]),
        r=(row_name,), w=(rowb_name,))
    add("tensor", lambda e: e.matmul(
        pb(bbm)[0:128, 0:width], o1b(128), S[rowb_name][0:1, 0:width],
        start=True, stop=True), r=(rowb_name, "onesb"), w=(PBR[bbm],))
    add("tensor", lambda e: e.matmul(
        pb(bbr)[0:128, 0:width], o1b(128),
        S[rowb_name][0:1, width:2 * width],
        start=True, stop=True), r=(rowb_name, "onesb"), w=(PBR[bbr],))
    m_b = _bcast_mid(pb(bbm)[0:128, 0:width], nseg)
    r_b = _bcast_mid(pb(bbr)[0:128, 0:width], nseg)
    tgt = S[xtmp] if xtmp else xt
    tres = xtmpres if xtmp else xres
    add("vector", lambda e, tgt=tgt: e.tensor_tensor(
        tgt[:], xt[:], m_b, ALU.subtract), r=(xres, PBR[bbm]), w=(tres,))
    add("vector", lambda e, tgt=tgt: e.tensor_tensor(
        xout[:], tgt[:], r_b, ALU.mult), r=(tres, PBR[bbr]), w=(xoutres,))


def _build_mixer(Hn, d_sp, d_out, mode):
    nc, sc, S, dt, pb, pbb, sb, dma, PBR = (
        Hn["nc"], Hn["sc"], Hn["S"], Hn["dt"], Hn["pb"], Hn["pbb"], Hn["sb"],
        Hn["dma"], Hn["PBR"])
    o1b, oall = Hn["o1b"], Hn["oall"]
    add = sc.add

    for nm_, shp, ty in (("win", (128, 8, 1032), BF16),
                         ("convd", (128, 6, 4, 128), BF16),
                         ("wo", (128, 2, D), BF16),
                         ("triu", (128, 128), F32),
                         ("maskb", (128, 256), F32),
                         ("m8", (128, 128), BF16),
                         ("l0", (128, 128), BF16),
                         ("l1", (128, 128), BF16),
                         ("gatep", (1, 8), F32)):
        sb(nm_, shp, ty)
        dma("sync", S[nm_][:], dt[nm_].ap(), w=(nm_,))
    sb("normw", (128, 1))
    dma("sync", S["normw"][:], dt["norm_w"].ap(), w=("normw",))
    sb("dtb", (128, 4)); sb("negA", (128, 4))

    sb("xt0", (128, 8, TT)); sb("xt1", (128, 8, TT))
    sb("xbf", (128, 8 * TT), BF16)
    sb("xnbf", (128, 8, TT), BF16)
    sb("rowln", (1, 3 * TT))
    sb("rowlnb", (1, 2 * TT), BF16)
    sb("preconv", (128, 6, TT + 3), BF16)
    sb("qkvt", (128, 6, TT), BF16)
    sb("zt", (128, 2 * TT), BF16)
    sb("silz", (128, 2 * TT), BF16)
    sb("barow", (8, TT))
    sb("bat", (128, 4, 8))
    sb("beta", (128, 16)); sb("lnb", (128, 16)); sb("gdec", (128, 16))
    sb("gtmp", (128, 16)); sb("gtmp2", (128, 16))
    sb("bcol", (128, 16)); sb("bCb", (128, 16)); sb("expbC", (128, 16))
    sb("ytile", (128, 2, TT), BF16)
    sb("ssb0", (128, 8, TT)); sb("ssb1", (128, 8, TT))
    sb("Ssb", (128, 128), BF16)
    for s in range(16):
        sb(f"amt{s}", (128, 256), BF16)
        sb(f"wsol{s}", (128, 128), BF16)
        sb(f"ktm{s}", (128, 64), BF16)
        sb(f"rkv{s}", (128, 128), BF16)
        sb(f"qt{s}", (128, 128), BF16)
    for s in range(2):
        sb(f"ktmr{s}", (128, 64), BF16)
        sb(f"vtmr{s}", (128, 64), BF16)
        sb(f"rows{s}", (2, 384))
        sb(f"rowe{s}", (1, 128), BF16)
        sb(f"cols{s}", (128, 6))
        sb(f"dcl{s}", (128, 3))
        sb(f"lnr{s}", (128, 2))
        sb(f"ecol{s}", (128, 2))
        sb(f"btmp{s}", (128, 256))
    for p in range(2):
        sb(f"xh{p}", (128, 128), BF16)
        sb(f"g{p}", (128, 128), BF16)
        sb(f"eneg{p}", (128, 128), BF16)
        sb(f"t32{p}", (128, 128), BF16)
        sb(f"t32T{p}", (128, 128), BF16)
        sb(f"td8T{p}", (128, 128), BF16)
        sb(f"hh{p}", (128, 128), BF16)
        sb(f"d8m{p}", (128, 128), BF16)
        sb(f"l0m{p}", (128, 128), BF16)
        sb(f"l1m{p}", (128, 128), BF16)
        sb(f"cdl{p}", (128, 64), BF16)
        sb(f"fsb{p}", (128, 128), BF16)
        sb(f"o2{p}", (128, 128), BF16)
        sb(f"rstf{p}", (1, 128))
        sb(f"rst{p}", (1, 128), BF16)
        sb(f"ytmp{p}", (128, 128))

    add("tensor", lambda e: e.matmul(
        pb(0)[0:128, 0:8], Hn["o1r"](128), S["gatep"][:],
        start=True, stop=True), r=("gatep", "onesf"), w=("pb0",))
    add("vector", lambda e: e.tensor_copy(S["dtb"][:], pb(0)[0:128, 0:4]),
        r=("pb0",), w=("dtb",))
    add("vector", lambda e: e.tensor_scalar(
        S["negA"][:], pb(0)[0:128, 4:8], -2.3, None, ALU.max),
        r=("pb0",), w=("negA",))
    add("scalar", lambda e: e.activation(S["negA"][:], S["negA"][:], AF.Exp),
        r=("negA",), w=("negA",))
    add("vector", lambda e: e.tensor_scalar(
        S["negA"][:], S["negA"][:], -1.0, None, ALU.mult),
        r=("negA",), w=("negA",))
    add("gpsimd", lambda e: e.memset(S["Ssb"][:], 0.0),
        w=("Ssb0", "Ssb1", "Ssb2", "Ssb3"))
    add("gpsimd", lambda e: e.memset(S["preconv"][:, :, 0:3], 0.0),
        w=("halo",))

    x_fm = dt["x_fm"].ap().rearrange("(k p) t -> p k t", p=128)
    if mode == "mixer":
        out_v = d_out.ap().rearrange("(k p) t -> p k t", p=128)
    else:
        out_v = d_sp.ap().rearrange("(k p) t -> p k t", p=128)

    for t in range(NTILE):
        xt = S["xt0"] if t % 2 == 0 else S["xt1"]
        xres = f"xt{t % 2}"
        ssb = S["ssb0"] if t % 2 == 0 else S["ssb1"]
        ssr = f"ssb{t % 2}"
        dma("sync", xt[:], x_fm[:, :, TT * t:TT * (t + 1)], w=(xres,))

        _ln_ops(Hn, xt, xres, S["xnbf"], "xnbf", "xbf", "rowln", "rowlnb",
                8, TT, 0, 1, 2, 3)

        for mc in range(9):
            cols = 128 if mc < 8 else 8
            bank = 4 + mc % 2
            for kk in range(8):
                add("tensor", lambda e, mc=mc, kk=kk, cols=cols, bank=bank:
                    e.matmul(pb(bank)[0:cols, 0:TT],
                             S["win"][:, kk, 128 * mc:128 * mc + cols],
                             S["xnbf"][:, kk, :],
                             start=(kk == 0), stop=(kk == 7)),
                    r=("win", "xnbf"), w=(PBR[bank],))
            if mc < 6:
                add("scalar", lambda e, mc=mc, bank=bank: e.copy(
                    S["preconv"][:, mc, 3:3 + TT], pb(bank)[0:128, 0:TT]),
                    r=(PBR[bank],), w=("preconv",))
            elif mc < 8:
                add("scalar", lambda e, mc=mc, bank=bank: e.copy(
                    S["zt"][:, TT * (mc - 6):TT * (mc - 5)],
                    pb(bank)[0:128, 0:TT]), r=(PBR[bank],), w=("zt",))
            else:
                add("vector", lambda e, bank=bank: e.tensor_copy(
                    S["barow"][:], pb(bank)[0:8, 0:TT]),
                    r=(PBR[bank],), w=("barow",))

        for cc in range(6):
            bank = 6 + cc % 2
            for tap in range(4):
                add("tensor", lambda e, cc=cc, tap=tap, bank=bank: e.matmul(
                    pb(bank)[0:128, 0:TT], S["convd"][:, cc, tap, :],
                    S["preconv"][:, cc, tap:tap + TT],
                    start=(tap == 0), stop=(tap == 3)),
                    r=("convd", "preconv", "halo"), w=(PBR[bank],))
            add("scalar", lambda e, cc=cc, bank=bank: e.activation(
                S["qkvt"][:, cc, :], pb(bank)[0:128, 0:TT], AF.Silu),
                r=(PBR[bank],), w=("qkvt",))
        add("scalar", lambda e: e.activation(
            S["silz"][:], S["zt"][:], AF.Silu), r=("zt",), w=("silz",))
        add("vector", lambda e: e.tensor_copy(
            S["preconv"][:, :, 0:3], S["preconv"][:, :, TT:TT + 3]),
            r=("preconv",), w=("halo",))

        for c4 in range(4):
            add("tensor", lambda e, c4=c4: e.transpose(
                pb(0)[0:128, 0:8],
                S["barow"][0:8, 128 * c4:128 * c4 + 128],
                S["identf"][0:8, 0:8]), r=("barow", "identf"), w=("pb0",))
            add("vector", lambda e, c4=c4: e.tensor_copy(
                S["bat"][:, c4, :], pb(0)[0:128, 0:8]),
                r=("pb0",), w=("bat",))
        b_all = S["bat"][:, :, 0:4]
        a_all = S["bat"][:, :, 4:8]
        beta = S["beta"][:].rearrange("p (c h) -> p c h", c=4)
        lnb = S["lnb"][:].rearrange("p (c h) -> p c h", c=4)
        gdec = S["gdec"][:].rearrange("p (c h) -> p c h", c=4)
        g1 = S["gtmp"][:].rearrange("p (c h) -> p c h", c=4)
        g2 = S["gtmp2"][:].rearrange("p (c h) -> p c h", c=4)
        dtb_b = _bcast_mid(S["dtb"][:], 4)
        negA_b = _bcast_mid(S["negA"][:], 4)
        add("scalar", lambda e: e.activation(g1, b_all, AF.Exp, scale=-1.0),
            r=("bat",), w=("gtmp",))
        add("vector", lambda e: e.tensor_scalar(
            g2, g1, 1.0, None, ALU.add), r=("gtmp",), w=("gtmp2",))
        add("vector", lambda e: e.reciprocal(beta, g2),
            r=("gtmp2",), w=("beta",))
        add("scalar", lambda e: e.activation(g1, b_all, AF.Abs),
            r=("bat",), w=("gtmp",))
        add("scalar", lambda e: e.activation(g1, g1, AF.Exp, scale=-1.0),
            r=("gtmp",), w=("gtmp",))
        add("scalar", lambda e: e.activation(g1, g1, AF.Ln, bias=1.0),
            r=("gtmp",), w=("gtmp",))
        add("vector", lambda e: e.tensor_scalar(
            g2, b_all, 0.0, None, ALU.min), r=("bat",), w=("gtmp2",))
        add("vector", lambda e: e.tensor_tensor(lnb, g2, g1, ALU.subtract),
            r=("gtmp", "gtmp2"), w=("lnb",))
        add("vector", lambda e: e.tensor_tensor(g1, a_all, dtb_b, ALU.add),
            r=("bat", "dtb"), w=("gtmp",))
        add("scalar", lambda e: e.activation(g2, g1, AF.Abs),
            r=("gtmp",), w=("gtmp2",))
        add("scalar", lambda e: e.activation(g2, g2, AF.Exp, scale=-1.0),
            r=("gtmp2",), w=("gtmp2",))
        add("scalar", lambda e: e.activation(g2, g2, AF.Ln, bias=1.0),
            r=("gtmp2",), w=("gtmp2",))
        add("vector", lambda e: e.tensor_scalar(
            g1, g1, 0.0, None, ALU.max), r=("gtmp",), w=("gtmp",))
        add("vector", lambda e: e.tensor_tensor(g1, g1, g2, ALU.add),
            r=("gtmp", "gtmp2"), w=("gtmp",))
        add("vector", lambda e: e.tensor_tensor(gdec, g1, negA_b, ALU.mult),
            r=("gtmp", "negA"), w=("gdec",))
        for c4 in range(4):
            add("tensor", lambda e, c4=c4: e.matmul(
                pb(1)[0:128, 0:4], S["triu"][:],
                S["gdec"][:, 4 * c4:4 * c4 + 4], start=True, stop=True),
                r=("triu", "gdec"), w=("pb1",))
            add("vector", lambda e, c4=c4: e.tensor_copy(
                S["bcol"][:, 4 * c4:4 * c4 + 4], pb(1)[0:128, 0:4]),
                r=("pb1",), w=("bcol",))
            add("tensor", lambda e, c4=c4: e.matmul(
                pb(2)[0:128, 0:4], oall(128),
                S["gdec"][:, 4 * c4:4 * c4 + 4], start=True, stop=True),
                r=("onesf", "gdec"), w=("pb2",))
            add("vector", lambda e, c4=c4: e.tensor_copy(
                S["bCb"][:, 4 * c4:4 * c4 + 4], pb(2)[0:128, 0:4]),
                r=("pb2",), w=("bCb",))
        add("scalar", lambda e: e.activation(
            S["expbC"][:], S["bCb"][:], AF.Exp), r=("bCb",), w=("expbC",))

        def drive_pair(g0, g1):
            d0 = d1 = False
            while not (d0 and d1):
                if not d0:
                    try:
                        next(g0)
                    except StopIteration:
                        d0 = True
                if not d1:
                    try:
                        next(g1)
                    except StopIteration:
                        d1 = True
        for c4 in range(4):
            drive_pair(_scan_waveA(Hn, c4, 0), _scan_waveA(Hn, c4, 1))
            drive_pair(_scan_waveA(Hn, c4, 2), _scan_waveA(Hn, c4, 3))
        for spair in range(8):
            drive_pair(_scan_solve(Hn, 2 * spair),
                       _scan_solve(Hn, 2 * spair + 1))
        for c4 in range(4):
            for h in range(NH):
                _scan_waveC(Hn, c4, h)

        for mh in range(8):
            bank = mh % 2
            for kk in range(2):
                add("tensor", lambda e, mh=mh, kk=kk, bank=bank: e.matmul(
                    pb(bank)[0:128, 0:TT],
                    S["wo"][:, kk, 128 * mh:128 * mh + 128],
                    S["ytile"][:, kk, :], start=(kk == 0), stop=(kk == 1)),
                    r=("wo", "ytile"), w=(PBR[bank],))
            add("scalar", lambda e, mh=mh, bank=bank, ssb=ssb: e.copy(
                ssb[:, mh, :], pb(bank)[0:128, 0:TT]),
                r=(PBR[bank],), w=(ssr,))
        wtag = (f"OUTs{t}",) if mode == "mixer" else (f"d_sp{t}",)
        dma("gpsimd", out_v[:, :, TT * t:TT * (t + 1)], ssb[:],
            r=(ssr,), w=wtag)
        if t == 0 and mode == "mixer" and Hn.get("dbg"):
            dbgt = Hn["dbg"]
            dma("gpsimd", dbgt["mxn"].ap(), S["xnbf"][:], r=("xnbf",),
                w=("OUTd0",))
            dma("gpsimd", dbgt["mqkv"].ap(), S["qkvt"][:], r=("qkvt",),
                w=("OUTd1",))
            dma("gpsimd", dbgt["msilz"].ap(), S["silz"][:], r=("silz",),
                w=("OUTd2",))
            dma("gpsimd", dbgt["mgat"].ap()[:, 0:16], S["beta"][:],
                r=("beta",), w=("OUTd3",))
            dma("gpsimd", dbgt["mgat"].ap()[:, 16:32], S["lnb"][:],
                r=("lnb",), w=("OUTd4",))
            dma("gpsimd", dbgt["mgat"].ap()[:, 32:48], S["gdec"][:],
                r=("gdec",), w=("OUTd5",))
            dma("gpsimd", dbgt["mgat"].ap()[:, 48:64], S["bcol"][:],
                r=("bcol",), w=("OUTd6",))
            dma("gpsimd", dbgt["myt"].ap(), S["ytile"][:], r=("ytile",),
                w=("OUTd7",))
            dma("gpsimd", dbgt["mamt"].ap(), S["amt0"][:], r=("amt0",),
                w=("OUTd8",))
            dma("gpsimd", dbgt["mwsol"].ap(), S["wsol0"][:], r=("wsol0",),
                w=("OUTd9",))
            dma("gpsimd", dbgt["mktm"].ap(), S["ktm0"][:], r=("ktm0",),
                w=("OUTd10",))
            dma("gpsimd", dbgt["mrkv"].ap(), S["rkv0"][:], r=("rkv0",),
                w=("OUTd11",))
            dma("gpsimd", dbgt["mqt"].ap(), S["qt0"][:], r=("qt0",),
                w=("OUTd12",))


def _scan_waveA(Hn, c4, h):
    sc, S, pb, pbb, PBR, o1b = (Hn["sc"], Hn["S"], Hn["pb"], Hn["pbb"],
                                Hn["PBR"], Hn["o1b"])
    add = sc.add
    sl = 4 * c4 + h
    ss = str(sl)
    sp = str(sl % 2)
    hb = 64 * (h % 2)
    qs = S["qkvt"][hb:hb + 64, h // 2, 128 * c4:128 * c4 + 128]
    ks = S["qkvt"][hb:hb + 64, 2 + h // 2, 128 * c4:128 * c4 + 128]
    vs = S["qkvt"][hb:hb + 64, 4 + h // 2, 128 * c4:128 * c4 + 128]
    idsl = S["identbf"][hb:hb + 64, hb:hb + 64]
    pset = sl % 2
    bp, bd = (0, 1) if pset == 0 else (2, 3)

    add("tensor", lambda e: e.transpose(pbb(bp)[0:128, 768:832], ks, idsl),
        r=("qkvt", "identbf"), w=(PBR[bp],))
    add("vector", lambda e: e.tensor_copy(
        S["ktmr" + sp][:], pbb(bp)[0:128, 768:832]),
        r=(PBR[bp],), w=("ktmr" + sp,))
    add("tensor", lambda e: e.transpose(pbb(bp)[0:128, 896:960], vs, idsl),
        r=("qkvt", "identbf"), w=(PBR[bp],))
    add("vector", lambda e: e.tensor_copy(
        S["vtmr" + sp][:], pbb(bp)[0:128, 896:960]),
        r=(PBR[bp],), w=("vtmr" + sp,))

    yield
    add("tensor", lambda e: e.matmul(pb(bp)[0:128, 0:128], ks, qs,
                                     start=True, stop=False),
        r=("qkvt",), w=(PBR[bp],))
    add("tensor", lambda e: e.matmul(pb(bp)[0:128, 128:256], ks, ks,
                                     start=False, stop=False),
        r=("qkvt",), w=(PBR[bp],))
    add("tensor", lambda e: e.matmul(pb(bp)[0:128, 256:384], qs, qs,
                                     start=False, stop=True),
        r=("qkvt",), w=(PBR[bp],))
    yield
    diag_in = pb(bp)[0:128, 128:384].rearrange("p (c x) -> p c x", c=2)
    ident2 = _bcast_mid(S["identf"][:], 2).rearrange("p c x -> p c x") \
        if False else bass.AP(S["identf"], 0, [[128, 128], [0, 2], [1, 128]])
    add("vector", lambda e: e.tensor_tensor(
        S["btmp" + sp][:].rearrange("p (c x) -> p c x", c=2), diag_in,
        ident2, ALU.mult), r=(PBR[bp], "identf"), w=("btmp" + sp,))
    add("vector", lambda e: e.reduce_sum(
        S["lnr" + sp][:], S["btmp" + sp][:].rearrange(
            "p (c x) -> p c x", c=2), axis=AX.X),
        r=("btmp" + sp,), w=("lnr" + sp,))
    add("scalar", lambda e: e.activation(
        S["lnr" + sp][:], S["lnr" + sp][:], AF.Ln, bias=1e-6),
        r=("lnr" + sp,), w=("lnr" + sp,))
    yield
    bcolh = S["bcol"][:, 4 * c4 + h:4 * c4 + h + 1]
    lnrk = S["lnr" + sp][:, 0:1]
    lnrq = S["lnr" + sp][:, 1:2]
    lnbh = S["lnb"][:, 4 * c4 + h:4 * c4 + h + 1]
    bCh = S["bCb"][:, 4 * c4 + h:4 * c4 + h + 1]
    beth = S["beta"][:, 4 * c4 + h:4 * c4 + h + 1]
    dcl = S["dcl" + sp]
    add("vector", lambda e: e.scalar_tensor_tensor(
        dcl[:, 0:1], lnrk, 0.5, bcolh, ALU.mult, ALU.add),
        r=("lnr" + sp, "bcol"), w=("dcl" + sp,))
    add("vector", lambda e: e.tensor_scalar(
        dcl[:, 0:1], dcl[:, 0:1], -1.0, None, ALU.mult),
        r=("dcl" + sp,), w=("dcl" + sp,))
    add("vector", lambda e: e.scalar_tensor_tensor(
        dcl[:, 1:2], lnrq, -0.5, bcolh, ALU.mult, ALU.add),
        r=("lnr" + sp, "bcol"), w=("dcl" + sp,))
    add("vector", lambda e: e.tensor_scalar(
        dcl[:, 1:2], dcl[:, 1:2], LN8, None, ALU.add),
        r=("dcl" + sp,), w=("dcl" + sp,))
    add("vector", lambda e: e.scalar_tensor_tensor(
        dcl[:, 2:3], lnrk, -1.0, lnbh, ALU.mult, ALU.add),
        r=("lnr" + sp, "lnb"), w=("dcl" + sp,))
    add("vector", lambda e: e.tensor_tensor(
        dcl[:, 2:3], dcl[:, 2:3], dcl[:, 0:1], ALU.subtract),
        r=("dcl" + sp,), w=("dcl" + sp,))
    yield
    add("scalar", lambda e: e.activation(
        S["ecol" + sp][:, 0:1], dcl[:, 2:3], AF.Exp),
        r=("dcl" + sp,), w=("ecol" + sp,))
    add("vector", lambda e: e.tensor_tensor(
        S["ecol" + sp][:, 1:2], bCh, dcl[:, 0:1], ALU.add),
        r=("bCb", "dcl" + sp), w=("ecol" + sp,))
    add("scalar", lambda e: e.activation(
        S["ecol" + sp][:, 1:2], S["ecol" + sp][:, 1:2], AF.Exp),
        r=("ecol" + sp,), w=("ecol" + sp,))
    yield
    add("vector", lambda e: e.tensor_scalar(
        S["ktm" + ss][:], S["ktmr" + sp][:], S["ecol" + sp][:, 1:2], None,
        ALU.mult), r=("ktmr" + sp, "ecol" + sp), w=("ktm" + ss,))
    add("vector", lambda e: e.tensor_scalar(
        S["rkv" + ss][:, 0:64], S["vtmr" + sp][:], beth, None, ALU.mult),
        r=("vtmr" + sp, "beta"), w=("rkv" + ss,))
    add("vector", lambda e: e.tensor_scalar(
        S["rkv" + ss][:, 64:128], S["ktmr" + sp][:], S["ecol" + sp][:, 0:1],
        None, ALU.mult), r=("ktmr" + sp, "ecol" + sp), w=("rkv" + ss,))
    yield
    cl = S["cols" + sp]
    add("vector", lambda e: e.tensor_scalar(
        cl[:, 0:1], dcl[:, 0:1], 0.0, 1.0, ALU.mult, ALU.add),
        r=("dcl" + sp,), w=("cols" + sp,))
    add("vector", lambda e: e.tensor_copy(cl[:, 1:2], dcl[:, 0:1]),
        r=("dcl" + sp,), w=("cols" + sp,))
    add("vector", lambda e: e.tensor_copy(cl[:, 2:3], dcl[:, 1:2]),
        r=("dcl" + sp,), w=("cols" + sp,))
    add("vector", lambda e: e.tensor_scalar(
        cl[:, 3:4], dcl[:, 0:1], 0.0, 1.0, ALU.mult, ALU.add),
        r=("dcl" + sp,), w=("cols" + sp,))
    add("vector", lambda e: e.tensor_copy(cl[:, 4:5], dcl[:, 2:3]),
        r=("dcl" + sp,), w=("cols" + sp,))
    add("vector", lambda e: e.tensor_scalar(
        cl[:, 5:6], dcl[:, 0:1], 0.0, 1.0, ALU.mult, ALU.add),
        r=("dcl" + sp,), w=("cols" + sp,))
    yield
    for pi in range(3):
        add("tensor", lambda e, pi=pi: e.transpose(
            pb(bd)[0:2, 256:384], cl[:, 2 * pi:2 * pi + 2], S["identf"][:]),
            r=("cols" + sp, "identf"), w=(PBR[bd],))
        add("vector", lambda e, pi=pi: e.tensor_copy(
            S["rows" + sp][0:2, 128 * pi:128 * pi + 128],
            pb(bd)[0:2, 256:384]), r=(PBR[bd],), w=("rows" + sp,))
    yield
    add("tensor", lambda e: e.matmul(
        pb(bd)[0:128, 0:128], S["rows" + sp][0:2, 0:128],
        S["rows" + sp][0:2, 128:256], start=True, stop=False),
        r=("rows" + sp,), w=(PBR[bd],))
    add("tensor", lambda e: e.matmul(
        pb(bd)[0:128, 128:256], S["rows" + sp][0:2, 0:128],
        S["rows" + sp][0:2, 256:384], start=False, stop=True),
        r=("rows" + sp,), w=(PBR[bd],))
    yield
    add("vector", lambda e: e.tensor_tensor(
        S["btmp" + sp][:], pb(bd)[0:128, 0:256], S["maskb"][:], ALU.add),
        r=(PBR[bd], "maskb"), w=("btmp" + sp,))
    add("scalar", lambda e: e.activation(
        S["btmp" + sp][:], S["btmp" + sp][:], AF.Exp),
        r=("btmp" + sp,), w=("btmp" + sp,))
    add("vector", lambda e: e.tensor_tensor(
        S["amt" + ss][:], S["btmp" + sp][:], pb(bp)[0:128, 0:256], ALU.mult),
        r=("btmp" + sp, PBR[bp]), w=("amt" + ss,))
    yield
    add("scalar", lambda e: e.activation(
        S["rowe" + sp][:], S["rows" + sp][0:1, 128:256], AF.Exp),
        r=("rows" + sp,), w=("rowe" + sp,))
    add("tensor", lambda e: e.matmul(
        pb(bd)[hb:hb + 64, 384:512], o1b(64), S["rowe" + sp][:],
        start=True, stop=True, tile_position=(0, hb)),
        r=("rowe" + sp, "onesb"), w=(PBR[bd],))
    add("vector", lambda e: e.tensor_tensor(
        S["qt" + ss][hb:hb + 64, :], qs, pb(bd)[hb:hb + 64, 384:512],
        ALU.mult), r=("qkvt", PBR[bd]), w=("qt" + ss,))


def _scan_solve(Hn, sl):
    sc, S, pb, pbb, PBR = Hn["sc"], Hn["S"], Hn["pb"], Hn["pbb"], Hn["PBR"]
    add = sc.add
    ss = str(sl)
    p = sl % 2
    bs = 4 + p
    amtM = S["amt" + ss][:, 128:256]
    add("vector", lambda e: e.tensor_tensor(
        S[f"d8m{p}"][:], amtM, S["m8"][:], ALU.mult),
        r=("amt" + ss, "m8"), w=(f"d8m{p}",))
    add("vector", lambda e: e.tensor_tensor(
        S[f"l0m{p}"][:], amtM, S["l0"][:], ALU.mult),
        r=("amt" + ss, "l0"), w=(f"l0m{p}",))
    add("vector", lambda e: e.tensor_tensor(
        S[f"l1m{p}"][:], amtM, S["l1"][:], ALU.mult),
        r=("amt" + ss, "l1"), w=(f"l1m{p}",))
    for it in range(7):
        rhs = S["identbf"] if it == 0 else S[f"xh{p}"]
        rr = "identbf" if it == 0 else f"xh{p}"
        add("tensor", lambda e, rhs=rhs: e.matmul(
            pb(bs)[0:128, 0:128], S[f"d8m{p}"][:], rhs[:],
            start=True, stop=False), r=(f"d8m{p}", rr), w=(PBR[bs],))
        add("tensor", lambda e: e.matmul(
            pb(bs)[0:128, 0:128], S["identbf"][:], S["identbf"][:],
            start=False, stop=True), r=("identbf",), w=(PBR[bs],))
        if it % 2 == 0:
            add("vector", lambda e: e.tensor_copy(
                S[f"xh{p}"][:], pb(bs)[0:128, 0:128]),
                r=(PBR[bs],), w=(f"xh{p}",))
        else:
            add("scalar", lambda e: e.copy(
                S[f"xh{p}"][:], pb(bs)[0:128, 0:128]),
                r=(PBR[bs],), w=(f"xh{p}",))
        yield
    add("tensor", lambda e: e.matmul(
        pb(bs)[0:128, 0:128], S[f"xh{p}"][:], S[f"l0m{p}"][:],
        start=True, stop=True), r=(f"xh{p}", f"l0m{p}"), w=(PBR[bs],))
    add("vector", lambda e: e.tensor_copy(
        S[f"eneg{p}"][:], pb(bs)[0:128, 0:128]),
        r=(PBR[bs],), w=(f"eneg{p}",))
    for it in range(3):
        rhs = S["identbf"] if it == 0 else S[f"g{p}"]
        rr = "identbf" if it == 0 else f"g{p}"
        add("tensor", lambda e, rhs=rhs: e.matmul(
            pb(bs)[0:128, 0:128], S[f"eneg{p}"][:], rhs[:],
            start=True, stop=False), r=(f"eneg{p}", rr), w=(PBR[bs],))
        add("tensor", lambda e: e.matmul(
            pb(bs)[0:128, 0:128], S["identbf"][:], S["identbf"][:],
            start=False, stop=True), r=("identbf",), w=(PBR[bs],))
        add("scalar", lambda e: e.copy(S[f"g{p}"][:], pb(bs)[0:128, 0:128]),
            r=(PBR[bs],), w=(f"g{p}",))
        yield
    add("tensor", lambda e: e.transpose(
        pbb(bs)[0:128, 512:640], S[f"xh{p}"][:], S["identbf"][:]),
        r=(f"xh{p}", "identbf"), w=(PBR[bs],))
    add("vector", lambda e: e.tensor_copy(
        S[f"td8T{p}"][:], pbb(bs)[0:128, 512:640]),
        r=(PBR[bs],), w=(f"td8T{p}",))
    yield
    add("tensor", lambda e: e.matmul(
        pb(bs)[0:128, 0:128], S[f"td8T{p}"][:], S[f"g{p}"][:],
        start=True, stop=True), r=(f"td8T{p}", f"g{p}"), w=(PBR[bs],))
    add("scalar", lambda e: e.copy(S[f"t32{p}"][:], pb(bs)[0:128, 0:128]),
        r=(PBR[bs],), w=(f"t32{p}",))
    yield
    add("tensor", lambda e: e.transpose(
        pbb(bs)[0:128, 512:640], S[f"t32{p}"][:], S["identbf"][:]),
        r=(f"t32{p}", "identbf"), w=(PBR[bs],))
    add("vector", lambda e: e.tensor_copy(
        S[f"t32T{p}"][:], pbb(bs)[0:128, 512:640]),
        r=(PBR[bs],), w=(f"t32T{p}",))
    yield
    add("tensor", lambda e: e.matmul(
        pb(bs)[0:128, 0:128], S[f"t32T{p}"][:], S["rkv" + ss][:],
        start=True, stop=True), r=(f"t32T{p}", "rkv" + ss), w=(PBR[bs],))
    add("vector", lambda e: e.tensor_copy(
        S[f"hh{p}"][:], pb(bs)[0:128, 0:128]), r=(PBR[bs],), w=(f"hh{p}",))
    yield
    for it in range(3):
        add("tensor", lambda e: e.matmul(
            pb(bs)[0:128, 0:128], S[f"l1m{p}"][:], S[f"hh{p}"][:],
            start=True, stop=False), r=(f"l1m{p}", f"hh{p}"), w=(PBR[bs],))
        add("tensor", lambda e: e.matmul(
            pb(bs)[0:128, 0:128], S["identbf"][:], S["rkv" + ss][:],
            start=False, stop=True), r=("identbf", "rkv" + ss),
            w=(PBR[bs],))
        add("scalar", lambda e: e.copy(S[f"hh{p}"][:], pb(bs)[0:128, 0:128]),
            r=(PBR[bs],), w=(f"hh{p}",))
        yield
        add("tensor", lambda e: e.matmul(
            pb(bs)[0:128, 0:128], S[f"t32T{p}"][:], S[f"hh{p}"][:],
            start=True, stop=True), r=(f"t32T{p}", f"hh{p}"), w=(PBR[bs],))
        if it < 2:
            add("vector", lambda e: e.tensor_copy(
                S[f"hh{p}"][:], pb(bs)[0:128, 0:128]),
                r=(PBR[bs],), w=(f"hh{p}",))
        yield
    add("vector", lambda e: e.tensor_copy(
        S["wsol" + ss][:], pb(bs)[0:128, 0:128]),
        r=(PBR[bs],), w=("wsol" + ss,))


def _scan_waveC(Hn, c4, h):
    sc, S, pb, pbb, PBR, o1b = (Hn["sc"], Hn["S"], Hn["pb"], Hn["pbb"],
                                Hn["PBR"], Hn["o1b"])
    add = sc.add
    sl = 4 * c4 + h
    ss = str(sl)
    p = sl % 2
    hb = 64 * (h % 2)
    hf = 64 * (h // 2)
    Ssl = S["Ssb"][hb:hb + 64, hf:hf + 64]
    Sres = f"Ssb{h}"
    bO = 7 if p == 0 else 2
    bF = 2 if p == 0 else 7
    ech = S["expbC"][hb:hb + 64, 4 * c4 + h:4 * c4 + h + 1]

    add("tensor", lambda e: e.matmul(
        pb(6)[hb:hb + 64, 0:64], S["ktm" + ss][:], S["wsol" + ss][:, 0:64],
        start=True, stop=False, tile_position=(0, hb)),
        r=("ktm" + ss, "wsol" + ss), w=("pb6",))
    add("tensor", lambda e: e.matmul(
        pb(bF)[hb:hb + 64, 0:128], S["wsol" + ss][:, 64:128],
        S["amt" + ss][:, 0:128], start=True, stop=False,
        tile_position=(0, hb)),
        r=("wsol" + ss, "amt" + ss), w=(PBR[bF],))
    add("tensor", lambda e: e.matmul(
        pb(bF)[hb:hb + 64, 128:192], S["wsol" + ss][:, 64:128],
        S["ktm" + ss][:], start=False, stop=True, tile_position=(0, hb)),
        r=("wsol" + ss, "ktm" + ss), w=(PBR[bF],))
    add("vector", lambda e: e.tensor_scalar(
        S[f"fsb{p}"][hb:hb + 64, :], pb(bF)[hb:hb + 64, 0:128], -1.0, None,
        ALU.mult), r=(PBR[bF],), w=(f"fsb{p}",))
    add("vector", lambda e: e.scalar_tensor_tensor(
        S[f"cdl{p}"][hb:hb + 64, :], S["ibf"][hb:hb + 64, hb:hb + 64], ech,
        pb(bF)[hb:hb + 64, 128:192], ALU.mult, ALU.subtract),
        r=("ibf", "expbC", PBR[bF]), w=(f"cdl{p}",))
    add("tensor", lambda e: e.matmul(
        pb(bO)[hb:hb + 64, 0:128], S["wsol" + ss][:, 0:64],
        S["amt" + ss][:, 0:128], start=True, stop=False,
        tile_position=(0, hb)),
        r=("wsol" + ss, "amt" + ss), w=(PBR[bO],))
    add("tensor", lambda e: e.matmul(
        pb(bO)[hb:hb + 64, 0:128], Ssl, S[f"fsb{p}"][hb:hb + 64, :],
        start=False, stop=False, tile_position=(hb, hb)),
        r=(Sres, f"fsb{p}"), w=(PBR[bO],))
    add("tensor", lambda e: e.matmul(
        pb(bO)[hb:hb + 64, 0:128], Ssl, S["qt" + ss][hb:hb + 64, :],
        start=False, stop=True, tile_position=(hb, hb)),
        r=(Sres, "qt" + ss), w=(PBR[bO],))
    add("tensor", lambda e: e.matmul(
        pb(6)[hb:hb + 64, 0:64], S[f"cdl{p}"][hb:hb + 64, :], Ssl,
        start=False, stop=True, tile_position=(hb, hb)),
        r=(f"cdl{p}", Sres), w=("pb6",))
    add("vector", lambda e: e.tensor_copy(Ssl, pb(6)[hb:hb + 64, 0:64]),
        r=("pb6",), w=(Sres,))
    add("scalar", lambda e: e.activation(
        S[f"o2{p}"][hb:hb + 64, :], pb(bO)[hb:hb + 64, 0:128], AF.Square),
        r=(PBR[bO],), w=(f"o2{p}",))
    bE = 4 + p
    add("tensor", lambda e: e.matmul(
        pb(bE)[0:1, 0:128], S["onesb"][hb:hb + 64, 0:1],
        S[f"o2{p}"][hb:hb + 64, :], start=True, stop=True,
        tile_position=(hb, 0)), r=(f"o2{p}", "onesb"), w=(PBR[bE],))
    add("scalar", lambda e: e.activation(
        S[f"rstf{p}"][:], pb(bE)[0:1, 0:128], AF.Ln, scale=1.0 / 64,
        bias=1e-6), r=(PBR[bE],), w=(f"rstf{p}",))
    add("scalar", lambda e: e.activation(
        S[f"rst{p}"][:], S[f"rstf{p}"][:], AF.Exp, scale=-0.5),
        r=(f"rstf{p}",), w=(f"rst{p}",))
    add("tensor", lambda e: e.matmul(
        pb(bE)[hb:hb + 64, 128:256], o1b(64), S[f"rst{p}"][:],
        start=True, stop=True, tile_position=(0, hb)),
        r=(f"rst{p}", "onesb"), w=(PBR[bE],))
    silzs = S["silz"][hb:hb + 64, TT * (h // 2) + 128 * c4:
                      TT * (h // 2) + 128 * c4 + 128]
    add("vector", lambda e: e.scalar_tensor_tensor(
        S[f"ytmp{p}"][hb:hb + 64, :], pb(bO)[hb:hb + 64, 0:128],
        S["normw"][hb:hb + 64, 0:1], silzs, ALU.mult, ALU.mult),
        r=(PBR[bO], "normw", "silz"), w=(f"ytmp{p}",))
    ysl = S["ytile"][hb:hb + 64, h // 2, 128 * c4:128 * c4 + 128]
    add("vector", lambda e: e.tensor_tensor(
        ysl, S[f"ytmp{p}"][hb:hb + 64, :], pb(bE)[hb:hb + 64, 128:256],
        ALU.mult), r=(f"ytmp{p}", PBR[bE]), w=("ytile",))


def _build_ffn(Hn, d_sr, d_wb, d_out, mode):
    nc, sc, S, dt, pb, pbb, sb, dma, PBR = (
        Hn["nc"], Hn["sc"], Hn["S"], Hn["dt"], Hn["pb"], Hn["pbb"], Hn["sb"],
        Hn["dma"], Hn["PBR"])
    o1b = Hn["o1b"]
    add = sc.add
    sb("xf0", (128, 8, FT))
    sb("xftmp", (128, 8, FT))
    sb("xfbf", (128, 8 * FT), BF16)
    sb("xnf", (128, 8, FT), BF16)
    sb("rowln2", (1, 3 * FT))
    sb("rowln2b", (1, 2 * FT), BF16)
    sb("wr", (128, 8, 7))
    sb("wrb", (128, 8, 7), BF16)
    sb("rb_b", (128, 7))
    sb("lsb", (7, FT))
    sb("ltm", (128, 4, 7))
    sb("rcol", (128, 4))
    sb("rt1", (128, 4, 8)); sb("rt2", (128, 4, 7)); sb("rt3", (128, 4, 7))
    sb("wtm", (128, 4, 7))
    sb("wrow", (1, 7 * FT))
    sb("wrowb", (1, 7 * FT), BF16)
    sb("wbc", (128, 7, FT), BF16)
    sb("ptile", (128, NMC, FT), BF16)
    sb("pt1", (128, FT), BF16)
    sb("acc", (128, 8, FT))
    sb("scnd", (128, 4, FT))
    sb("outb", (128, 2, FT))
    for p in range(2):
        sb(f"wg{p}", (128, 8, 512), BF16)
        sb(f"wu{p}", (128, 8, 512), BF16)
        sb(f"wd{p}", (128, NMC, 256), BF16)
    dma("sync", S["wr"][:], dt["wr"].ap(), w=("wr",))
    add("vector", lambda e: e.tensor_copy(S["wrb"][:], S["wr"][:]),
        r=("wr",), w=("wrb",))
    dma("sync", S["rb_b"][:], dt["rb_b"].ap(), w=("rb_b",))
    sb("cmb", (128, 4))
    dma("sync", S["cmb"][:], dt["cmb"].ap(), w=("cmb",))

    x_fv = dt["x_ffn"].ap().rearrange("(k p) t -> p k t", p=128)
    out_v = d_out.ap().rearrange("(k p) t -> p k t", p=128)
    gu_v = dt["ffn_gu"].ap()
    dd_v = dt["ffn_d"].ap()

    for u in range(NFT):
        xf = S["xf0"]
        xres = "xf0"
        dma("sync", xf[:], x_fv[:, :, FT * u:FT * (u + 1)], w=(xres,))
        _ln_ops(Hn, xf, xres, S["xnf"], "xnf", "xfbf", "rowln2", "rowln2b",
                8, FT, 0, 1, 2, 3, xtmp="xftmp", xtmpres="xftmp")

        for kk in range(8):
            add("tensor", lambda e, kk=kk: e.matmul(
                pb(4)[0:7, 0:FT], S["wr"][:, kk, :], S["xftmp"][:, kk, :],
                start=(kk == 0), stop=(kk == 7)),
                r=("wr", "xftmp"), w=("pb4",))
        add("vector", lambda e: e.tensor_copy(S["lsb"][:], pb(4)[0:7, 0:FT]),
            r=("pb4",), w=("lsb",))
        for j in range(4):
            add("tensor", lambda e, j=j: e.transpose(
                pb(5)[0:128, 0:7], S["lsb"][0:7, 128 * j:128 * j + 128],
                S["identf"][0:7, 0:7]), r=("lsb", "identf"), w=("pb5",))
            add("vector", lambda e, j=j: e.tensor_copy(
                S["ltm"][:, j, :], pb(5)[0:128, 0:7]),
                r=("pb5",), w=("ltm",))
            add("tensor", lambda e, j=j: e.transpose(
                pb(5)[0:128, 8:9],
                S["rowln2"][0:1, FT + 128 * j:FT + 128 * j + 128],
                S["identf"][0:1, 0:1]), r=("rowln2", "identf"), w=("pb5",))
            add("vector", lambda e, j=j: e.tensor_copy(
                S["rcol"][:, j:j + 1], pb(5)[0:128, 8:9]),
                r=("pb5",), w=("rcol",))
        rc_b = bass.AP(S["rcol"], 0, [[4, 128], [1, 4], [0, 7]])
        rb_bb = _bcast_mid(S["rb_b"][:], 4)
        add("vector", lambda e: e.tensor_tensor(
            S["ltm"][:], S["ltm"][:], rc_b, ALU.mult),
            r=("ltm", "rcol"), w=("ltm",))
        add("vector", lambda e: e.tensor_tensor(
            S["ltm"][:], S["ltm"][:], rb_bb, ALU.add),
            r=("ltm", "rb_b"), w=("ltm",))
        lt = S["ltm"][:]
        r1 = S["rt1"][:]
        m1 = r1[:, :, 0:1]; m2 = r1[:, :, 1:2]; cnt = r1[:, :, 2:3]
        mm2 = r1[:, :, 3:4]; den = r1[:, :, 4:5]
        t5 = r1[:, :, 5:6]; t6 = r1[:, :, 6:7]; t7 = r1[:, :, 7:8]
        r2 = S["rt2"][:]
        r3 = S["rt3"][:]

        def bc(col):
            a = col.ap
            return bass.AP(col.tensor, col.offset,
                           [list(a[0]), list(a[1]), [0, 7]])
        add("vector", lambda e: e.reduce_max(m1, lt, axis=AX.X),
            r=("ltm",), w=("rt1",))
        add("vector", lambda e: e.tensor_tensor(
            r2, lt, bc(m1), ALU.is_ge), r=("ltm", "rt1"), w=("rt2",))
        add("vector", lambda e: e.reduce_sum(cnt, r2, axis=AX.X),
            r=("rt2",), w=("rt1",))
        add("vector", lambda e: e.scalar_tensor_tensor(
            r3, r2, -1e30, lt, ALU.mult, ALU.add), r=("rt2", "ltm"),
            w=("rt3",))
        add("vector", lambda e: e.reduce_max(m2, r3, axis=AX.X),
            r=("rt3",), w=("rt1",))
        add("vector", lambda e: e.tensor_scalar(
            t5, cnt, 2.0, None, ALU.is_ge), r=("rt1",), w=("rt1",))
        add("vector", lambda e: e.tensor_tensor(
            t6, m1, m2, ALU.subtract), r=("rt1",), w=("rt1",))
        add("vector", lambda e: e.tensor_tensor(
            t7, t5, t6, ALU.mult), r=("rt1",), w=("rt1",))
        add("vector", lambda e: e.tensor_tensor(
            mm2, m2, t7, ALU.add), r=("rt1",), w=("rt1",))
        add("vector", lambda e: e.tensor_tensor(
            t5, mm2, m1, ALU.subtract), r=("rt1",), w=("rt1",))
        add("scalar", lambda e: e.activation(t5, t5, AF.Exp),
            r=("rt1",), w=("rt1",))
        add("vector", lambda e: e.tensor_scalar(
            den, t5, 1.0, None, ALU.add), r=("rt1",), w=("rt1",))
        add("vector", lambda e: e.reciprocal(den, den), r=("rt1",),
            w=("rt1",))
        add("vector", lambda e: e.tensor_tensor(
            r2, lt, bc(mm2), ALU.is_ge), r=("ltm", "rt1"), w=("rt2",))
        add("vector", lambda e: e.tensor_tensor(
            r3, lt, bc(m1), ALU.subtract), r=("ltm", "rt1"), w=("rt3",))
        add("scalar", lambda e: e.activation(r3, r3, AF.Exp),
            r=("rt3",), w=("rt3",))
        add("vector", lambda e: e.tensor_tensor(r3, r3, r2, ALU.mult),
            r=("rt3", "rt2"), w=("rt3",))
        add("vector", lambda e: e.tensor_tensor(
            S["wtm"][:], r3, bc(den), ALU.mult), r=("rt3", "rt1"),
            w=("wtm",))
        for j in range(4):
            add("tensor", lambda e, j=j: e.transpose(
                pb(5)[0:7, 0:128], S["wtm"][:, j, :], S["identf"][:]),
                r=("wtm", "identf"), w=("pb5",))
            add("vector", lambda e, j=j: e.tensor_copy(
                S["lsb"][0:7, 128 * j:128 * j + 128], pb(5)[0:7, 0:128]),
                r=("pb5",), w=("lsb",))
        dma("sync", d_wb.ap(), S["lsb"][:], r=("lsb",), w=("d_wb",))
        dma("sync", S["wrow"][:],
            d_wb.ap().rearrange("e t -> (e t)").rearrange("(o n) -> o n",
                                                          o=1),
            r=("d_wb",), w=("wrow",))
        add("vector", lambda e: e.tensor_copy(S["wrowb"][:], S["wrow"][:]),
            r=("wrow",), w=("wrowb",))
        for ee in range(7):
            add("tensor", lambda e, ee=ee: e.matmul(
                pb(5)[0:128, 0:FT], o1b(128),
                S["wrowb"][0:1, FT * ee:FT * (ee + 1)],
                start=True, stop=True), r=("wrowb", "onesb"), w=("pb5",))
            add("vector", lambda e, ee=ee: e.tensor_copy(
                S["wbc"][:, ee, :], pb(5)[0:128, 0:FT]),
                r=("pb5",), w=("wbc",))

        gu_blocks = [(b0, min(b0 + 4, NMC)) for b0 in range(0, NMC, 4)]
        mh_blocks = [(b0, b0 + 2) for b0 in range(0, 8, 2)]

        def em_gu_dma(e_i, bi):
            b0, b1 = gu_blocks[bi]
            pwm = bi % 2
            dma("sync", S[f"wg{pwm}"][:, :, 0:128 * (b1 - b0)],
                gu_v[e_i, 0, :, :, 128 * b0:128 * b1],
                w=(f"wg{pwm}",))
            dma("sync", S[f"wu{pwm}"][:, :, 0:128 * (b1 - b0)],
                gu_v[e_i, 1, :, :, 128 * b0:128 * b1],
                w=(f"wu{pwm}",))

        def em_wd_dma(e_i, bi):
            b0, b1 = mh_blocks[bi]
            pw = bi % 2
            dma("sync", S[f"wd{pw}"][:],
                dd_v[e_i][:, :, 128 * b0:128 * b1], w=(f"wd{pw}",))

        for e_i in range(NE):
            if e_i == 0:
                em_gu_dma(0, 0)
            for bi, (b0, b1) in enumerate(gu_blocks):
                pwm = bi % 2
                if bi + 1 < len(gu_blocks):
                    em_gu_dma(e_i, bi + 1)
                else:
                    em_wd_dma(e_i, 0)
                for mc in range(b0, b1):
                    off = 128 * (mc - b0)
                    for kk in range(8):
                        add("tensor", lambda e, kk=kk, pwm=pwm, off=off:
                            e.matmul(
                                pb(0)[0:128, 0:FT],
                                S[f"wg{pwm}"][:, kk, off:off + 128],
                                S["xnf"][:, kk, :],
                                start=(kk == 0), stop=(kk == 7)),
                            r=(f"wg{pwm}", "xnf"), w=("pb0",))
                    for kk in range(8):
                        add("tensor", lambda e, kk=kk, pwm=pwm, off=off:
                            e.matmul(
                                pb(1)[0:128, 0:FT],
                                S[f"wu{pwm}"][:, kk, off:off + 128],
                                S["xnf"][:, kk, :],
                                start=(kk == 0), stop=(kk == 7)),
                            r=(f"wu{pwm}", "xnf"), w=("pb1",))
                    add("scalar", lambda e: e.activation(
                        S["pt1"][:], pb(0)[0:128, 0:FT], AF.Silu),
                        r=("pb0",), w=("pt1",))
                    if e_i == 0:
                        add("vector", lambda e, mc=mc: e.tensor_tensor(
                            S["ptile"][:, mc, :], S["pt1"][:],
                            pb(1)[0:128, 0:FT], ALU.mult),
                            r=("pt1", "pb1"), w=("ptile",))
                    else:
                        add("vector", lambda e, e_i=e_i: e.tensor_tensor(
                            S["pt1"][:], S["pt1"][:],
                            S["wbc"][:, e_i - 1, :], ALU.mult),
                            r=("pt1", "wbc"), w=("pt1",))
                        add("vector", lambda e, mc=mc: e.tensor_tensor(
                            S["ptile"][:, mc, :], S["pt1"][:],
                            pb(1)[0:128, 0:FT], ALU.mult),
                            r=("pt1", "pb1"), w=("ptile",))
            for bi, (b0, b1) in enumerate(mh_blocks):
                pw = bi % 2
                if bi + 1 < len(mh_blocks):
                    em_wd_dma(e_i, bi + 1)
                elif e_i + 1 < NE:
                    em_gu_dma(e_i + 1, 0)
                for mh in range(b0, b1):
                    bank = 2 + mh % 2
                    off = 128 * (mh - b0)
                    for kk in range(NMC):
                        add("tensor", lambda e, kk=kk, pw=pw, off=off,
                            bank=bank: e.matmul(
                                pb(bank)[0:128, 0:FT],
                                S[f"wd{pw}"][:, kk, off:off + 128],
                                S["ptile"][:, kk, :],
                                start=(kk == 0), stop=(kk == NMC - 1)),
                            r=(f"wd{pw}", "ptile"), w=(PBR[bank],))
                    if e_i == 0:
                        add("scalar", lambda e, mh=mh, bank=bank: e.copy(
                            S["acc"][:, mh, :], pb(bank)[0:128, 0:FT]),
                            r=(PBR[bank],), w=("acc",))
                    else:
                        add("vector", lambda e, mh=mh, bank=bank:
                            e.tensor_tensor(
                                S["acc"][:, mh, :], S["acc"][:, mh, :],
                                pb(bank)[0:128, 0:FT], ALU.add),
                            r=(PBR[bank],), w=("acc",))

        if u == 0 and Hn.get("dbg"):
            dbg = Hn["dbg"]
            dma("gpsimd", dbg["xn"].ap(), S["xnf"][:], r=("xnf",),
                w=("OUTdxn",))
            dma("gpsimd", dbg["ltm"].ap(), S["ltm"][:], r=("ltm",),
                w=("OUTdltm",))
            dma("gpsimd", dbg["lsb"].ap(), S["lsb"][:], r=("lsb",),
                w=("OUTdlsb",))
            dma("gpsimd", dbg["rcol"].ap(), S["rcol"][:], r=("rcol",),
                w=("OUTdrcol",))
            dma("gpsimd", dbg["xtmp"].ap(), S["xftmp"][:], r=("xftmp",),
                w=("OUTdxtmp",))
            dma("gpsimd", dbg["rt1"].ap(), S["rt1"][:], r=("rt1",),
                w=("OUTdrt1",))
            dma("gpsimd", dbg["rt2"].ap(), S["rt2"][:], r=("rt2",),
                w=("OUTdrt2",))
            dma("gpsimd", dbg["wtm"].ap(), S["wtm"][:], r=("wtm",),
                w=("OUTdwtm",))
            dma("gpsimd", dbg["acc"].ap(), S["acc"][:], r=("acc",),
                w=("OUTdacc",))
            dma("gpsimd", dbg["pt"].ap(), S["ptile"][:], r=("ptile",),
                w=("OUTdpt",))
        for mh in range(8):
            if mode == "full":
                src = d_sr.ap()[128 * mh:128 * mh + 128, :] \
                    .rearrange("p (r q) -> p r q", r=4)[:, :,
                                                        FT * u:FT * (u + 1)]
                dma("gpsimd", S["scnd"][:], src, r=("d_sr",), w=("scnd",))
                add("vector", lambda e: e.tensor_scalar(
                    S["outb"][:, 0, :], S["scnd"][:, 0, :],
                    S["cmb"][:, 0:1], None, ALU.mult),
                    r=("scnd", "cmb"), w=("outb",))
                for rr in range(1, 4):
                    add("vector", lambda e, rr=rr: e.scalar_tensor_tensor(
                        S["outb"][:, 0, :], S["scnd"][:, rr, :],
                        S["cmb"][:, rr:rr + 1], S["outb"][:, 0, :],
                        ALU.mult, ALU.add), r=("scnd", "cmb", "outb"),
                        w=("outb",))
            add("scalar", lambda e, mh=mh: e.activation(
                S["outb"][:, 1, :], S["acc"][:, mh, :], AF.Silu),
                r=("acc",), w=("outb",))
            if mode == "full":
                add("vector", lambda e: e.tensor_tensor(
                    S["outb"][:, 1, :], S["outb"][:, 1, :],
                    S["outb"][:, 0, :], ALU.mult), r=("outb",), w=("outb",))
                add("vector", lambda e, mh=mh, xf=xf: e.tensor_tensor(
                    S["outb"][:, 1, :], S["outb"][:, 1, :], xf[:, mh, :],
                    ALU.add), r=("outb", xres), w=("outb",))
            dma("gpsimd", out_v[:, mh, FT * u:FT * (u + 1)],
                S["outb"][:, 1, :], r=("outb",), w=(f"OUTo{u}_{mh}",))


def kernel(**inputs):
    sh = _prep_shared(inputs)
    in_maps = [_prep_core(c, inputs, sh) for c in range(8)]
    nc = build_program(_BUILD_MODE)
    res = run_bass_kernel_spmd(nc, in_maps, core_ids=list(range(8)))
    out = np.empty((B, T, D), np.float32)
    for c in range(8):
        b, r = c // 4, c % 4
        out[b, 4096 * r:4096 * (r + 1), :] = res.results[c]["out_fm"].T
    return out


# version 10 (final)
# speedup vs baseline: 1.2613x; 1.0049x over previous
import sys
sys.path.insert(0, '/opt/trn_rl_repo')
import math
import numpy as np
import ml_dtypes
from contextlib import ExitStack
import concourse.bass as bass
import concourse.mybir as mybir
from concourse.bass_utils import run_bass_kernel_spmd

F32 = mybir.dt.float32
BF16 = mybir.dt.bfloat16
AF = mybir.ActivationFunctionType
ALU = mybir.AluOpType
AX = mybir.AxisListType
BF = ml_dtypes.bfloat16

B, T, D = 2, 16384, 1024
H, HD = 16, 64
KV = H * HD
NH = 4
TT = 512
NTILE = T // TT
C = 128
TF = 4096
FT = 512
NFT = TF // FT
I_FFN = 2816
NMC = 22
NMH = 8
NE = 8
LN8 = math.log(0.125)

_BUILD_MODE = "full"


class Sched:
    ENGS = ("tensor", "vector", "scalar", "gpsimd", "sync")

    SEM_ROLL = 30000

    def __init__(self, nc):
        self.nc = nc
        self.ops = []
        self._semidx = 0
        self.sem_c = {e: self._new_sem() for e in self.ENGS}
        self.sem_d = {e: self._new_sem() for e in ("sync", "gpsimd",
                                                   "scalar")}

    def _new_sem(self):
        self._semidx += 1
        return self.nc.alloc_semaphore(f"sem{self._semidx}")

    def add(self, eng, fn, r=(), w=(), dma=False, force_inc=False):
        op = {"eng": eng, "fn": fn, "r": tuple(r), "w": tuple(w), "dma": dma,
              "need": bool(dma or force_inc), "deps": None, "tok": None}
        self.ops.append(op)
        return op

    def resolve(self):
        lastw = {}
        readers = {}
        for op in self.ops:
            deps = []
            seen = set()

            def dep(p):
                if p is None or p is op or id(p) in seen:
                    return
                if (p["eng"] == op["eng"]) and not p["dma"] \
                        and p["eng"] == "tensor":
                    return
                p["need"] = True
                seen.add(id(p))
                deps.append(p)

            for r in op["r"]:
                dep(lastw.get(r))
            for r in op["w"]:
                dep(lastw.get(r))
                for rd in readers.get(r, ()):
                    dep(rd)
            op["deps"] = deps
            for r in op["r"]:
                readers.setdefault(r, []).append(op)
            for r in op["w"]:
                lastw[r] = op
                readers[r] = []
        cnt = {}
        for op in self.ops:
            if op["need"]:
                pool = self.sem_d if op["dma"] else self.sem_c
                sem = pool[op["eng"]]
                amt = 16 if op["dma"] else 1
                if cnt.get(id(sem), 0) + amt > self.SEM_ROLL:
                    sem = self._new_sem()
                    pool[op["eng"]] = sem
                k = id(sem)
                cnt[k] = cnt.get(k, 0) + amt
                op["tok"] = (sem, cnt[k], amt)

    def emit(self, engname, engobj):
        hw = {}
        for op in self.ops:
            if op["eng"] != engname:
                continue
            best = {}
            for p in op["deps"]:
                sem, val, _ = p["tok"]
                cur = best.get(id(sem))
                if cur is None or cur[1] < val:
                    best[id(sem)] = (sem, val)
            for k, (sem, val) in best.items():
                if hw.get(k, -1) < val:
                    engobj.wait_ge(sem, val)
                    hw[k] = val
            inst = op["fn"](engobj)
            if op["tok"] is not None:
                sem, val, amt = op["tok"]
                inst.then_inc(sem, amt)


def _lhsT_blocks(wT, kdim, mdim):
    assert wT.shape == (kdim, mdim)
    return np.ascontiguousarray(
        wT.reshape(kdim // 128, 128, mdim).transpose(1, 0, 2))


def _prep_shared(inp):
    sh = {}
    sh["identbf"] = np.eye(128, dtype=BF)
    sh["identf"] = np.eye(128, dtype=np.float32)
    jj, ii = np.mgrid[0:128, 0:128]
    sh["triu"] = (jj <= ii).astype(np.float32)
    mb = np.empty((128, 256), np.float32)
    mb[:, 0:128] = np.where(jj <= ii, 0.0, -1e30)
    mb[:, 128:256] = np.where(jj < ii, 0.0, -1e30)
    sh["maskb"] = mb
    sh["m8"] = (-((jj // 8 == ii // 8) & (jj < ii)).astype(np.float32)).astype(BF)
    sh["l0"] = (-((jj // 32 == ii // 32) & (jj // 8 < ii // 8))
                .astype(np.float32)).astype(BF)
    sh["l1"] = (-(jj // 32 < ii // 32).astype(np.float32)).astype(BF)
    sh["ibf"] = np.eye(128, dtype=BF)

    gs = [np.asarray(inp["L0_ffn_sg"], np.float32)] + \
         [np.asarray(inp["L0_ffn_rg"][e], np.float32) for e in range(7)]
    us = [np.asarray(inp["L0_ffn_su"], np.float32)] + \
         [np.asarray(inp["L0_ffn_ru"][e], np.float32) for e in range(7)]
    ds = [np.asarray(inp["L0_ffn_sd"], np.float32)] + \
         [np.asarray(inp["L0_ffn_rd"][e], np.float32) for e in range(7)]
    gu = np.empty((NE, 2, 128, 8, I_FFN), BF)
    dd = np.empty((NE, 128, NMC, D), BF)
    for e in range(NE):
        gu[e, 0] = _lhsT_blocks(np.ascontiguousarray(gs[e].T).astype(BF),
                                D, I_FFN)
        gu[e, 1] = _lhsT_blocks(np.ascontiguousarray(us[e].T).astype(BF),
                                D, I_FFN)
        dd[e] = _lhsT_blocks(np.ascontiguousarray(ds[e].T).astype(BF),
                             I_FFN, D)
    sh["ffn_gu"] = gu
    sh["ffn_d"] = dd
    sh["wr"] = _lhsT_blocks(
        np.ascontiguousarray(np.asarray(inp["L0_ffn_wr"], np.float32).T),
        D, 7)
    sh["rb_col"] = np.asarray(inp["L0_ffn_rb"], np.float32).reshape(7, 1)
    sh["rb_b"] = np.tile(np.asarray(inp["L0_ffn_rb"], np.float32), (128, 1))
    nw = np.asarray(inp["L0_attn_norm_w"], np.float32)
    sh["norm_w"] = np.tile(nw, 2).reshape(128, 1)
    return sh


def _prep_core(c, inp, sh):
    b, r = c // 4, c % 4
    hs = [4 * r + j for j in range(NH)]
    W_in = np.asarray(inp["L0_attn_W_in"], np.float32)
    W_ba = np.asarray(inp["L0_attn_W_ba"], np.float32)
    conv_w = np.asarray(inp["L0_attn_conv_w"], np.float32)[:, 0, :]
    x = np.asarray(inp["L0_attn_x"], np.float32)

    cols = []
    for blk in range(3):
        for h in hs:
            cols.append(W_in[:, blk * KV + 64 * h: blk * KV + 64 * h + 64])
    for h in hs:
        cols.append(W_in[:, 3 * KV + 64 * h: 3 * KV + 64 * h + 64])
    bacols = [W_ba[:, [h]] for h in hs] + [W_ba[:, [H + h]] for h in hs]
    wcat = np.concatenate(cols + bacols, axis=1)
    d = {}
    d["win"] = _lhsT_blocks(wcat.astype(BF), D, 1032)

    rows = []
    for blk in range(3):
        for h in hs:
            rows.append(conv_w[blk * KV + 64 * h: blk * KV + 64 * h + 64, :])
    cwg = np.concatenate(rows, axis=0)
    cd = np.zeros((128, 6, 4, 128), np.float32)
    for cc in range(6):
        for tap in range(4):
            np.fill_diagonal(cd[:, cc, tap, :],
                             cwg[128 * cc:128 * cc + 128, tap])
    d["convd"] = cd.astype(BF)

    W_o = np.asarray(inp["L0_attn_W_o"], np.float32)
    worows = np.concatenate([W_o[64 * h:64 * h + 64, :] for h in hs], axis=0)
    d["wo"] = _lhsT_blocks(worows.astype(BF), 256, D)

    gp = np.zeros((1, 8), np.float32)
    gp[0, 0:4] = np.asarray(inp["L0_attn_dt_bias"], np.float32)[hs]
    gp[0, 4:8] = np.asarray(inp["L0_attn_A_log"], np.float32)[hs]
    d["gatep"] = gp
    d["x_fm"] = np.ascontiguousarray(x[b].T)
    d["x_ffn"] = np.ascontiguousarray(x[b, 4096 * r:4096 * (r + 1)].T)
    cm = np.zeros((128, 4), np.float32)
    cm[:, r] = 1.0
    d["cmb"] = cm
    d.update(sh)
    return d


def build_program(mode="full"):
    nc = bass.Bass("TRN2", target_bir_lowering=False, debug=False,
                   disable_frame_to_traceback=True)
    mixer_on = mode in ("full", "mixer")
    ffn_on = mode in ("full", "ffn")

    dt = {}

    def din(name, shape, ty=F32):
        dt[name] = nc.dram_tensor(name, list(shape), ty, kind="ExternalInput")

    din("identbf", (128, 128), BF16)
    din("identf", (128, 128))
    din("ibf", (128, 128), BF16)
    if mixer_on:
        din("x_fm", (D, T))
        din("win", (128, 8, 1032), BF16)
        din("convd", (128, 6, 4, 128), BF16)
        din("wo", (128, 2, D), BF16)
        din("gatep", (1, 8))
        din("norm_w", (128, 1))
        din("triu", (128, 128))
        din("maskb", (128, 256))
        din("m8", (128, 128), BF16)
        din("l0", (128, 128), BF16)
        din("l1", (128, 128), BF16)
    if ffn_on:
        din("x_ffn", (D, TF))
        din("ffn_gu", (NE, 2, 128, 8, I_FFN), BF16)
        din("ffn_d", (NE, 128, NMC, D), BF16)
        din("wr", (128, 8, 7))
        din("rb_b", (128, 7))
        din("cmb", (128, 4))

    d_sp = nc.dram_tensor("d_sp", [D, T], F32, kind="Internal")
    d_sr = nc.dram_tensor("d_sr", [D, T], F32, kind="Internal")
    d_wb = nc.dram_tensor("d_wb", [7, FT], F32, kind="Internal")
    if mode == "mixer":
        d_out = nc.dram_tensor("out_s", [D, T], F32, kind="ExternalOutput")
    else:
        d_out = nc.dram_tensor("out_fm", [D, TF], F32, kind="ExternalOutput")
    dbg = {}
    if mode == "mixer":
        for nm_, shp, ty in (("mxn", (128, 8, TT), BF16),
                             ("mqkv", (128, 6, TT), BF16),
                             ("msilz", (128, 2 * TT), BF16),
                             ("mgat", (128, 64), F32),
                             ("myt", (128, 2, TT), BF16),
                             ("mamt", (128, 256), BF16),
                             ("mwsol", (128, 128), BF16),
                             ("mktm", (128, 64), BF16),
                             ("mrkv", (128, 128), BF16),
                             ("mqt", (128, 128), BF16)):
            dbg[nm_] = nc.dram_tensor("dbg_" + nm_, list(shp), ty,
                                      kind="ExternalOutput")
    if mode == "ffn":
        dbg["xn"] = nc.dram_tensor("dbg_xn", [128, 8, FT], BF16,
                                   kind="ExternalOutput")
        dbg["ltm"] = nc.dram_tensor("dbg_ltm", [128, 4, 7], F32,
                                    kind="ExternalOutput")
        dbg["wtm"] = nc.dram_tensor("dbg_wtm", [128, 4, 7], F32,
                                    kind="ExternalOutput")
        dbg["acc"] = nc.dram_tensor("dbg_acc", [128, 8, FT], F32,
                                    kind="ExternalOutput")
        dbg["pt"] = nc.dram_tensor("dbg_pt", [128, NMC, FT], BF16,
                                   kind="ExternalOutput")
        dbg["lsb"] = nc.dram_tensor("dbg_lsb", [7, FT], F32,
                                    kind="ExternalOutput")
        dbg["rcol"] = nc.dram_tensor("dbg_rcol", [128, 4], F32,
                                     kind="ExternalOutput")
        dbg["xtmp"] = nc.dram_tensor("dbg_xtmp", [128, 8, FT], F32,
                                     kind="ExternalOutput")
        dbg["rt1"] = nc.dram_tensor("dbg_rt1", [128, 4, 8], F32,
                                    kind="ExternalOutput")
        dbg["rt2"] = nc.dram_tensor("dbg_rt2", [128, 4, 7], F32,
                                    kind="ExternalOutput")

    sc = Sched(nc)
    S = {}
    scope = {"stack": None}

    def sb(name, shape, ty=F32):
        if scope["stack"] is not None:
            S[name] = scope["stack"].enter_context(
                nc.sbuf_tensor("s_" + name, list(shape), ty))
        else:
            S[name] = nc.alloc_sbuf_tensor("s_" + name, list(shape), ty)
        return S[name]

    PQ = [nc.alloc_psum_tensor(f"pq{i}", [128, 1024], F32) for i in range(4)]

    def pb(i):
        return PQ[i // 2][:, 512 * (i % 2):512 * (i % 2) + 512]

    def pbb(i):
        full = PQ[i // 2][:].bitcast(BF16)
        return full[:, 1024 * (i % 2):1024 * (i % 2) + 1024]

    PBR = [f"pb{i}" for i in range(8)]

    sb("identbf", (128, 128), BF16)
    sb("identf", (128, 128))
    sb("ibf", (128, 128), BF16)
    sb("onesf", (128, 1))
    sb("onesb", (128, 1), BF16)
    sb("cbias", (128, 4))

    def dma(eng, dst, src, r=(), w=()):
        return sc.add(eng, lambda e, a=dst, b_=src: e.dma_start(a, b_),
                      r=r, w=w, dma=True)

    for nm in ("identbf", "identf", "ibf"):
        dma("sync", S[nm][:], dt[nm].ap(), w=(nm,))
    sc.add("gpsimd", lambda e: e.memset(S["onesf"][:], 1.0), w=("onesf",))
    sc.add("gpsimd", lambda e: e.memset(S["onesb"][:], 1.0), w=("onesb",))
    sc.add("gpsimd", lambda e: e.memset(S["cbias"][:, 0:1], 1e-6),
           w=("cbias",))
    sc.add("gpsimd", lambda e: e.memset(S["cbias"][:, 1:2], 1e-5),
           w=("cbias",))
    sc.add("gpsimd", lambda e: e.memset(S["cbias"][:, 2:3], 1.0),
           w=("cbias",))
    nc.const_aps.register(F32, 1e-6, S["cbias"][:, 0:1])
    nc.const_aps.register(F32, 1e-5, S["cbias"][:, 1:2])
    nc.const_aps.register(F32, 1.0, S["cbias"][:, 2:3])

    def ones_f32_row(n):
        return bass.AP(S["onesf"], 0, [[1, 1], [0, n]])

    def ones_bf_row(n):
        return bass.AP(S["onesb"], 0, [[1, 1], [0, n]])

    def allones_f32(n):
        return bass.AP(S["onesf"], 0, [[1, 128], [0, n]])

    H_ = dict(nc=nc, sc=sc, S=S, dt=dt, pb=pb, pbb=pbb, sb=sb, dma=dma,
              PBR=PBR, o1r=ones_f32_row, o1b=ones_bf_row, oall=allones_f32,
              dbg=dbg)

    if mixer_on:
        if mode == "full":
            scope["stack"] = ExitStack()
        _build_mixer(H_, d_sp, d_out, mode)
    if mode == "full":
        def cc(e):
            return e.collective_compute(
                "AllReduce", ALU.add,
                replica_groups=[[0, 1, 2, 3], [4, 5, 6, 7]],
                ins=[d_sp.ap().opt()], outs=[d_sr.ap().opt()])
        sc.add("gpsimd", cc, r=tuple(f"d_sp{t}" for t in range(NTILE)),
               w=("d_sr",), force_inc=True)
        _phase_barrier(sc)
        scope["stack"].close()
        scope["stack"] = None
    if ffn_on:
        _build_ffn(H_, d_sr, d_wb, d_out, mode)

    outres = tuple(k for k in _all_written(sc) if k.startswith("OUT"))
    sc.add("sync", lambda e: e.nop(), r=outres)

    sc.resolve()
    with nc.Block() as blk:
        @blk.tensor
        def _(t):
            sc.emit("tensor", t)

        @blk.vector
        def _(v):
            sc.emit("vector", v)

        @blk.scalar
        def _(s):
            sc.emit("scalar", s)

        @blk.gpsimd
        def _(g):
            sc.emit("gpsimd", g)

        @blk.sync
        def _(s):
            sc.emit("sync", s)
    return nc


def _all_written(sc):
    out = set()
    for op in sc.ops:
        out.update(op["w"])
    return out


def _phase_barrier(sc, extra_reads=()):
    engs = Sched.ENGS
    for e in engs:
        sc.add(e, lambda x: x.nop(), r=tuple(extra_reads), w=("bar_" + e,))
    for e in engs:
        sc.add(e, lambda x: x.nop(), r=tuple("bar_" + f for f in engs))


def _bcast_mid(ap2d, n):
    a = ap2d.ap
    return bass.AP(ap2d.tensor, ap2d.offset, [list(a[0]), [0, n], list(a[1])])


def _ln_ops(Hn, xt, xres, xout, xoutres, xbf_name, row_name, rowb_name,
            nseg, width, bmean, bsq, bbm, bbr, xtmp=None, xtmpres=None):
    sc, S, pb, o1b = Hn["sc"], Hn["S"], Hn["pb"], Hn["o1b"]
    add = sc.add
    PBR = Hn["PBR"]
    W = nseg * width
    add("vector", lambda e: e.tensor_copy(
        S[xbf_name][:], xt[:].rearrange("p k t -> p (k t)")),
        r=(xres,), w=(xbf_name,))
    xb3 = S[xbf_name][:].rearrange("p (k t) -> p k t", k=nseg)
    for kk in range(nseg):
        add("tensor", lambda e, kk=kk: e.matmul(
            pb(bmean)[0:1, 0:width], S["onesb"][:, 0:1], xb3[:, kk, :],
            start=(kk == 0), stop=(kk == nseg - 1)),
            r=(xbf_name, "onesb"), w=(PBR[bmean],))
    add("scalar", lambda e: e.activation(
        S[xbf_name][:], S[xbf_name][:], AF.Square),
        r=(xbf_name,), w=(xbf_name,))
    for kk in range(nseg):
        add("tensor", lambda e, kk=kk: e.matmul(
            pb(bsq)[0:1, 0:width], S["onesb"][:, 0:1], xb3[:, kk, :],
            start=(kk == 0), stop=(kk == nseg - 1)),
            r=(xbf_name, "onesb"), w=(PBR[bsq],))
    row = S[row_name]
    add("vector", lambda e: e.tensor_scalar(
        row[0:1, 0:width], pb(bmean)[0:1, 0:width], 1.0 / D, None, ALU.mult),
        r=(PBR[bmean],), w=(row_name,))
    add("vector", lambda e: e.tensor_scalar(
        row[0:1, width:2 * width], pb(bsq)[0:1, 0:width], 1.0 / D, None,
        ALU.mult), r=(PBR[bsq],), w=(row_name,))
    add("scalar", lambda e: e.activation(
        row[0:1, 2 * width:3 * width], row[0:1, 0:width], AF.Square),
        r=(row_name,), w=(row_name,))
    add("vector", lambda e: e.tensor_tensor(
        row[0:1, width:2 * width], row[0:1, width:2 * width],
        row[0:1, 2 * width:3 * width], ALU.subtract),
        r=(row_name,), w=(row_name,))
    add("scalar", lambda e: e.activation(
        row[0:1, width:2 * width], row[0:1, width:2 * width], AF.Ln,
        bias=1e-5), r=(row_name,), w=(row_name,))
    add("scalar", lambda e: e.activation(
        row[0:1, width:2 * width], row[0:1, width:2 * width], AF.Exp,
        scale=-0.5), r=(row_name,), w=(row_name,))
    add("vector", lambda e: e.tensor_copy(
        S[rowb_name][:], row[0:1, 0:2 * width]),
        r=(row_name,), w=(rowb_name,))
    add("tensor", lambda e: e.matmul(
        pb(bbm)[0:128, 0:width], o1b(128), S[rowb_name][0:1, 0:width],
        start=True, stop=True), r=(rowb_name, "onesb"), w=(PBR[bbm],))
    add("tensor", lambda e: e.matmul(
        pb(bbr)[0:128, 0:width], o1b(128),
        S[rowb_name][0:1, width:2 * width],
        start=True, stop=True), r=(rowb_name, "onesb"), w=(PBR[bbr],))
    m_b = _bcast_mid(pb(bbm)[0:128, 0:width], nseg)
    r_b = _bcast_mid(pb(bbr)[0:128, 0:width], nseg)
    tgt = S[xtmp] if xtmp else xt
    tres = xtmpres if xtmp else xres
    add("vector", lambda e, tgt=tgt: e.tensor_tensor(
        tgt[:], xt[:], m_b, ALU.subtract), r=(xres, PBR[bbm]), w=(tres,))
    add("vector", lambda e, tgt=tgt: e.tensor_tensor(
        xout[:], tgt[:], r_b, ALU.mult), r=(tres, PBR[bbr]), w=(xoutres,))


def _build_mixer(Hn, d_sp, d_out, mode):
    nc, sc, S, dt, pb, pbb, sb, dma, PBR = (
        Hn["nc"], Hn["sc"], Hn["S"], Hn["dt"], Hn["pb"], Hn["pbb"], Hn["sb"],
        Hn["dma"], Hn["PBR"])
    o1b, oall = Hn["o1b"], Hn["oall"]
    add = sc.add

    for nm_, shp, ty in (("win", (128, 8, 1032), BF16),
                         ("convd", (128, 6, 4, 128), BF16),
                         ("wo", (128, 2, D), BF16),
                         ("triu", (128, 128), F32),
                         ("maskb", (128, 256), F32),
                         ("m8", (128, 128), BF16),
                         ("l0", (128, 128), BF16),
                         ("l1", (128, 128), BF16),
                         ("gatep", (1, 8), F32)):
        sb(nm_, shp, ty)
        dma("sync", S[nm_][:], dt[nm_].ap(), w=(nm_,))
    sb("normw", (128, 1))
    dma("sync", S["normw"][:], dt["norm_w"].ap(), w=("normw",))
    sb("dtb", (128, 4)); sb("negA", (128, 4))

    sb("xt0", (128, 8, TT)); sb("xt1", (128, 8, TT))
    sb("xbf", (128, 8 * TT), BF16)
    sb("xnbf", (128, 8, TT), BF16)
    sb("rowln", (1, 3 * TT))
    sb("rowlnb", (1, 2 * TT), BF16)
    sb("preconv", (128, 6, TT + 3), BF16)
    sb("qkvt", (128, 6, TT), BF16)
    sb("zt", (128, 2 * TT), BF16)
    sb("silz", (128, 2 * TT), BF16)
    sb("barow", (8, TT))
    sb("bat", (128, 4, 8))
    sb("beta", (128, 16)); sb("lnb", (128, 16)); sb("gdec", (128, 16))
    sb("gtmp", (128, 16)); sb("gtmp2", (128, 16))
    sb("bcol", (128, 16)); sb("bCb", (128, 16)); sb("expbC", (128, 16))
    sb("ytile", (128, 2, TT), BF16)
    sb("ssb0", (128, 8, TT)); sb("ssb1", (128, 8, TT))
    sb("Ssb", (128, 128), BF16)
    for s in range(16):
        sb(f"amt{s}", (128, 256), BF16)
        sb(f"wsol{s}", (128, 128), BF16)
        sb(f"ktm{s}", (128, 64), BF16)
        sb(f"rkv{s}", (128, 128), BF16)
        sb(f"qt{s}", (128, 128), BF16)
    for s in range(2):
        sb(f"ktmr{s}", (128, 64), BF16)
        sb(f"vtmr{s}", (128, 64), BF16)
        sb(f"rows{s}", (2, 384))
        sb(f"rowe{s}", (1, 128), BF16)
        sb(f"cols{s}", (128, 6))
        sb(f"dcl{s}", (128, 3))
        sb(f"lnr{s}", (128, 2))
        sb(f"ecol{s}", (128, 2))
        sb(f"btmp{s}", (128, 256))
    for p in range(2):
        sb(f"xh{p}", (128, 128), BF16)
        sb(f"g{p}", (128, 128), BF16)
        sb(f"eneg{p}", (128, 128), BF16)
        sb(f"t32{p}", (128, 128), BF16)
        sb(f"t32T{p}", (128, 128), BF16)
        sb(f"td8T{p}", (128, 128), BF16)
        sb(f"hh{p}", (128, 128), BF16)
        sb(f"d8m{p}", (128, 128), BF16)
        sb(f"l0m{p}", (128, 128), BF16)
        sb(f"l1m{p}", (128, 128), BF16)
        sb(f"cdl{p}", (128, 64), BF16)
        sb(f"fsb{p}", (128, 128), BF16)
        sb(f"o2{p}", (128, 128), BF16)
        sb(f"rstf{p}", (1, 128))
        sb(f"rst{p}", (1, 128), BF16)
        sb(f"ytmp{p}", (128, 128))

    add("tensor", lambda e: e.matmul(
        pb(0)[0:128, 0:8], Hn["o1r"](128), S["gatep"][:],
        start=True, stop=True), r=("gatep", "onesf"), w=("pb0",))
    add("vector", lambda e: e.tensor_copy(S["dtb"][:], pb(0)[0:128, 0:4]),
        r=("pb0",), w=("dtb",))
    add("vector", lambda e: e.tensor_scalar(
        S["negA"][:], pb(0)[0:128, 4:8], -2.3, None, ALU.max),
        r=("pb0",), w=("negA",))
    add("scalar", lambda e: e.activation(S["negA"][:], S["negA"][:], AF.Exp),
        r=("negA",), w=("negA",))
    add("vector", lambda e: e.tensor_scalar(
        S["negA"][:], S["negA"][:], -1.0, None, ALU.mult),
        r=("negA",), w=("negA",))
    add("gpsimd", lambda e: e.memset(S["Ssb"][:], 0.0),
        w=("Ssb0", "Ssb1", "Ssb2", "Ssb3"))
    add("gpsimd", lambda e: e.memset(S["preconv"][:, :, 0:3], 0.0),
        w=("halo",))

    x_fm = dt["x_fm"].ap().rearrange("(k p) t -> p k t", p=128)
    if mode == "mixer":
        out_v = d_out.ap().rearrange("(k p) t -> p k t", p=128)
    else:
        out_v = d_sp.ap().rearrange("(k p) t -> p k t", p=128)

    for t in range(NTILE):
        xt = S["xt0"] if t % 2 == 0 else S["xt1"]
        xres = f"xt{t % 2}"
        ssb = S["ssb0"] if t % 2 == 0 else S["ssb1"]
        ssr = f"ssb{t % 2}"
        dma("sync", xt[:], x_fm[:, :, TT * t:TT * (t + 1)], w=(xres,))

        _ln_ops(Hn, xt, xres, S["xnbf"], "xnbf", "xbf", "rowln", "rowlnb",
                8, TT, 0, 1, 2, 3)

        for mc in range(9):
            cols = 128 if mc < 8 else 8
            bank = 4 + mc % 2
            for kk in range(8):
                add("tensor", lambda e, mc=mc, kk=kk, cols=cols, bank=bank:
                    e.matmul(pb(bank)[0:cols, 0:TT],
                             S["win"][:, kk, 128 * mc:128 * mc + cols],
                             S["xnbf"][:, kk, :],
                             start=(kk == 0), stop=(kk == 7)),
                    r=("win", "xnbf"), w=(PBR[bank],))
            if mc < 6:
                add("scalar", lambda e, mc=mc, bank=bank: e.copy(
                    S["preconv"][:, mc, 3:3 + TT], pb(bank)[0:128, 0:TT]),
                    r=(PBR[bank],), w=("preconv",))
            elif mc < 8:
                add("scalar", lambda e, mc=mc, bank=bank: e.copy(
                    S["zt"][:, TT * (mc - 6):TT * (mc - 5)],
                    pb(bank)[0:128, 0:TT]), r=(PBR[bank],), w=("zt",))
            else:
                add("vector", lambda e, bank=bank: e.tensor_copy(
                    S["barow"][:], pb(bank)[0:8, 0:TT]),
                    r=(PBR[bank],), w=("barow",))

        for cc in range(6):
            bank = 6 + cc % 2
            for tap in range(4):
                add("tensor", lambda e, cc=cc, tap=tap, bank=bank: e.matmul(
                    pb(bank)[0:128, 0:TT], S["convd"][:, cc, tap, :],
                    S["preconv"][:, cc, tap:tap + TT],
                    start=(tap == 0), stop=(tap == 3)),
                    r=("convd", "preconv", "halo"), w=(PBR[bank],))
            add("scalar", lambda e, cc=cc, bank=bank: e.activation(
                S["qkvt"][:, cc, :], pb(bank)[0:128, 0:TT], AF.Silu),
                r=(PBR[bank],), w=("qkvt",))
        add("scalar", lambda e: e.activation(
            S["silz"][:], S["zt"][:], AF.Silu), r=("zt",), w=("silz",))
        add("vector", lambda e: e.tensor_copy(
            S["preconv"][:, :, 0:3], S["preconv"][:, :, TT:TT + 3]),
            r=("preconv",), w=("halo",))

        for c4 in range(4):
            add("tensor", lambda e, c4=c4: e.transpose(
                pb(0)[0:128, 0:8],
                S["barow"][0:8, 128 * c4:128 * c4 + 128],
                S["identf"][0:8, 0:8]), r=("barow", "identf"), w=("pb0",))
            add("vector", lambda e, c4=c4: e.tensor_copy(
                S["bat"][:, c4, :], pb(0)[0:128, 0:8]),
                r=("pb0",), w=("bat",))
        b_all = S["bat"][:, :, 0:4]
        a_all = S["bat"][:, :, 4:8]
        beta = S["beta"][:].rearrange("p (c h) -> p c h", c=4)
        lnb = S["lnb"][:].rearrange("p (c h) -> p c h", c=4)
        gdec = S["gdec"][:].rearrange("p (c h) -> p c h", c=4)
        g1 = S["gtmp"][:].rearrange("p (c h) -> p c h", c=4)
        g2 = S["gtmp2"][:].rearrange("p (c h) -> p c h", c=4)
        dtb_b = _bcast_mid(S["dtb"][:], 4)
        negA_b = _bcast_mid(S["negA"][:], 4)
        add("scalar", lambda e: e.activation(g1, b_all, AF.Exp, scale=-1.0),
            r=("bat",), w=("gtmp",))
        add("vector", lambda e: e.tensor_scalar(
            g2, g1, 1.0, None, ALU.add), r=("gtmp",), w=("gtmp2",))
        add("vector", lambda e: e.reciprocal(beta, g2),
            r=("gtmp2",), w=("beta",))
        add("scalar", lambda e: e.activation(g1, b_all, AF.Abs),
            r=("bat",), w=("gtmp",))
        add("scalar", lambda e: e.activation(g1, g1, AF.Exp, scale=-1.0),
            r=("gtmp",), w=("gtmp",))
        add("scalar", lambda e: e.activation(g1, g1, AF.Ln, bias=1.0),
            r=("gtmp",), w=("gtmp",))
        add("vector", lambda e: e.tensor_scalar(
            g2, b_all, 0.0, None, ALU.min), r=("bat",), w=("gtmp2",))
        add("vector", lambda e: e.tensor_tensor(lnb, g2, g1, ALU.subtract),
            r=("gtmp", "gtmp2"), w=("lnb",))
        add("vector", lambda e: e.tensor_tensor(g1, a_all, dtb_b, ALU.add),
            r=("bat", "dtb"), w=("gtmp",))
        add("scalar", lambda e: e.activation(g2, g1, AF.Abs),
            r=("gtmp",), w=("gtmp2",))
        add("scalar", lambda e: e.activation(g2, g2, AF.Exp, scale=-1.0),
            r=("gtmp2",), w=("gtmp2",))
        add("scalar", lambda e: e.activation(g2, g2, AF.Ln, bias=1.0),
            r=("gtmp2",), w=("gtmp2",))
        add("vector", lambda e: e.tensor_scalar(
            g1, g1, 0.0, None, ALU.max), r=("gtmp",), w=("gtmp",))
        add("vector", lambda e: e.tensor_tensor(g1, g1, g2, ALU.add),
            r=("gtmp", "gtmp2"), w=("gtmp",))
        add("vector", lambda e: e.tensor_tensor(gdec, g1, negA_b, ALU.mult),
            r=("gtmp", "negA"), w=("gdec",))
        for c4 in range(4):
            add("tensor", lambda e, c4=c4: e.matmul(
                pb(1)[0:128, 0:4], S["triu"][:],
                S["gdec"][:, 4 * c4:4 * c4 + 4], start=True, stop=True),
                r=("triu", "gdec"), w=("pb1",))
            add("vector", lambda e, c4=c4: e.tensor_copy(
                S["bcol"][:, 4 * c4:4 * c4 + 4], pb(1)[0:128, 0:4]),
                r=("pb1",), w=("bcol",))
            add("tensor", lambda e, c4=c4: e.matmul(
                pb(2)[0:128, 0:4], oall(128),
                S["gdec"][:, 4 * c4:4 * c4 + 4], start=True, stop=True),
                r=("onesf", "gdec"), w=("pb2",))
            add("vector", lambda e, c4=c4: e.tensor_copy(
                S["bCb"][:, 4 * c4:4 * c4 + 4], pb(2)[0:128, 0:4]),
                r=("pb2",), w=("bCb",))
        add("scalar", lambda e: e.activation(
            S["expbC"][:], S["bCb"][:], AF.Exp), r=("bCb",), w=("expbC",))

        def drive_pair(g0, g1):
            d0 = d1 = False
            while not (d0 and d1):
                if not d0:
                    try:
                        next(g0)
                    except StopIteration:
                        d0 = True
                if not d1:
                    try:
                        next(g1)
                    except StopIteration:
                        d1 = True
        for c4 in range(4):
            drive_pair(_scan_waveA(Hn, c4, 0), _scan_waveA(Hn, c4, 1))
            drive_pair(_scan_waveA(Hn, c4, 2), _scan_waveA(Hn, c4, 3))
        for spair in range(8):
            drive_pair(_scan_solve(Hn, 2 * spair),
                       _scan_solve(Hn, 2 * spair + 1))
        for c4 in range(4):
            drive_pair(_scan_waveC(Hn, c4, 0), _scan_waveC(Hn, c4, 1))
            drive_pair(_scan_waveC(Hn, c4, 2), _scan_waveC(Hn, c4, 3))

        for mh in range(8):
            bank = mh % 2
            for kk in range(2):
                add("tensor", lambda e, mh=mh, kk=kk, bank=bank: e.matmul(
                    pb(bank)[0:128, 0:TT],
                    S["wo"][:, kk, 128 * mh:128 * mh + 128],
                    S["ytile"][:, kk, :], start=(kk == 0), stop=(kk == 1)),
                    r=("wo", "ytile"), w=(PBR[bank],))
            add("scalar", lambda e, mh=mh, bank=bank, ssb=ssb: e.copy(
                ssb[:, mh, :], pb(bank)[0:128, 0:TT]),
                r=(PBR[bank],), w=(ssr,))
        wtag = (f"OUTs{t}",) if mode == "mixer" else (f"d_sp{t}",)
        dma("gpsimd", out_v[:, :, TT * t:TT * (t + 1)], ssb[:],
            r=(ssr,), w=wtag)
        if t == 0 and mode == "mixer" and Hn.get("dbg"):
            dbgt = Hn["dbg"]
            dma("gpsimd", dbgt["mxn"].ap(), S["xnbf"][:], r=("xnbf",),
                w=("OUTd0",))
            dma("gpsimd", dbgt["mqkv"].ap(), S["qkvt"][:], r=("qkvt",),
                w=("OUTd1",))
            dma("gpsimd", dbgt["msilz"].ap(), S["silz"][:], r=("silz",),
                w=("OUTd2",))
            dma("gpsimd", dbgt["mgat"].ap()[:, 0:16], S["beta"][:],
                r=("beta",), w=("OUTd3",))
            dma("gpsimd", dbgt["mgat"].ap()[:, 16:32], S["lnb"][:],
                r=("lnb",), w=("OUTd4",))
            dma("gpsimd", dbgt["mgat"].ap()[:, 32:48], S["gdec"][:],
                r=("gdec",), w=("OUTd5",))
            dma("gpsimd", dbgt["mgat"].ap()[:, 48:64], S["bcol"][:],
                r=("bcol",), w=("OUTd6",))
            dma("gpsimd", dbgt["myt"].ap(), S["ytile"][:], r=("ytile",),
                w=("OUTd7",))
            dma("gpsimd", dbgt["mamt"].ap(), S["amt0"][:], r=("amt0",),
                w=("OUTd8",))
            dma("gpsimd", dbgt["mwsol"].ap(), S["wsol0"][:], r=("wsol0",),
                w=("OUTd9",))
            dma("gpsimd", dbgt["mktm"].ap(), S["ktm0"][:], r=("ktm0",),
                w=("OUTd10",))
            dma("gpsimd", dbgt["mrkv"].ap(), S["rkv0"][:], r=("rkv0",),
                w=("OUTd11",))
            dma("gpsimd", dbgt["mqt"].ap(), S["qt0"][:], r=("qt0",),
                w=("OUTd12",))


def _scan_waveA(Hn, c4, h):
    sc, S, pb, pbb, PBR, o1b = (Hn["sc"], Hn["S"], Hn["pb"], Hn["pbb"],
                                Hn["PBR"], Hn["o1b"])
    add = sc.add
    sl = 4 * c4 + h
    ss = str(sl)
    sp = str(sl % 2)
    hb = 64 * (h % 2)
    qs = S["qkvt"][hb:hb + 64, h // 2, 128 * c4:128 * c4 + 128]
    ks = S["qkvt"][hb:hb + 64, 2 + h // 2, 128 * c4:128 * c4 + 128]
    vs = S["qkvt"][hb:hb + 64, 4 + h // 2, 128 * c4:128 * c4 + 128]
    idsl = S["identbf"][hb:hb + 64, hb:hb + 64]
    pset = sl % 2
    bp, bd = (0, 1) if pset == 0 else (2, 3)

    add("tensor", lambda e: e.transpose(pbb(bp)[0:128, 768:832], ks, idsl),
        r=("qkvt", "identbf"), w=(PBR[bp],))
    add("vector", lambda e: e.tensor_copy(
        S["ktmr" + sp][:], pbb(bp)[0:128, 768:832]),
        r=(PBR[bp],), w=("ktmr" + sp,))
    add("tensor", lambda e: e.transpose(pbb(bp)[0:128, 896:960], vs, idsl),
        r=("qkvt", "identbf"), w=(PBR[bp],))
    add("vector", lambda e: e.tensor_copy(
        S["vtmr" + sp][:], pbb(bp)[0:128, 896:960]),
        r=(PBR[bp],), w=("vtmr" + sp,))

    yield
    add("tensor", lambda e: e.matmul(pb(bp)[0:128, 0:128], ks, qs,
                                     start=True, stop=False),
        r=("qkvt",), w=(PBR[bp],))
    add("tensor", lambda e: e.matmul(pb(bp)[0:128, 128:256], ks, ks,
                                     start=False, stop=False),
        r=("qkvt",), w=(PBR[bp],))
    add("tensor", lambda e: e.matmul(pb(bp)[0:128, 256:384], qs, qs,
                                     start=False, stop=True),
        r=("qkvt",), w=(PBR[bp],))
    yield
    diag_in = pb(bp)[0:128, 128:384].rearrange("p (c x) -> p c x", c=2)
    ident2 = _bcast_mid(S["identf"][:], 2).rearrange("p c x -> p c x") \
        if False else bass.AP(S["identf"], 0, [[128, 128], [0, 2], [1, 128]])
    add("vector", lambda e: e.tensor_tensor(
        S["btmp" + sp][:].rearrange("p (c x) -> p c x", c=2), diag_in,
        ident2, ALU.mult), r=(PBR[bp], "identf"), w=("btmp" + sp,))
    add("vector", lambda e: e.reduce_sum(
        S["lnr" + sp][:], S["btmp" + sp][:].rearrange(
            "p (c x) -> p c x", c=2), axis=AX.X),
        r=("btmp" + sp,), w=("lnr" + sp,))
    add("scalar", lambda e: e.activation(
        S["lnr" + sp][:], S["lnr" + sp][:], AF.Ln, bias=1e-6),
        r=("lnr" + sp,), w=("lnr" + sp,))
    yield
    bcolh = S["bcol"][:, 4 * c4 + h:4 * c4 + h + 1]
    lnrk = S["lnr" + sp][:, 0:1]
    lnrq = S["lnr" + sp][:, 1:2]
    lnbh = S["lnb"][:, 4 * c4 + h:4 * c4 + h + 1]
    bCh = S["bCb"][:, 4 * c4 + h:4 * c4 + h + 1]
    beth = S["beta"][:, 4 * c4 + h:4 * c4 + h + 1]
    dcl = S["dcl" + sp]
    add("vector", lambda e: e.scalar_tensor_tensor(
        dcl[:, 0:1], lnrk, 0.5, bcolh, ALU.mult, ALU.add),
        r=("lnr" + sp, "bcol"), w=("dcl" + sp,))
    add("vector", lambda e: e.tensor_scalar(
        dcl[:, 0:1], dcl[:, 0:1], -1.0, None, ALU.mult),
        r=("dcl" + sp,), w=("dcl" + sp,))
    add("vector", lambda e: e.scalar_tensor_tensor(
        dcl[:, 1:2], lnrq, -0.5, bcolh, ALU.mult, ALU.add),
        r=("lnr" + sp, "bcol"), w=("dcl" + sp,))
    add("vector", lambda e: e.tensor_scalar(
        dcl[:, 1:2], dcl[:, 1:2], LN8, None, ALU.add),
        r=("dcl" + sp,), w=("dcl" + sp,))
    add("vector", lambda e: e.scalar_tensor_tensor(
        dcl[:, 2:3], lnrk, -1.0, lnbh, ALU.mult, ALU.add),
        r=("lnr" + sp, "lnb"), w=("dcl" + sp,))
    add("vector", lambda e: e.tensor_tensor(
        dcl[:, 2:3], dcl[:, 2:3], dcl[:, 0:1], ALU.subtract),
        r=("dcl" + sp,), w=("dcl" + sp,))
    yield
    add("scalar", lambda e: e.activation(
        S["ecol" + sp][:, 0:1], dcl[:, 2:3], AF.Exp),
        r=("dcl" + sp,), w=("ecol" + sp,))
    add("vector", lambda e: e.tensor_tensor(
        S["ecol" + sp][:, 1:2], bCh, dcl[:, 0:1], ALU.add),
        r=("bCb", "dcl" + sp), w=("ecol" + sp,))
    add("scalar", lambda e: e.activation(
        S["ecol" + sp][:, 1:2], S["ecol" + sp][:, 1:2], AF.Exp),
        r=("ecol" + sp,), w=("ecol" + sp,))
    yield
    add("vector", lambda e: e.tensor_scalar(
        S["ktm" + ss][:], S["ktmr" + sp][:], S["ecol" + sp][:, 1:2], None,
        ALU.mult), r=("ktmr" + sp, "ecol" + sp), w=("ktm" + ss,))
    add("vector", lambda e: e.tensor_scalar(
        S["rkv" + ss][:, 0:64], S["vtmr" + sp][:], beth, None, ALU.mult),
        r=("vtmr" + sp, "beta"), w=("rkv" + ss,))
    add("vector", lambda e: e.tensor_scalar(
        S["rkv" + ss][:, 64:128], S["ktmr" + sp][:], S["ecol" + sp][:, 0:1],
        None, ALU.mult), r=("ktmr" + sp, "ecol" + sp), w=("rkv" + ss,))
    yield
    cl = S["cols" + sp]
    add("vector", lambda e: e.tensor_scalar(
        cl[:, 0:1], dcl[:, 0:1], 0.0, 1.0, ALU.mult, ALU.add),
        r=("dcl" + sp,), w=("cols" + sp,))
    add("vector", lambda e: e.tensor_copy(cl[:, 1:2], dcl[:, 0:1]),
        r=("dcl" + sp,), w=("cols" + sp,))
    add("vector", lambda e: e.tensor_copy(cl[:, 2:3], dcl[:, 1:2]),
        r=("dcl" + sp,), w=("cols" + sp,))
    add("vector", lambda e: e.tensor_scalar(
        cl[:, 3:4], dcl[:, 0:1], 0.0, 1.0, ALU.mult, ALU.add),
        r=("dcl" + sp,), w=("cols" + sp,))
    add("vector", lambda e: e.tensor_copy(cl[:, 4:5], dcl[:, 2:3]),
        r=("dcl" + sp,), w=("cols" + sp,))
    add("vector", lambda e: e.tensor_scalar(
        cl[:, 5:6], dcl[:, 0:1], 0.0, 1.0, ALU.mult, ALU.add),
        r=("dcl" + sp,), w=("cols" + sp,))
    yield
    for pi in range(3):
        add("tensor", lambda e, pi=pi: e.transpose(
            pb(bd)[0:2, 256:384], cl[:, 2 * pi:2 * pi + 2], S["identf"][:]),
            r=("cols" + sp, "identf"), w=(PBR[bd],))
        add("vector", lambda e, pi=pi: e.tensor_copy(
            S["rows" + sp][0:2, 128 * pi:128 * pi + 128],
            pb(bd)[0:2, 256:384]), r=(PBR[bd],), w=("rows" + sp,))
    yield
    add("tensor", lambda e: e.matmul(
        pb(bd)[0:128, 0:128], S["rows" + sp][0:2, 0:128],
        S["rows" + sp][0:2, 128:256], start=True, stop=False),
        r=("rows" + sp,), w=(PBR[bd],))
    add("tensor", lambda e: e.matmul(
        pb(bd)[0:128, 128:256], S["rows" + sp][0:2, 0:128],
        S["rows" + sp][0:2, 256:384], start=False, stop=True),
        r=("rows" + sp,), w=(PBR[bd],))
    yield
    add("vector", lambda e: e.tensor_tensor(
        S["btmp" + sp][:], pb(bd)[0:128, 0:256], S["maskb"][:], ALU.add),
        r=(PBR[bd], "maskb"), w=("btmp" + sp,))
    add("scalar", lambda e: e.activation(
        S["btmp" + sp][:], S["btmp" + sp][:], AF.Exp),
        r=("btmp" + sp,), w=("btmp" + sp,))
    add("vector", lambda e: e.tensor_tensor(
        S["amt" + ss][:], S["btmp" + sp][:], pb(bp)[0:128, 0:256], ALU.mult),
        r=("btmp" + sp, PBR[bp]), w=("amt" + ss,))
    yield
    add("scalar", lambda e: e.activation(
        S["rowe" + sp][:], S["rows" + sp][0:1, 128:256], AF.Exp),
        r=("rows" + sp,), w=("rowe" + sp,))
    add("tensor", lambda e: e.matmul(
        pb(bd)[hb:hb + 64, 384:512], o1b(64), S["rowe" + sp][:],
        start=True, stop=True, tile_position=(0, hb)),
        r=("rowe" + sp, "onesb"), w=(PBR[bd],))
    add("vector", lambda e: e.tensor_tensor(
        S["qt" + ss][hb:hb + 64, :], qs, pb(bd)[hb:hb + 64, 384:512],
        ALU.mult), r=("qkvt", PBR[bd]), w=("qt" + ss,))


def _scan_solve(Hn, sl):
    sc, S, pb, pbb, PBR = Hn["sc"], Hn["S"], Hn["pb"], Hn["pbb"], Hn["PBR"]
    add = sc.add
    ss = str(sl)
    p = sl % 2
    bs = 4 + p
    amtM = S["amt" + ss][:, 128:256]
    add("vector", lambda e: e.tensor_tensor(
        S[f"d8m{p}"][:], amtM, S["m8"][:], ALU.mult),
        r=("amt" + ss, "m8"), w=(f"d8m{p}",))
    add("vector", lambda e: e.tensor_tensor(
        S[f"l0m{p}"][:], amtM, S["l0"][:], ALU.mult),
        r=("amt" + ss, "l0"), w=(f"l0m{p}",))
    add("vector", lambda e: e.tensor_tensor(
        S[f"l1m{p}"][:], amtM, S["l1"][:], ALU.mult),
        r=("amt" + ss, "l1"), w=(f"l1m{p}",))
    for it in range(7):
        rhs = S["identbf"] if it == 0 else S[f"xh{p}"]
        rr = "identbf" if it == 0 else f"xh{p}"
        add("tensor", lambda e, rhs=rhs: e.matmul(
            pb(bs)[0:128, 0:128], S[f"d8m{p}"][:], rhs[:],
            start=True, stop=False), r=(f"d8m{p}", rr), w=(PBR[bs],))
        add("tensor", lambda e: e.matmul(
            pb(bs)[0:128, 0:128], S["identbf"][:], S["identbf"][:],
            start=False, stop=True), r=("identbf",), w=(PBR[bs],))
        if it % 2 == 0:
            add("vector", lambda e: e.tensor_copy(
                S[f"xh{p}"][:], pb(bs)[0:128, 0:128]),
                r=(PBR[bs],), w=(f"xh{p}",))
        else:
            add("scalar", lambda e: e.copy(
                S[f"xh{p}"][:], pb(bs)[0:128, 0:128]),
                r=(PBR[bs],), w=(f"xh{p}",))
        yield
    add("tensor", lambda e: e.matmul(
        pb(bs)[0:128, 0:128], S[f"xh{p}"][:], S[f"l0m{p}"][:],
        start=True, stop=True), r=(f"xh{p}", f"l0m{p}"), w=(PBR[bs],))
    add("vector", lambda e: e.tensor_copy(
        S[f"eneg{p}"][:], pb(bs)[0:128, 0:128]),
        r=(PBR[bs],), w=(f"eneg{p}",))
    for it in range(3):
        rhs = S["identbf"] if it == 0 else S[f"g{p}"]
        rr = "identbf" if it == 0 else f"g{p}"
        add("tensor", lambda e, rhs=rhs: e.matmul(
            pb(bs)[0:128, 0:128], S[f"eneg{p}"][:], rhs[:],
            start=True, stop=False), r=(f"eneg{p}", rr), w=(PBR[bs],))
        add("tensor", lambda e: e.matmul(
            pb(bs)[0:128, 0:128], S["identbf"][:], S["identbf"][:],
            start=False, stop=True), r=("identbf",), w=(PBR[bs],))
        add("scalar", lambda e: e.copy(S[f"g{p}"][:], pb(bs)[0:128, 0:128]),
            r=(PBR[bs],), w=(f"g{p}",))
        yield
    add("tensor", lambda e: e.transpose(
        pbb(bs)[0:128, 512:640], S[f"xh{p}"][:], S["identbf"][:]),
        r=(f"xh{p}", "identbf"), w=(PBR[bs],))
    add("vector", lambda e: e.tensor_copy(
        S[f"td8T{p}"][:], pbb(bs)[0:128, 512:640]),
        r=(PBR[bs],), w=(f"td8T{p}",))
    yield
    add("tensor", lambda e: e.matmul(
        pb(bs)[0:128, 0:128], S[f"td8T{p}"][:], S[f"g{p}"][:],
        start=True, stop=True), r=(f"td8T{p}", f"g{p}"), w=(PBR[bs],))
    add("scalar", lambda e: e.copy(S[f"t32{p}"][:], pb(bs)[0:128, 0:128]),
        r=(PBR[bs],), w=(f"t32{p}",))
    yield
    add("tensor", lambda e: e.transpose(
        pbb(bs)[0:128, 512:640], S[f"t32{p}"][:], S["identbf"][:]),
        r=(f"t32{p}", "identbf"), w=(PBR[bs],))
    add("vector", lambda e: e.tensor_copy(
        S[f"t32T{p}"][:], pbb(bs)[0:128, 512:640]),
        r=(PBR[bs],), w=(f"t32T{p}",))
    yield
    add("tensor", lambda e: e.matmul(
        pb(bs)[0:128, 0:128], S[f"t32T{p}"][:], S["rkv" + ss][:],
        start=True, stop=True), r=(f"t32T{p}", "rkv" + ss), w=(PBR[bs],))
    add("vector", lambda e: e.tensor_copy(
        S[f"hh{p}"][:], pb(bs)[0:128, 0:128]), r=(PBR[bs],), w=(f"hh{p}",))
    yield
    for it in range(3):
        add("tensor", lambda e: e.matmul(
            pb(bs)[0:128, 0:128], S[f"l1m{p}"][:], S[f"hh{p}"][:],
            start=True, stop=False), r=(f"l1m{p}", f"hh{p}"), w=(PBR[bs],))
        add("tensor", lambda e: e.matmul(
            pb(bs)[0:128, 0:128], S["identbf"][:], S["rkv" + ss][:],
            start=False, stop=True), r=("identbf", "rkv" + ss),
            w=(PBR[bs],))
        add("scalar", lambda e: e.copy(S[f"hh{p}"][:], pb(bs)[0:128, 0:128]),
            r=(PBR[bs],), w=(f"hh{p}",))
        yield
        add("tensor", lambda e: e.matmul(
            pb(bs)[0:128, 0:128], S[f"t32T{p}"][:], S[f"hh{p}"][:],
            start=True, stop=True), r=(f"t32T{p}", f"hh{p}"), w=(PBR[bs],))
        if it < 2:
            add("vector", lambda e: e.tensor_copy(
                S[f"hh{p}"][:], pb(bs)[0:128, 0:128]),
                r=(PBR[bs],), w=(f"hh{p}",))
        yield
    add("vector", lambda e: e.tensor_copy(
        S["wsol" + ss][:], pb(bs)[0:128, 0:128]),
        r=(PBR[bs],), w=("wsol" + ss,))


def _scan_waveC(Hn, c4, h):
    sc, S, pb, pbb, PBR, o1b = (Hn["sc"], Hn["S"], Hn["pb"], Hn["pbb"],
                                Hn["PBR"], Hn["o1b"])
    add = sc.add
    sl = 4 * c4 + h
    ss = str(sl)
    p = sl % 2
    hb = 64 * (h % 2)
    hf = 64 * (h // 2)
    Ssl = S["Ssb"][hb:hb + 64, hf:hf + 64]
    Sres = f"Ssb{h}"
    bO = 7 if p == 0 else 2
    bF = 2 if p == 0 else 7
    ech = S["expbC"][hb:hb + 64, 4 * c4 + h:4 * c4 + h + 1]

    add("tensor", lambda e: e.matmul(
        pb(6)[hb:hb + 64, 0:64], S["ktm" + ss][:], S["wsol" + ss][:, 0:64],
        start=True, stop=False, tile_position=(0, hb)),
        r=("ktm" + ss, "wsol" + ss), w=("pb6",))
    yield
    add("tensor", lambda e: e.matmul(
        pb(bF)[hb:hb + 64, 0:128], S["wsol" + ss][:, 64:128],
        S["amt" + ss][:, 0:128], start=True, stop=False,
        tile_position=(0, hb)),
        r=("wsol" + ss, "amt" + ss), w=(PBR[bF],))
    add("tensor", lambda e: e.matmul(
        pb(bF)[hb:hb + 64, 128:192], S["wsol" + ss][:, 64:128],
        S["ktm" + ss][:], start=False, stop=True, tile_position=(0, hb)),
        r=("wsol" + ss, "ktm" + ss), w=(PBR[bF],))
    add("vector", lambda e: e.tensor_scalar(
        S[f"fsb{p}"][hb:hb + 64, :], pb(bF)[hb:hb + 64, 0:128], -1.0, None,
        ALU.mult), r=(PBR[bF],), w=(f"fsb{p}",))
    add("vector", lambda e: e.scalar_tensor_tensor(
        S[f"cdl{p}"][hb:hb + 64, :], S["ibf"][hb:hb + 64, hb:hb + 64], ech,
        pb(bF)[hb:hb + 64, 128:192], ALU.mult, ALU.subtract),
        r=("ibf", "expbC", PBR[bF]), w=(f"cdl{p}",))
    yield
    add("tensor", lambda e: e.matmul(
        pb(bO)[hb:hb + 64, 0:128], S["wsol" + ss][:, 0:64],
        S["amt" + ss][:, 0:128], start=True, stop=False,
        tile_position=(0, hb)),
        r=("wsol" + ss, "amt" + ss), w=(PBR[bO],))
    add("tensor", lambda e: e.matmul(
        pb(bO)[hb:hb + 64, 0:128], Ssl, S[f"fsb{p}"][hb:hb + 64, :],
        start=False, stop=False, tile_position=(hb, hb)),
        r=(Sres, f"fsb{p}"), w=(PBR[bO],))
    add("tensor", lambda e: e.matmul(
        pb(bO)[hb:hb + 64, 0:128], Ssl, S["qt" + ss][hb:hb + 64, :],
        start=False, stop=True, tile_position=(hb, hb)),
        r=(Sres, "qt" + ss), w=(PBR[bO],))
    yield
    add("tensor", lambda e: e.matmul(
        pb(6)[hb:hb + 64, 0:64], S[f"cdl{p}"][hb:hb + 64, :], Ssl,
        start=False, stop=True, tile_position=(hb, hb)),
        r=(f"cdl{p}", Sres), w=("pb6",))
    add("vector", lambda e: e.tensor_copy(Ssl, pb(6)[hb:hb + 64, 0:64]),
        r=("pb6",), w=(Sres,))
    yield
    add("scalar", lambda e: e.activation(
        S[f"o2{p}"][hb:hb + 64, :], pb(bO)[hb:hb + 64, 0:128], AF.Square),
        r=(PBR[bO],), w=(f"o2{p}",))
    bE = 4 + p
    add("tensor", lambda e: e.matmul(
        pb(bE)[0:1, 0:128], S["onesb"][hb:hb + 64, 0:1],
        S[f"o2{p}"][hb:hb + 64, :], start=True, stop=True,
        tile_position=(hb, 0)), r=(f"o2{p}", "onesb"), w=(PBR[bE],))
    yield
    add("scalar", lambda e: e.activation(
        S[f"rstf{p}"][:], pb(bE)[0:1, 0:128], AF.Ln, scale=1.0 / 64,
        bias=1e-6), r=(PBR[bE],), w=(f"rstf{p}",))
    add("scalar", lambda e: e.activation(
        S[f"rst{p}"][:], S[f"rstf{p}"][:], AF.Exp, scale=-0.5),
        r=(f"rstf{p}",), w=(f"rst{p}",))
    add("tensor", lambda e: e.matmul(
        pb(bE)[hb:hb + 64, 128:256], o1b(64), S[f"rst{p}"][:],
        start=True, stop=True, tile_position=(0, hb)),
        r=(f"rst{p}", "onesb"), w=(PBR[bE],))
    yield
    silzs = S["silz"][hb:hb + 64, TT * (h // 2) + 128 * c4:
                      TT * (h // 2) + 128 * c4 + 128]
    add("vector", lambda e: e.scalar_tensor_tensor(
        S[f"ytmp{p}"][hb:hb + 64, :], pb(bO)[hb:hb + 64, 0:128],
        S["normw"][hb:hb + 64, 0:1], silzs, ALU.mult, ALU.mult),
        r=(PBR[bO], "normw", "silz"), w=(f"ytmp{p}",))
    ysl = S["ytile"][hb:hb + 64, h // 2, 128 * c4:128 * c4 + 128]
    add("vector", lambda e: e.tensor_tensor(
        ysl, S[f"ytmp{p}"][hb:hb + 64, :], pb(bE)[hb:hb + 64, 128:256],
        ALU.mult), r=(f"ytmp{p}", PBR[bE]), w=("ytile",))


def _build_ffn(Hn, d_sr, d_wb, d_out, mode):
    nc, sc, S, dt, pb, pbb, sb, dma, PBR = (
        Hn["nc"], Hn["sc"], Hn["S"], Hn["dt"], Hn["pb"], Hn["pbb"], Hn["sb"],
        Hn["dma"], Hn["PBR"])
    o1b = Hn["o1b"]
    add = sc.add
    sb("xf0", (128, 8, FT))
    sb("xftmp", (128, 8, FT))
    sb("xfbf", (128, 8 * FT), BF16)
    sb("xnf", (128, 8, FT), BF16)
    sb("rowln2", (1, 3 * FT))
    sb("rowln2b", (1, 2 * FT), BF16)
    sb("wr", (128, 8, 7))
    sb("wrb", (128, 8, 7), BF16)
    sb("rb_b", (128, 7))
    sb("lsb", (7, FT))
    sb("ltm", (128, 4, 7))
    sb("rcol", (128, 4))
    sb("rt1", (128, 4, 8)); sb("rt2", (128, 4, 7)); sb("rt3", (128, 4, 7))
    sb("wtm", (128, 4, 7))
    sb("wrow", (1, 7 * FT))
    sb("wrowb", (1, 7 * FT), BF16)
    sb("wbc", (128, 7, FT), BF16)
    sb("ptile", (128, NMC, FT), BF16)
    sb("pt1", (128, FT), BF16)
    sb("acc", (128, 8, FT))
    sb("scnd", (128, 4, FT))
    sb("outb", (128, 2, FT))
    for p in range(2):
        sb(f"wg{p}", (128, 8, 512), BF16)
        sb(f"wu{p}", (128, 8, 512), BF16)
        sb(f"wd{p}", (128, NMC, 256), BF16)
    dma("sync", S["wr"][:], dt["wr"].ap(), w=("wr",))
    add("vector", lambda e: e.tensor_copy(S["wrb"][:], S["wr"][:]),
        r=("wr",), w=("wrb",))
    dma("sync", S["rb_b"][:], dt["rb_b"].ap(), w=("rb_b",))
    sb("cmb", (128, 4))
    dma("sync", S["cmb"][:], dt["cmb"].ap(), w=("cmb",))

    x_fv = dt["x_ffn"].ap().rearrange("(k p) t -> p k t", p=128)
    out_v = d_out.ap().rearrange("(k p) t -> p k t", p=128)
    gu_v = dt["ffn_gu"].ap()
    dd_v = dt["ffn_d"].ap()

    for u in range(NFT):
        xf = S["xf0"]
        xres = "xf0"
        dma("sync", xf[:], x_fv[:, :, FT * u:FT * (u + 1)], w=(xres,))
        _ln_ops(Hn, xf, xres, S["xnf"], "xnf", "xfbf", "rowln2", "rowln2b",
                8, FT, 0, 1, 2, 3, xtmp="xftmp", xtmpres="xftmp")

        for kk in range(8):
            add("tensor", lambda e, kk=kk: e.matmul(
                pb(4)[0:7, 0:FT], S["wr"][:, kk, :], S["xftmp"][:, kk, :],
                start=(kk == 0), stop=(kk == 7)),
                r=("wr", "xftmp"), w=("pb4",))
        add("vector", lambda e: e.tensor_copy(S["lsb"][:], pb(4)[0:7, 0:FT]),
            r=("pb4",), w=("lsb",))
        for j in range(4):
            add("tensor", lambda e, j=j: e.transpose(
                pb(5)[0:128, 0:7], S["lsb"][0:7, 128 * j:128 * j + 128],
                S["identf"][0:7, 0:7]), r=("lsb", "identf"), w=("pb5",))
            add("vector", lambda e, j=j: e.tensor_copy(
                S["ltm"][:, j, :], pb(5)[0:128, 0:7]),
                r=("pb5",), w=("ltm",))
            add("tensor", lambda e, j=j: e.transpose(
                pb(5)[0:128, 8:9],
                S["rowln2"][0:1, FT + 128 * j:FT + 128 * j + 128],
                S["identf"][0:1, 0:1]), r=("rowln2", "identf"), w=("pb5",))
            add("vector", lambda e, j=j: e.tensor_copy(
                S["rcol"][:, j:j + 1], pb(5)[0:128, 8:9]),
                r=("pb5",), w=("rcol",))
        rc_b = bass.AP(S["rcol"], 0, [[4, 128], [1, 4], [0, 7]])
        rb_bb = _bcast_mid(S["rb_b"][:], 4)
        add("vector", lambda e: e.tensor_tensor(
            S["ltm"][:], S["ltm"][:], rc_b, ALU.mult),
            r=("ltm", "rcol"), w=("ltm",))
        add("vector", lambda e: e.tensor_tensor(
            S["ltm"][:], S["ltm"][:], rb_bb, ALU.add),
            r=("ltm", "rb_b"), w=("ltm",))
        lt = S["ltm"][:]
        r1 = S["rt1"][:]
        m1 = r1[:, :, 0:1]; m2 = r1[:, :, 1:2]; cnt = r1[:, :, 2:3]
        mm2 = r1[:, :, 3:4]; den = r1[:, :, 4:5]
        t5 = r1[:, :, 5:6]; t6 = r1[:, :, 6:7]; t7 = r1[:, :, 7:8]
        r2 = S["rt2"][:]
        r3 = S["rt3"][:]

        def bc(col):
            a = col.ap
            return bass.AP(col.tensor, col.offset,
                           [list(a[0]), list(a[1]), [0, 7]])
        add("vector", lambda e: e.reduce_max(m1, lt, axis=AX.X),
            r=("ltm",), w=("rt1",))
        add("vector", lambda e: e.tensor_tensor(
            r2, lt, bc(m1), ALU.is_ge), r=("ltm", "rt1"), w=("rt2",))
        add("vector", lambda e: e.reduce_sum(cnt, r2, axis=AX.X),
            r=("rt2",), w=("rt1",))
        add("vector", lambda e: e.scalar_tensor_tensor(
            r3, r2, -1e30, lt, ALU.mult, ALU.add), r=("rt2", "ltm"),
            w=("rt3",))
        add("vector", lambda e: e.reduce_max(m2, r3, axis=AX.X),
            r=("rt3",), w=("rt1",))
        add("vector", lambda e: e.tensor_scalar(
            t5, cnt, 2.0, None, ALU.is_ge), r=("rt1",), w=("rt1",))
        add("vector", lambda e: e.tensor_tensor(
            t6, m1, m2, ALU.subtract), r=("rt1",), w=("rt1",))
        add("vector", lambda e: e.tensor_tensor(
            t7, t5, t6, ALU.mult), r=("rt1",), w=("rt1",))
        add("vector", lambda e: e.tensor_tensor(
            mm2, m2, t7, ALU.add), r=("rt1",), w=("rt1",))
        add("vector", lambda e: e.tensor_tensor(
            t5, mm2, m1, ALU.subtract), r=("rt1",), w=("rt1",))
        add("scalar", lambda e: e.activation(t5, t5, AF.Exp),
            r=("rt1",), w=("rt1",))
        add("vector", lambda e: e.tensor_scalar(
            den, t5, 1.0, None, ALU.add), r=("rt1",), w=("rt1",))
        add("vector", lambda e: e.reciprocal(den, den), r=("rt1",),
            w=("rt1",))
        add("vector", lambda e: e.tensor_tensor(
            r2, lt, bc(mm2), ALU.is_ge), r=("ltm", "rt1"), w=("rt2",))
        add("vector", lambda e: e.tensor_tensor(
            r3, lt, bc(m1), ALU.subtract), r=("ltm", "rt1"), w=("rt3",))
        add("scalar", lambda e: e.activation(r3, r3, AF.Exp),
            r=("rt3",), w=("rt3",))
        add("vector", lambda e: e.tensor_tensor(r3, r3, r2, ALU.mult),
            r=("rt3", "rt2"), w=("rt3",))
        add("vector", lambda e: e.tensor_tensor(
            S["wtm"][:], r3, bc(den), ALU.mult), r=("rt3", "rt1"),
            w=("wtm",))
        for j in range(4):
            add("tensor", lambda e, j=j: e.transpose(
                pb(5)[0:7, 0:128], S["wtm"][:, j, :], S["identf"][:]),
                r=("wtm", "identf"), w=("pb5",))
            add("vector", lambda e, j=j: e.tensor_copy(
                S["lsb"][0:7, 128 * j:128 * j + 128], pb(5)[0:7, 0:128]),
                r=("pb5",), w=("lsb",))
        dma("sync", d_wb.ap(), S["lsb"][:], r=("lsb",), w=("d_wb",))
        dma("sync", S["wrow"][:],
            d_wb.ap().rearrange("e t -> (e t)").rearrange("(o n) -> o n",
                                                          o=1),
            r=("d_wb",), w=("wrow",))
        add("vector", lambda e: e.tensor_copy(S["wrowb"][:], S["wrow"][:]),
            r=("wrow",), w=("wrowb",))
        for ee in range(7):
            add("tensor", lambda e, ee=ee: e.matmul(
                pb(5)[0:128, 0:FT], o1b(128),
                S["wrowb"][0:1, FT * ee:FT * (ee + 1)],
                start=True, stop=True), r=("wrowb", "onesb"), w=("pb5",))
            add("vector", lambda e, ee=ee: e.tensor_copy(
                S["wbc"][:, ee, :], pb(5)[0:128, 0:FT]),
                r=("pb5",), w=("wbc",))

        gu_blocks = [(b0, min(b0 + 4, NMC)) for b0 in range(0, NMC, 4)]
        mh_blocks = [(b0, b0 + 2) for b0 in range(0, 8, 2)]

        def em_gu_dma(e_i, bi):
            b0, b1 = gu_blocks[bi]
            pwm = bi % 2
            dma("sync", S[f"wg{pwm}"][:, :, 0:128 * (b1 - b0)],
                gu_v[e_i, 0, :, :, 128 * b0:128 * b1],
                w=(f"wg{pwm}",))
            dma("sync", S[f"wu{pwm}"][:, :, 0:128 * (b1 - b0)],
                gu_v[e_i, 1, :, :, 128 * b0:128 * b1],
                w=(f"wu{pwm}",))

        def em_wd_dma(e_i, bi):
            b0, b1 = mh_blocks[bi]
            pw = bi % 2
            dma("sync", S[f"wd{pw}"][:],
                dd_v[e_i][:, :, 128 * b0:128 * b1], w=(f"wd{pw}",))

        for e_i in range(NE):
            if e_i == 0:
                em_gu_dma(0, 0)
            for bi, (b0, b1) in enumerate(gu_blocks):
                pwm = bi % 2
                if bi + 1 < len(gu_blocks):
                    em_gu_dma(e_i, bi + 1)
                else:
                    em_wd_dma(e_i, 0)
                for mc in range(b0, b1):
                    off = 128 * (mc - b0)
                    for kk in range(8):
                        add("tensor", lambda e, kk=kk, pwm=pwm, off=off:
                            e.matmul(
                                pb(0)[0:128, 0:FT],
                                S[f"wg{pwm}"][:, kk, off:off + 128],
                                S["xnf"][:, kk, :],
                                start=(kk == 0), stop=(kk == 7)),
                            r=(f"wg{pwm}", "xnf"), w=("pb0",))
                    for kk in range(8):
                        add("tensor", lambda e, kk=kk, pwm=pwm, off=off:
                            e.matmul(
                                pb(1)[0:128, 0:FT],
                                S[f"wu{pwm}"][:, kk, off:off + 128],
                                S["xnf"][:, kk, :],
                                start=(kk == 0), stop=(kk == 7)),
                            r=(f"wu{pwm}", "xnf"), w=("pb1",))
                    add("scalar", lambda e: e.activation(
                        S["pt1"][:], pb(0)[0:128, 0:FT], AF.Silu),
                        r=("pb0",), w=("pt1",))
                    if e_i == 0:
                        add("vector", lambda e, mc=mc: e.tensor_tensor(
                            S["ptile"][:, mc, :], S["pt1"][:],
                            pb(1)[0:128, 0:FT], ALU.mult),
                            r=("pt1", "pb1"), w=("ptile",))
                    else:
                        add("vector", lambda e, e_i=e_i: e.tensor_tensor(
                            S["pt1"][:], S["pt1"][:],
                            S["wbc"][:, e_i - 1, :], ALU.mult),
                            r=("pt1", "wbc"), w=("pt1",))
                        add("vector", lambda e, mc=mc: e.tensor_tensor(
                            S["ptile"][:, mc, :], S["pt1"][:],
                            pb(1)[0:128, 0:FT], ALU.mult),
                            r=("pt1", "pb1"), w=("ptile",))
            for bi, (b0, b1) in enumerate(mh_blocks):
                pw = bi % 2
                if bi + 1 < len(mh_blocks):
                    em_wd_dma(e_i, bi + 1)
                elif e_i + 1 < NE:
                    em_gu_dma(e_i + 1, 0)
                for mh in range(b0, b1):
                    bank = 2 + mh % 2
                    off = 128 * (mh - b0)
                    for kk in range(NMC):
                        add("tensor", lambda e, kk=kk, pw=pw, off=off,
                            bank=bank: e.matmul(
                                pb(bank)[0:128, 0:FT],
                                S[f"wd{pw}"][:, kk, off:off + 128],
                                S["ptile"][:, kk, :],
                                start=(kk == 0), stop=(kk == NMC - 1)),
                            r=(f"wd{pw}", "ptile"), w=(PBR[bank],))
                    if e_i == 0:
                        add("scalar", lambda e, mh=mh, bank=bank: e.copy(
                            S["acc"][:, mh, :], pb(bank)[0:128, 0:FT]),
                            r=(PBR[bank],), w=("acc",))
                    else:
                        add("vector", lambda e, mh=mh, bank=bank:
                            e.tensor_tensor(
                                S["acc"][:, mh, :], S["acc"][:, mh, :],
                                pb(bank)[0:128, 0:FT], ALU.add),
                            r=(PBR[bank],), w=("acc",))

        if u == 0 and Hn.get("dbg"):
            dbg = Hn["dbg"]
            dma("gpsimd", dbg["xn"].ap(), S["xnf"][:], r=("xnf",),
                w=("OUTdxn",))
            dma("gpsimd", dbg["ltm"].ap(), S["ltm"][:], r=("ltm",),
                w=("OUTdltm",))
            dma("gpsimd", dbg["lsb"].ap(), S["lsb"][:], r=("lsb",),
                w=("OUTdlsb",))
            dma("gpsimd", dbg["rcol"].ap(), S["rcol"][:], r=("rcol",),
                w=("OUTdrcol",))
            dma("gpsimd", dbg["xtmp"].ap(), S["xftmp"][:], r=("xftmp",),
                w=("OUTdxtmp",))
            dma("gpsimd", dbg["rt1"].ap(), S["rt1"][:], r=("rt1",),
                w=("OUTdrt1",))
            dma("gpsimd", dbg["rt2"].ap(), S["rt2"][:], r=("rt2",),
                w=("OUTdrt2",))
            dma("gpsimd", dbg["wtm"].ap(), S["wtm"][:], r=("wtm",),
                w=("OUTdwtm",))
            dma("gpsimd", dbg["acc"].ap(), S["acc"][:], r=("acc",),
                w=("OUTdacc",))
            dma("gpsimd", dbg["pt"].ap(), S["ptile"][:], r=("ptile",),
                w=("OUTdpt",))
        for mh in range(8):
            if mode == "full":
                src = d_sr.ap()[128 * mh:128 * mh + 128, :] \
                    .rearrange("p (r q) -> p r q", r=4)[:, :,
                                                        FT * u:FT * (u + 1)]
                dma("gpsimd", S["scnd"][:], src, r=("d_sr",), w=("scnd",))
                add("vector", lambda e: e.tensor_scalar(
                    S["outb"][:, 0, :], S["scnd"][:, 0, :],
                    S["cmb"][:, 0:1], None, ALU.mult),
                    r=("scnd", "cmb"), w=("outb",))
                for rr in range(1, 4):
                    add("vector", lambda e, rr=rr: e.scalar_tensor_tensor(
                        S["outb"][:, 0, :], S["scnd"][:, rr, :],
                        S["cmb"][:, rr:rr + 1], S["outb"][:, 0, :],
                        ALU.mult, ALU.add), r=("scnd", "cmb", "outb"),
                        w=("outb",))
            add("scalar", lambda e, mh=mh: e.activation(
                S["outb"][:, 1, :], S["acc"][:, mh, :], AF.Silu),
                r=("acc",), w=("outb",))
            if mode == "full":
                add("vector", lambda e: e.tensor_tensor(
                    S["outb"][:, 1, :], S["outb"][:, 1, :],
                    S["outb"][:, 0, :], ALU.mult), r=("outb",), w=("outb",))
                add("vector", lambda e, mh=mh, xf=xf: e.tensor_tensor(
                    S["outb"][:, 1, :], S["outb"][:, 1, :], xf[:, mh, :],
                    ALU.add), r=("outb", xres), w=("outb",))
            dma("gpsimd", out_v[:, mh, FT * u:FT * (u + 1)],
                S["outb"][:, 1, :], r=("outb",), w=(f"OUTo{u}_{mh}",))


def kernel(**inputs):
    sh = _prep_shared(inputs)
    in_maps = [_prep_core(c, inputs, sh) for c in range(8)]
    nc = build_program(_BUILD_MODE)
    res = run_bass_kernel_spmd(nc, in_maps, core_ids=list(range(8)))
    out = np.empty((B, T, D), np.float32)
    for c in range(8):
        b, r = c // 4, c % 4
        out[b, 4096 * r:4096 * (r + 1), :] = res.results[c]["out_fm"].T
    return out
